# version 05
# baseline (speedup 1.0000x reference)
import numpy as np
from contextlib import ExitStack

import concourse.bass as bass
import concourse.mybir as mybir
from concourse.bass_utils import run_bass_kernel_spmd

F32 = mybir.dt.float32
BF16 = mybir.dt.bfloat16
AF = mybir.ActivationFunctionType
ALU = mybir.AluOpType
AX = mybir.AxisListType

D = 2048
FT = D // 128
NH = 16
HD = 128
TATT = 2048
TF = 1536
TOWN = 1024
I0 = 5632
I1 = 8192
C_SM = 8.0
EPS_LN = 1e-5
M32C = 12582912.0
M64C = float(np.float32(1.5 * 2 ** 52))
SC_ATT = float(1.0 / np.sqrt(np.float32(128.0)))

ENGINES = ("sync", "tensor", "vector", "scalar", "gpsimd")



class Rec:
    def __init__(self, ctx):
        self.ctx = ctx
        self.steps = {en: [] for en in ENGINES}
        self._drain = {}

    def op(self, en, fn, inc=None, inc_by=None):
        ctx = self.ctx
        entry = {"fn": fn, "incs": []}
        self.steps[en].append(entry)
        if inc is not None:
            v = inc_by if inc_by is not None else (16 if inc in ("dma", "dmb", "dmc", "dmag", "dmah", "out") else 1)
            ctx.cnt[inc] += v
            entry["incs"].append((inc, v))
            return ctx.cnt[inc]
        return None

    def wait(self, en, sem, val):
        h = self.ctx.wait_hwm
        if val > h.get(sem, 0):
            h[sem] = val
        self.steps[en].append({"wait": (sem, val)})

    def wait_cur(self, en, sem):
        self.wait(en, sem, self.ctx.cnt[sem])

    def _last_op(self, en):
        for e in reversed(self.steps[en]):
            if "fn" in e and not e.get("shim"):
                return e
        return None

    def mark(self, en, sem):
        ctx = self.ctx
        e = self._last_op(en)
        assert e is not None, f"mark({en}) with no prior instruction"
        if e["incs"]:
            s0, v0 = e["incs"][0]
            assert s0 == sem, f"last {en} op already incs {s0}, wanted {sem}"
            return ctx.cnt[s0]
        v = 16 if sem in ("dma", "dmb", "dmc", "dmag", "dmah", "out") else 1
        ctx.cnt[sem] += v
        e["incs"].append((sem, v))
        return ctx.cnt[sem]

    def sem_inc_raw(self, en, sem, v=1):
        self.ctx.cnt[sem] += v
        val = self.ctx.cnt[sem]
        self.steps[en].append({"fn": lambda eng, s=self.ctx.sem[sem], v=v: _SemIncShim(eng, s, v),
                               "incs": [], "shim": True})
        return val

    def fence(self, en, sem=None):
        e = self._last_op(en)
        assert e is not None, f"fence({en}) with no prior instruction"
        if e["incs"]:
            s0, _ = e["incs"][0]
            v = self.ctx.cnt[s0]
            self.wait(en, s0, v)
            return v
        sem = sem or {"vector": "vec", "scalar": "act", "gpsimd": "gps",
                      "tensor": "tns", "sync": "ph"}[en]
        v = self.mark(en, sem)
        self.wait(en, sem, v)
        return v

    def dma(self, q, out, in_, sem=None):
        if sem is None:
            sem = "dma" if q == "sync" else "dmag"
        key = (q, sem)
        hwm = self.ctx.wait_hwm.get(sem, 0)
        if hwm > self._drain.get(key, 0):
            self.steps[q].append({"wait": (sem, hwm)})
            self._drain[key] = hwm
        v = self.op(q, lambda eng, o=out, i=in_: eng.dma_start(out=o, in_=i), inc=sem)
        if v - self._drain.get(key, 0) >= 12 * 16:
            self.wait(q, sem, v)
            self._drain[key] = v
        return v

    def flush(self):
        nc = self.ctx.nc
        ctx = self.ctx

        def runner(eng, entries):
            for e in entries:
                if "wait" in e:
                    sem, val = e["wait"]
                    eng.wait_ge(ctx.sem[sem], val)
                else:
                    inst = e["fn"](eng)
                    for sem, v in e["incs"]:
                        inst.then_inc(ctx.sem[sem], v)
        with nc.Block() as block:
            for en in ENGINES:
                entries = self.steps[en]
                if entries:
                    getattr(block, en)(lambda eng, entries=entries: runner(eng, entries))
        self.steps = {en: [] for en in ENGINES}


class Ctx:
    def __init__(self, nc):
        self.nc = nc
        self.es = ExitStack()
        self.sem = {}
        self.cnt = {}
        self.wait_hwm = {}
        for name in ["dma", "dmb", "dmc", "dmag", "dmah", "tns", "vec", "act", "gps", "ph", "out"]:
            self.sem[name] = self.es.enter_context(nc.semaphore(name))
            self.cnt[name] = 0

    def barrier(self, rec):
        for en in ENGINES:
            if rec._last_op(en) is not None:
                rec.fence(en)
        for en, sems in (("sync", ("dma", "dmb", "dmc", "out")),
                         ("gpsimd", ("dmag", "dmah"))):
            for s in sems:
                if self.cnt[s] > 0:
                    rec.wait(en, s, self.cnt[s])
        for en in ENGINES:
            rec.sem_inc_raw(en, "ph", 1)
        tgt = self.cnt["ph"]
        for en in ENGINES:
            rec.wait(en, "ph", tgt)


class _SemIncShim:
    def __init__(self, eng, sem, v):
        eng.sem_inc(sem, v)

    def then_inc(self, sem, v):
        raise RuntimeError("raw sem_inc cannot carry further incs")



def _stage_inputs(inputs):
    import ml_dtypes
    x = np.asarray(inputs["L0_attn_x"], dtype=np.float32)
    w_qkv = np.ascontiguousarray(np.asarray(inputs["L0_attn_w_qkv"], dtype=np.float32))
    w_o = np.ascontiguousarray(np.asarray(inputs["L0_attn_w_o"], dtype=np.float32))
    wgu0 = np.ascontiguousarray(np.asarray(inputs["L0_ffn_w_gate_up"], dtype=np.float32))
    wd0 = np.ascontiguousarray(np.asarray(inputs["L0_ffn_w_down"], dtype=np.float32))
    l1wq = np.ascontiguousarray(np.asarray(inputs["L1_attn_Wq"], dtype=np.float32).T)
    l1wk = np.ascontiguousarray(np.asarray(inputs["L1_attn_Wk"], dtype=np.float32).T)
    l1wv = np.ascontiguousarray(np.asarray(inputs["L1_attn_Wv"], dtype=np.float32).T)
    l1wo = np.ascontiguousarray(np.asarray(inputs["L1_attn_Wo"], dtype=np.float32).T)
    l1wg = np.ascontiguousarray(np.asarray(inputs["L1_ffn_w_gate"], dtype=np.float32).T)
    l1wu = np.ascontiguousarray(np.asarray(inputs["L1_ffn_w_up"], dtype=np.float32).T)
    l1wd = np.ascontiguousarray(np.asarray(inputs["L1_ffn_w_down"], dtype=np.float32).T)

    ident = np.eye(128, dtype=np.float32)
    onescol = np.ones((128, 1), dtype=np.float32)
    onesrow = np.ones((1, 128), dtype=np.float32)
    H = np.array([[1.0]], dtype=np.float32)
    while H.shape[0] < 128:
        H = np.block([[H, H], [H, -H]])
    H128 = np.ascontiguousarray(H.astype(np.float32))

    inv_freq = 1.0 / (10000.0 ** (np.arange(0, HD, 2, dtype=np.float32) / HD))

    def rope_tables(positions):
        freqs = positions[None, :].astype(np.float32) * inv_freq[:, None]
        c = np.cos(freqs).astype(np.float32)
        s_ = np.sin(freqs).astype(np.float32)
        ct = np.empty((HD, len(positions)), dtype=np.float32)
        st = np.empty((HD, len(positions)), dtype=np.float32)
        ct[0::2] = c
        ct[1::2] = c
        st[0::2] = -s_
        st[1::2] = s_
        return np.ascontiguousarray(ct), np.ascontiguousarray(st)

    jj = np.arange(1024)[:, None]
    qq = np.arange(512)[None, :] + 512
    base_valid = (jj > qq - 512) & (jj <= qq)
    mask_full = np.where(base_valid, 0.0, -1e30).astype(np.float32)
    mask_pad = mask_full.copy()
    mask_pad[0:512, :] = -1e30
    mask_full16 = np.ascontiguousarray(mask_full.astype(ml_dtypes.bfloat16))
    mask_pad16 = np.ascontiguousarray(mask_pad.astype(ml_dtypes.bfloat16))

    in_maps = []
    for c in range(8):
        b, r = c // 2, c % 2
        xs = np.ascontiguousarray(x[b].T)
        xatt = np.ascontiguousarray(np.roll(xs, 1024, axis=1)) if r == 0 else xs
        own0 = 0 if r == 0 else 1024
        qpos = np.arange(own0, own0 + TOWN)
        if r == 1:
            kpos = np.arange(512, 2048)
        else:
            kpos = np.concatenate([np.arange(1536, 2048), np.arange(0, 1024)])
        cq, sq = rope_tables(qpos)
        ck, sk = rope_tables(kpos)
        mA = mask_pad16 if r == 0 else mask_full16
        in_maps.append({
            "xatt": xatt,
            "wqkv": w_qkv, "wo0": w_o, "wgu0": wgu0, "wd0": wd0,
            "l1wq": l1wq, "l1wk": l1wk, "l1wv": l1wv, "l1wo": l1wo,
            "l1wg": l1wg, "l1wu": l1wu, "l1wd": l1wd,
            "ident": ident, "onescol": onescol, "onesrow": onesrow, "h128": H128,
            "ropeq_c": cq, "ropeq_s": sq, "ropek_c": ck, "ropek_s": sk,
            "maskA": mA, "maskB": mask_full16,
        })
    return in_maps


def declare_io(nc):
    t = {}
    t["xatt"] = nc.dram_tensor("xatt", [D, TATT], F32, kind="ExternalInput").ap()
    t["wqkv"] = nc.dram_tensor("wqkv", [D, 3 * D], F32, kind="ExternalInput").ap()
    t["wo0"] = nc.dram_tensor("wo0", [D, D], F32, kind="ExternalInput").ap()
    t["wgu0"] = nc.dram_tensor("wgu0", [2 * I0, D], F32, kind="ExternalInput").ap()
    t["wd0"] = nc.dram_tensor("wd0", [D, I0], F32, kind="ExternalInput").ap()
    for n in ["l1wq", "l1wk", "l1wv", "l1wo"]:
        t[n] = nc.dram_tensor(n, [D, D], F32, kind="ExternalInput").ap()
    t["l1wg"] = nc.dram_tensor("l1wg", [D, I1], F32, kind="ExternalInput").ap()
    t["l1wu"] = nc.dram_tensor("l1wu", [D, I1], F32, kind="ExternalInput").ap()
    t["l1wd"] = nc.dram_tensor("l1wd", [I1, D], F32, kind="ExternalInput").ap()
    t["ident"] = nc.dram_tensor("ident", [128, 128], F32, kind="ExternalInput").ap()
    t["onescol"] = nc.dram_tensor("onescol", [128, 1], F32, kind="ExternalInput").ap()
    t["onesrow"] = nc.dram_tensor("onesrow", [1, 128], F32, kind="ExternalInput").ap()
    t["h128"] = nc.dram_tensor("h128", [128, 128], F32, kind="ExternalInput").ap()
    t["ropeq_c"] = nc.dram_tensor("ropeq_c", [HD, TOWN], F32, kind="ExternalInput").ap()
    t["ropeq_s"] = nc.dram_tensor("ropeq_s", [HD, TOWN], F32, kind="ExternalInput").ap()
    t["ropek_c"] = nc.dram_tensor("ropek_c", [HD, TF], F32, kind="ExternalInput").ap()
    t["ropek_s"] = nc.dram_tensor("ropek_s", [HD, TF], F32, kind="ExternalInput").ap()
    t["maskA"] = nc.dram_tensor("maskA", [1024, 512], BF16, kind="ExternalInput").ap()
    t["maskB"] = nc.dram_tensor("maskB", [1024, 512], BF16, kind="ExternalInput").ap()
    t["yout"] = nc.dram_tensor("yout", [D, TOWN], F32, kind="ExternalOutput").ap()
    t["s_wq0"] = nc.dram_tensor("s_wq0", [D, 2 * I0], BF16).ap()
    t["s_wd0"] = nc.dram_tensor("s_wd0", [I0, D], BF16).ap()
    t["s_wg1"] = nc.dram_tensor("s_wg1", [D, I1], BF16).ap()
    t["s_wu1"] = nc.dram_tensor("s_wu1", [D, I1], BF16).ap()
    t["s_wd1"] = nc.dram_tensor("s_wd1", [I1, D], BF16).ap()
    t["s_xq0"] = nc.dram_tensor("s_xq0", [D, TF], BF16).ap()
    t["s_vh"] = nc.dram_tensor("s_vh", [D, TF], BF16).ap()
    t["s_vl"] = nc.dram_tensor("s_vl", [D, TF], BF16).ap()
    t["s_g0"] = nc.dram_tensor("s_g0", [D, TF], F32).ap()
    t["s_xl1"] = nc.dram_tensor("s_xl1", [D, TF], F32).ap()
    t["s_krh"] = nc.dram_tensor("s_krh", [D, TF], BF16).ap()
    t["s_krl"] = nc.dram_tensor("s_krl", [D, TF], BF16).ap()
    t["s_vT"] = nc.dram_tensor("s_vT", [TF, D], F32).ap()
    t["s_s1"] = nc.dram_tensor("s_s1", [D, TOWN], F32).ap()
    t["s_xq1"] = nc.dram_tensor("s_xq1", [D, TOWN], BF16).ap()
    t["s_qrh"] = nc.dram_tensor("s_qrh", [D, TOWN], BF16).ap()
    t["s_qrl"] = nc.dram_tensor("s_qrl", [D, TOWN], BF16).ap()
    t["s_h1"] = nc.dram_tensor("s_h1", [I1, TOWN], F32).ap()
    t["s_yq1"] = nc.dram_tensor("s_yq1", [I1, TOWN], BF16).ap()
    return t




def alloc_persistent(ctx, nc):
    sb = {}
    a = lambda n, shape, dt=F32: ctx.es.enter_context(nc.sbuf_tensor(n, shape, dt))
    sb["ident"] = a("c_ident", [128, 128])
    sb["onescol"] = a("c_onescol", [128, 1])
    sb["onesrow"] = a("c_onesrow", [1, 128])
    sb["h128f"] = a("c_h128f", [128, 128])
    sb["h128b"] = a("c_h128b", [128, 128], BF16)
    sb["identb"] = a("c_identb", [128, 128], BF16)
    for n, v in [("c075", 0.75), ("negC", -C_SM), ("epsln", EPS_LN), ("zero", 0.0),
                 ("m64t", M64C), ("m32t", M32C)]:
        sb[n] = a("c_" + n, [128, 1])
        sb["_val_" + n] = v
    sb["canary"] = a("c_canary", [128, 1])
    sb["can1"] = a("c_can1", [128, 1])
    sb["magicM"] = a("c_magicM", [128, 1])
    sb["negM"] = a("c_negM", [128, 1])
    sb["csum"] = a("g_csum", [128, NH * 12])
    sb["sgu0"] = a("g_sgu0", [128, 88])
    sb["sd0"] = a("g_sd0", [128, 16])
    for n in ["deq_g1", "deq_u1", "deq_d1"]:
        sb[n] = a("g_" + n, [128, 1])
    for n in ["thp_g1", "thn_g1", "thp_u1", "thn_u1", "thp_d1", "thn_d1"]:
        sb[n] = a("g_" + n, [128, 1])
    sb["dxq0"] = a("g_dxq0", [128, 12])
    sb["dxq1"] = a("g_dxq1", [128, 8])
    sb["dyq1"] = a("g_dyq1", [128, 8])
    sb["csrow"] = a("g_csrow", [1, 128])
    return sb


def p0_consts(rec, ctx, t, sb):
    for name in ["ident", "onescol", "onesrow", "h128f"]:
        src = {"h128f": "h128"}.get(name, name)
        rec.dma("sync", sb[name][:], t[src])
    dmadone = ctx.cnt["dma"]
    for n in ["c075", "negC", "epsln", "zero", "m64t", "m32t"]:
        rec.op("gpsimd", lambda eng, ap=sb[n][:], v=sb["_val_" + n]: eng.memset(ap, v),
               inc="gps")
    gv = ctx.cnt["gps"]
    rec.wait("vector", "gps", gv)
    rec.wait("vector", "dma", dmadone)
    rec.op("vector", lambda eng: eng.tensor_add(sb["can1"][:], sb["c075"][:], sb["m64t"][:]))
    rec.fence("vector")
    rec.op("vector", lambda eng: eng.tensor_sub(sb["canary"][:], sb["can1"][:], sb["m64t"][:]))
    rec.fence("vector")
    rec.op("vector", lambda eng: eng.tensor_mul(sb["magicM"][:], sb["canary"][:], sb["m64t"][:]))
    rec.fence("vector")
    rec.op("vector", lambda eng: eng.tensor_add(sb["magicM"][:], sb["magicM"][:], sb["m32t"][:]))
    rec.fence("vector")
    rec.op("vector", lambda eng: eng.tensor_sub(sb["negM"][:], sb["zero"][:], sb["magicM"][:]))
    rec.op("vector", lambda eng: eng.tensor_copy(sb["h128b"][:], sb["h128f"][:]))
    rec.op("vector", lambda eng: eng.tensor_copy(sb["identb"][:], sb["ident"][:]), inc="vec")


def mm3(rec, out_ap, lT_hi, lT_lo, r_hi, r_lo, start, stop):
    rec.op("tensor", lambda eng: eng.matmul(out_ap, lT_hi, r_hi, start=start, stop=False))
    rec.op("tensor", lambda eng: eng.matmul(out_ap, lT_hi, r_lo, start=False, stop=False))
    rec.op("tensor", lambda eng: eng.matmul(out_ap, lT_lo, r_hi, start=False, stop=stop))


def ln_quant_chunk(rec, ctx, sb, bufs, src_hbm, col0, TC, mode, out=None):
    nsub = TC // 128
    for s in range(nsub):
        rec.wait_cur("sync", "tns")
        for ft in range(FT):
            rec.dma("sync", bufs["xch"][:, ft * 128:(ft + 1) * 128],
                    src_hbm[ft * 128:(ft + 1) * 128, col0 + s * 128:col0 + s * 128 + 128],
                    sem="dmc")
        dma_done = ctx.cnt["dmc"]
        rec.wait("tensor", "dmc", dma_done)
        for ft in range(FT):
            pb = bufs["pb0"] if ft % 2 == 0 else bufs["pb1"]
            rec.wait_cur("tensor", "vec")
            rec.op("tensor", lambda eng, pb=pb, ft=ft: eng.transpose(
                pb[:, 0:128], bufs["xch"][:, ft * 128:(ft + 1) * 128],
                sb["ident"][:]), inc="tns")
            rec.wait_cur("vector", "tns")
            rec.op("vector", lambda eng, pb=pb, ft=ft: eng.tensor_copy(
                bufs["xT"][:, ft * 128:(ft + 1) * 128], pb[:, 0:128]), inc="vec")
        rec.fence("vector")
        for i in range(4):
            rec.op("vector", lambda eng, i=i: eng.bn_stats(
                bufs["stats"][:, 6 * i:6 * i + 6], bufs["xT"][:, 512 * i:512 * (i + 1)]))
        rec.fence("vector")
        rec.op("vector", lambda eng: eng.bn_aggr(bufs["mv"][:], bufs["stats"][:]), inc="vec")
        rec.fence("vector")
        rec.wait_cur("scalar", "vec")
        rec.op("scalar", lambda eng: eng.activation(
            bufs["rr"][:], bufs["mv"][:, 1:2], AF.Sqrt, bias=sb["epsln"][:], scale=1.0), inc="act")
        rec.wait_cur("vector", "act")
        rec.op("vector", lambda eng: eng.reciprocal(bufs["rr"][:], bufs["rr"][:]))
        rec.fence("vector")
        rec.op("vector", lambda eng: eng.tensor_mul(bufs["b2"][:], bufs["mv"][:, 0:1], bufs["rr"][:]))
        rec.fence("vector")
        rec.op("vector", lambda eng: eng.tensor_single_scalar(
            bufs["b2"][:], bufs["b2"][:], -1.0, ALU.mult), inc="vec")
        rec.wait_cur("scalar", "vec")
        rec.op("scalar", lambda eng: eng.activation(
            bufs["xnT"][:], bufs["xT"][:], AF.Identity, bias=bufs["b2"][:],
            scale=bufs["rr"][:]), inc="act")
        act_ln = ctx.cnt["act"]
        if out is not None and "hi" in out:
            rec.wait("tensor", "act", act_ln)
            for ft in range(FT):
                pb = bufs["pb0"] if ft % 2 == 0 else bufs["pb1"]
                rec.wait_cur("tensor", "vec")
                rec.op("tensor", lambda eng, pb=pb, ft=ft: eng.transpose(
                    pb[:, 0:128], bufs["xnT"][:, ft * 128:(ft + 1) * 128],
                    sb["ident"][:]), inc="tns")
                rec.wait_cur("vector", "tns")
                co = ft * TC + s * 128
                hi = out["hi"][:, co:co + 128]
                lo = out["lo"][:, co:co + 128]
                tmp = bufs["tmp1"][:, 0:128]
                rec.op("vector", lambda eng, pb=pb, hi=hi: eng.tensor_copy(hi, pb[:, 0:128]))
                rec.fence("vector")
                rec.op("vector", lambda eng, pb=pb, hi=hi, tmp=tmp: eng.tensor_sub(tmp, pb[:, 0:128], hi))
                rec.fence("vector")
                rec.op("vector", lambda eng, lo=lo, tmp=tmp: eng.tensor_copy(lo, tmp), inc="vec")
        if out is not None and "quant" in out:
            q = out["quant"]
            rec.wait("vector", "act", act_ln)
            rec.op("vector", lambda eng: eng.reduce_max(
                bufs["amax"][:], bufs["xnT"][:], AX.X, apply_absolute_value=True))
            rec.fence("vector")
            if q["kind"] == "l0":
                rec.op("vector", lambda eng: eng.tensor_scalar(
                    bufs["sc"][:], bufs["amax"][:], 1.0 / 127.0, 1e-8, ALU.mult, ALU.max))
                rec.fence("vector")
                rec.op("vector", lambda eng: eng.reciprocal(bufs["rin"][:], bufs["sc"][:]), inc="vec")
                deq_src = bufs["sc"]
            else:
                rec.op("vector", lambda eng: eng.tensor_single_scalar(
                    bufs["sc"][:], bufs["amax"][:], 1e-5, ALU.max))
                rec.fence("vector")
                rec.op("vector", lambda eng: eng.reciprocal(bufs["rin"][:], bufs["sc"][:]))
                rec.fence("vector")
                rec.op("vector", lambda eng: eng.tensor_single_scalar(
                    bufs["rin"][:], bufs["rin"][:], 128.0, ALU.mult))
                rec.op("vector", lambda eng: eng.tensor_single_scalar(
                    bufs["sc"][:], bufs["sc"][:], 1.0 / 128.0, ALU.mult), inc="vec")
                deq_src = bufs["sc"]
            rec.fence("vector")
            rec.op("vector", lambda eng, s=s: eng.tensor_copy(
                q["deq_cols"][:, s:s + 1], deq_src[:]))
            rec.wait_cur("scalar", "vec")
            rec.op("scalar", lambda eng: eng.activation(
                bufs["aT"][:], bufs["xnT"][:], AF.Identity, bias=sb["magicM"][:],
                scale=bufs["rin"][:]), inc="act")
            rec.wait_cur("vector", "act")
            rec.op("vector", lambda eng: eng.tensor_single_scalar(
                bufs["aT"][:], bufs["aT"][:], sb["magicM"][:], ALU.subtract))
            if q["kind"] == "l1":
                rec.fence("vector")
                rec.op("vector", lambda eng: eng.tensor_scalar(
                    bufs["aT"][:], bufs["aT"][:], 127.0, -128.0, ALU.min, ALU.max))
            qv = rec.mark("vector", "vec")
            rec.wait("tensor", "vec", qv)
            for ft in range(FT):
                pb = bufs["pb0"] if ft % 2 == 0 else bufs["pb1"]
                rec.wait_cur("tensor", "vec")
                rec.op("tensor", lambda eng, pb=pb, ft=ft: eng.transpose(
                    pb[:, 0:128], bufs["aT"][:, ft * 128:(ft + 1) * 128],
                    sb["ident"][:]), inc="tns")
                rec.wait_cur("vector", "tns")
                if "dma_to" in q:
                    dst, colb = q["dma_to"]
                    stg = q["stage"][ft % 2]
                    rec.wait_cur("vector", "dmag")
                    rec.op("vector", lambda eng, pb=pb, stg=stg: eng.tensor_copy(
                        stg[:], pb[:, 0:128]), inc="vec")
                    vv = ctx.cnt["vec"]
                    rec.wait("gpsimd", "vec", vv)
                    rec.dma("gpsimd", dst[ft * 128:(ft + 1) * 128,
                                          colb + s * 128:colb + s * 128 + 128], stg[:])
                else:
                    co = ft * TC + s * 128
                    rec.op("vector", lambda eng, pb=pb, co=co: eng.tensor_copy(
                        q["q_slab"][:, co:co + 128], pb[:, 0:128]), inc="vec")


def p1_l0_attention(rec, ctx, t, sb, rounds=4):
    nc = ctx.nc
    NQH = 4
    for rnd in range(rounds):
        with ExitStack() as ph:
            a = lambda n, shape, dt=F32: ph.enter_context(
                nc.sbuf_tensor(f"p1_{n}_{rnd}", shape, dt))
            kh = a("kh", [128, NQH * TATT], BF16)
            kl = a("kl", [128, NQH * TATT], BF16)
            qh = a("qh", [128, NQH * TATT], BF16)
            ql = a("ql", [128, NQH * TATT], BF16)
            bufs = {
                "xch": a("xch", [128, FT * 128]),
                "xT": a("xT", [128, D]), "xnT": a("xnT", [128, D]),
                "stats": a("stats", [128, 24]), "mv": a("mv", [128, 2]),
                "rr": a("rr", [128, 1]), "b2": a("b2", [128, 1]),
                "tmp1": a("tmp1", [128, 1024]),
                "amax": a("amax", [128, 1]), "sc": a("sc", [128, 1]),
                "rin": a("rin", [128, 1]),
            }
            with ExitStack() as ps:
                pj = [ps.enter_context(nc.psum_tensor(f"p1_pj{i}_{rnd}", [128, 512], F32))
                      for i in range(8)]
                bufs["pb0"], bufs["pb1"] = pj[0], pj[1]
                wh = ps.enter_context(nc.sbuf_tensor(f"p1_wh_{rnd}", [128, FT * 1024], BF16))
                wl = ps.enter_context(nc.sbuf_tensor(f"p1_wl_{rnd}", [128, FT * 1024], BF16))
                wtmp = ps.enter_context(nc.sbuf_tensor(f"p1_wt_{rnd}", [128, 1024], F32))
                xnh = ps.enter_context(nc.sbuf_tensor(f"p1_xnh_{rnd}", [128, FT * 512], BF16))
                xnl = ps.enter_context(nc.sbuf_tensor(f"p1_xnl_{rnd}", [128, FT * 512], BF16))
                for ft in range(FT):
                    wsem = ("dma", "dmb")[ft % 2]
                    rec.wait_cur("sync", "vec")
                    rec.dma("sync", wtmp[:, 0:512],
                            t["wqkv"][ft * 128:(ft + 1) * 128, rnd * 512:rnd * 512 + 512],
                            sem=wsem)
                    rec.dma("sync", wtmp[:, 512:1024],
                            t["wqkv"][ft * 128:(ft + 1) * 128, 2048 + rnd * 512:2048 + rnd * 512 + 512],
                            sem=wsem)
                    rec.wait_cur("vector", wsem)
                    co = ft * 1024
                    rec.op("vector", lambda eng, co=co: eng.tensor_copy(wh[:, co:co + 1024], wtmp[:]))
                    rec.fence("vector")
                    rec.op("vector", lambda eng, co=co: eng.tensor_copy(
                        bufs["tmp1"][:, 0:1024], wh[:, co:co + 1024]))
                    rec.fence("vector")
                    rec.op("vector", lambda eng: eng.tensor_sub(wtmp[:], wtmp[:], bufs["tmp1"][:, 0:1024]))
                    rec.fence("vector")
                    rec.op("vector", lambda eng, co=co: eng.tensor_copy(wl[:, co:co + 1024], wtmp[:]),
                           inc="vec")
                wsplit_done = ctx.cnt["vec"]
                pj_free = [0] * 8
                for tc in range(4):
                    rec.wait("tensor", "vec", max(pj_free[0], pj_free[1]))
                    ln_quant_chunk(rec, ctx, sb, bufs, t["xatt"], tc * 512, 512,
                                   "split", out={"hi": xnh, "lo": xnl})
                    rec.wait("tensor", "vec", wsplit_done)
                    rec.wait("tensor", "vec", ctx.cnt["vec"])
                    for grp in range(2):
                        for hh in range(NQH):
                            bi = grp * NQH + hh
                            bank = pj[bi]
                            rec.wait("tensor", "vec", pj_free[bi])
                            for ft in range(FT):
                                co = ft * 1024 + grp * 512 + hh * 128
                                mm3(rec, bank[:],
                                    wh[:, co:co + 128], wl[:, co:co + 128],
                                    xnh[:, ft * 512:(ft + 1) * 512],
                                    xnl[:, ft * 512:(ft + 1) * 512],
                                    start=(ft == 0), stop=(ft == FT - 1))
                            tv = rec.mark("tensor", "tns")
                            rec.wait("vector", "tns", tv)
                            dst_h = (qh, kh)[grp]
                            dst_l = (ql, kl)[grp]
                            co = hh * TATT + tc * 512
                            tmp = bufs["tmp1"][:, 512 * (bi % 2):512 * (bi % 2) + 512]
                            rec.op("vector", lambda eng, b=bank, d=dst_h, co=co: eng.tensor_copy(
                                d[:, co:co + 512], b[:]))
                            rec.fence("vector")
                            rec.op("vector", lambda eng, b=bank, d=dst_h, co=co, tmp=tmp: eng.tensor_sub(
                                tmp, b[:], d[:, co:co + 512]))
                            rec.fence("vector")
                            pj_free[bi] = rec.op("vector", lambda eng, d=dst_l, co=co, tmp=tmp: eng.tensor_copy(
                                d[:, co:co + 512], tmp), inc="vec")
            ctx.barrier(rec)
            with ExitStack() as ps:
                ps_sc = [ps.enter_context(nc.psum_tensor(f"p1_sc{i}_{rnd}", [128, 1024], F32))
                         for i in range(2)]
                ps_cs = ps.enter_context(nc.psum_tensor(f"p1_cs_{rnd}", [128, 512], F32))
                eb = [ps.enter_context(nc.sbuf_tensor(f"p1_eb{i}_{rnd}", [128, 2048], F32))
                      for i in range(2)]
                zpart = ps.enter_context(nc.sbuf_tensor(f"p1_zp_{rnd}", [128, 2], F32))
                zinv = ps.enter_context(nc.sbuf_tensor(f"p1_zi_{rnd}", [128, 1], F32))
                eb_free = [0, 0]
                sc_free = [0, 0]
                z_read = 0
                z_done = 0
                for hh in range(NQH):
                    for qt in range(16):
                        e = eb[qt % 2]
                        for hlf in range(2):
                            rec.wait("tensor", "act", sc_free[hlf])
                            for kc in range(2):
                                kco = hh * TATT + hlf * 1024 + kc * 512
                                qco = hh * TATT + qt * 128
                                mm3(rec, ps_sc[hlf][:, kc * 512:(kc + 1) * 512],
                                    qh[:, qco:qco + 128], ql[:, qco:qco + 128],
                                    kh[:, kco:kco + 512], kl[:, kco:kco + 512],
                                    start=True, stop=True)
                        tmm = rec.mark("tensor", "tns")
                        rec.wait("scalar", "tns", tmm)
                        rec.wait("scalar", "tns", eb_free[qt % 2])
                        rec.wait("scalar", "vec", z_done)
                        a0 = rec.op("scalar", lambda eng, e=e: eng.activation(
                            e[:, 0:1024], ps_sc[0][:], AF.Exp, bias=sb["negC"][:],
                            scale=SC_ATT, accum_out=zpart[:, 0:1]), inc="act")
                        a1 = rec.op("scalar", lambda eng, e=e: eng.activation(
                            e[:, 1024:2048], ps_sc[1][:], AF.Exp, bias=sb["negC"][:],
                            scale=SC_ATT, accum_out=zpart[:, 1:2]), inc="act")
                        sc_free[0] = a0
                        sc_free[1] = a1
                        rec.wait("vector", "act", a1)
                        rec.wait("vector", "tns", z_read)
                        rec.op("vector", lambda eng: eng.tensor_add(zinv[:], zpart[:, 0:1], zpart[:, 1:2]))
                        rec.fence("vector")
                        zv = rec.op("vector", lambda eng: eng.reciprocal(zinv[:], zinv[:]), inc="vec")
                        z_done = zv
                        rec.wait("tensor", "vec", zv)
                        for kt in range(4, 16):
                            rec.op("tensor", lambda eng, e=e, kt=kt, qt=qt: eng.matmul(
                                ps_cs[:, kt - 4:kt - 3], e[:, kt * 128:(kt + 1) * 128], zinv[:],
                                start=(qt == 0 and kt == 4), stop=(qt == 15 and kt == 15)))
                        eb_free[qt % 2] = rec.mark("tensor", "tns")
                        z_read = eb_free[qt % 2]
                    tcs = ctx.cnt["tns"]
                    rec.wait("vector", "tns", tcs)
                    hg = rnd * NQH + hh
                    cv = rec.op("vector", lambda eng, hg=hg: eng.tensor_copy(
                        sb["csum"][:, hg * 12:hg * 12 + 12], ps_cs[:, 0:12]), inc="vec")
                    rec.wait("tensor", "vec", cv)
            ctx.barrier(rec)
            rec.flush()
    ctx.barrier(rec)
    rec.flush()


def p2a_quant_w0(rec, ctx, t, sb):
    nc = ctx.nc
    for (src_t, dst_t, n_in, n_rows, sc_dst) in [
            ("wgu0", "s_wq0", D, 2 * I0, "sgu0"),
            ("wd0", "s_wd0", I0, D, "sd0")]:
        with ExitStack() as ph:
            a = lambda n, shape, dt=F32: ph.enter_context(
                nc.sbuf_tensor(f"p2a_{n}_{src_t}", shape, dt))
            nat = [a(f"nat{i}", [128, n_in]) for i in range(2)]
            aa = [a(f"aa{i}", [128, n_in]) for i in range(2)]
            ib = [a(f"ib{i}", [128, n_in], BF16) for i in range(2)]
            amax = a("amax", [128, 2])
            sc = a("sc", [128, 2])
            rin = a("rin", [128, 2])
            ob = [a(f"ob{i}", [128, 128], BF16) for i in range(2)]
            pt = [ph.enter_context(nc.psum_tensor(f"p2a_pt{i}_{src_t}", [128, 128], BF16))
                  for i in range(4)]
            nt = n_in // 128
            nat_free = [0, 0]
            ib_free = [0, 0]
            pt_free = [0, 0, 0, 0]
            ob_free = [0, 0]
            for g in range(n_rows // 128):
                s = g % 2
                nsem = "dma" if s == 0 else "dmb"
                rec.wait("sync", "act", nat_free[s])
                dv = rec.dma("sync", nat[s][:], t[src_t][g * 128:(g + 1) * 128, :], sem=nsem)
                rec.wait("vector", nsem, dv)
                rec.op("vector", lambda eng, s=s: eng.reduce_max(
                    amax[:, s:s + 1], nat[s][:], AX.X, apply_absolute_value=True))
                rec.fence("vector")
                rec.op("vector", lambda eng, s=s: eng.tensor_scalar(
                    sc[:, s:s + 1], amax[:, s:s + 1], 1.0 / 127.0, 1e-8, ALU.mult, ALU.max))
                rec.fence("vector")
                rec.op("vector", lambda eng, s=s: eng.reciprocal(rin[:, s:s + 1], sc[:, s:s + 1]))
                rec.fence("vector")
                v1 = rec.op("vector", lambda eng, g=g, s=s: eng.tensor_copy(
                    sb[sc_dst][:, g:g + 1], sc[:, s:s + 1]), inc="vec")
                rec.wait("scalar", "vec", v1)
                rec.wait("scalar", "tns", ib_free[s])
                rec.op("scalar", lambda eng, s=s: eng.activation(
                    aa[s][:], nat[s][:], AF.Identity, bias=sb["magicM"][:],
                    scale=rin[:, s:s + 1]))
                rec.fence("scalar")
                av = rec.op("scalar", lambda eng, s=s: eng.activation(
                    ib[s][:], aa[s][:], AF.Identity, bias=sb["negM"][:], scale=1.0),
                    inc="act")
                nat_free[s] = av
                rec.wait("tensor", "act", av)
                for ft in range(nt):
                    i4 = ft % 4
                    i2 = ft % 2
                    osem = "dmag" if i2 == 0 else "dmah"
                    rec.wait("tensor", "vec", pt_free[i4])
                    tv = rec.op("tensor", lambda eng, i4=i4, s=s, ft=ft: eng.transpose(
                        pt[i4][:], ib[s][:, ft * 128:(ft + 1) * 128], sb["identb"][:]),
                        inc="tns")
                    rec.wait("vector", "tns", tv)
                    rec.wait("vector", osem, ob_free[i2])
                    ev = rec.op("vector", lambda eng, i4=i4, i2=i2: eng.tensor_copy(
                        ob[i2][:], pt[i4][:]), inc="vec")
                    pt_free[i4] = ev
                    rec.wait("gpsimd", "vec", ev)
                    ob_free[i2] = rec.dma(
                        "gpsimd", t[dst_t][ft * 128:(ft + 1) * 128, g * 128:(g + 1) * 128],
                        ob[i2][:], sem=osem)
                ib_free[s] = ctx.cnt["tns"]
            ctx.barrier(rec)
            rec.flush()
    ctx.barrier(rec)
    rec.flush()


def p2b_quant_w1(rec, ctx, t, sb):
    nc = ctx.nc
    for (src_t, dst_t, n_k, n_o, deq, thp, thn) in [
            ("l1wg", "s_wg1", D, I1, "deq_g1", "thp_g1", "thn_g1"),
            ("l1wu", "s_wu1", D, I1, "deq_u1", "thp_u1", "thn_u1"),
            ("l1wd", "s_wd1", I1, D, "deq_d1", "thp_d1", "thn_d1")]:
        with ExitStack() as ph:
            a = lambda n, shape, dt=F32: ph.enter_context(
                nc.sbuf_tensor(f"p2b_{n}_{src_t}", shape, dt))
            wt = [a(f"wt{i}", [128, n_o]) for i in range(2)]
            s1 = a("s1", [128, n_o])
            s2 = a("s2", [128, n_o])
            qb = [a(f"qb{i}", [128, n_o], BF16) for i in range(2)]
            acc = a("acc", [128, 16 if n_k == D else 64])
            tot = a("tot", [128, 1])
            pm = ph.enter_context(nc.psum_tensor(f"p2b_pm_{src_t}", [128, 512], F32))
            nkt = n_k // 128
            wt_free = [0, 0]
            for it in range(nkt):
                s = it % 2
                nsem = "dma" if s == 0 else "dmb"
                rec.wait("sync", "vec", wt_free[s])
                dv = rec.dma("sync", wt[s][:], t[src_t][it * 128:(it + 1) * 128, :], sem=nsem)
                rec.wait("vector", nsem, dv)
                wt_free[s] = rec.op("vector", lambda eng, it=it, s=s: eng.reduce_sum(
                    acc[:, it:it + 1], wt[s][:], AX.X, apply_absolute_value=True), inc="vec")
            rec.fence("vector")
            rec.op("vector", lambda eng: eng.reduce_sum(tot[:], acc[:], AX.X), inc="vec")
            tv = ctx.cnt["vec"]
            rec.wait("tensor", "vec", tv)
            rec.op("tensor", lambda eng: eng.matmul(
                pm[0:1, 0:1], tot[:], sb["onescol"][:], start=True, stop=True), inc="tns")
            rec.wait_cur("vector", "tns")
            rec.op("vector", lambda eng, n_k=n_k, n_o=n_o: eng.tensor_single_scalar(
                tot[0:1, 0:1], pm[0:1, 0:1], 1.0 / float(n_k * n_o), ALU.mult))
            rec.fence("vector")
            rec.op("vector", lambda eng: eng.tensor_single_scalar(
                tot[0:1, 0:1], tot[0:1, 0:1], 1e-5, ALU.max), inc="vec")
            tv = ctx.cnt["vec"]
            rec.wait("tensor", "vec", tv)
            rec.op("tensor", lambda eng: eng.matmul(
                pm[:, 1:2], sb["onesrow"][:], tot[0:1, 0:1],
                start=True, stop=True), inc="tns")
            rec.wait_cur("vector", "tns")
            rec.op("vector", lambda eng, deq=deq: eng.tensor_single_scalar(
                sb[deq][:], pm[:, 1:2], 0.5, ALU.mult))
            rec.fence("vector")
            rec.op("vector", lambda eng, deq=deq, thn=thn: eng.tensor_sub(
                sb[thn][:], sb["zero"][:], sb[deq][:]))
            rec.op("vector", lambda eng, deq=deq, thp=thp: eng.tensor_copy(
                sb[thp][:], sb[deq][:]), inc="vec")
            tv = ctx.cnt["vec"]
            rec.wait("scalar", "vec", tv)
            rec.wait_cur("sync", "vec")
            wt_free = [0, 0]
            qb_free = [0, 0]
            for it in range(nkt):
                s = it % 2
                nsem = "dma" if s == 0 else "dmb"
                osem = "dmag" if s == 0 else "dmah"
                rec.wait("sync", "act", wt_free[s])
                dv = rec.dma("sync", wt[s][:], t[src_t][it * 128:(it + 1) * 128, :], sem=nsem)
                rec.wait("scalar", nsem, dv)
                rec.wait_cur("scalar", "vec")
                rec.op("scalar", lambda eng, s=s, thn=thn: eng.activation(
                    s1[:], wt[s][:], AF.Sign, bias=sb[thn][:], scale=1.0))
                av = rec.op("scalar", lambda eng, s=s, thp=thp: eng.activation(
                    s2[:], wt[s][:], AF.Sign, bias=sb[thp][:], scale=1.0), inc="act")
                wt_free[s] = av
                rec.wait("vector", "act", av)
                rec.wait("vector", osem, qb_free[s])
                vv = rec.op("vector", lambda eng, s=s: eng.tensor_add(qb[s][:], s1[:], s2[:]),
                            inc="vec")
                rec.wait("gpsimd", "vec", vv)
                qb_free[s] = rec.dma("gpsimd", t[dst_t][it * 128:(it + 1) * 128, :], qb[s][:],
                                     sem=osem)
            ctx.barrier(rec)
            rec.flush()
    ctx.barrier(rec)
    rec.flush()


def _csum_bcast(rec, ctx, sb, col, ptr, pbc):
    rec.op("tensor", lambda eng: eng.transpose(
        ptr[:], sb["csum"][:, col:col + 1], sb["ident"][:]), inc="tns")
    rec.wait_cur("vector", "tns")
    rec.op("vector", lambda eng: eng.tensor_copy(sb["csrow"][:], ptr[0:1, 0:128]), inc="vec")
    vv = ctx.cnt["vec"]
    rec.wait("tensor", "vec", vv)
    rec.op("tensor", lambda eng: eng.matmul(
        pbc[:], sb["onesrow"][:], sb["csrow"][:], start=True, stop=True), inc="tns")
    return ctx.cnt["tns"]


def p2c_values_xq(rec, ctx, t, sb):
    nc = ctx.nc
    with ExitStack() as ph:
        a = lambda n, shape, dt=F32: ph.enter_context(nc.sbuf_tensor("p2c_" + n, shape, dt))
        bufs = {
            "xch": a("xch", [128, FT * 128]),
            "xT": a("xT", [128, D]), "xnT": a("xnT", [128, D]),
            "stats": a("stats", [128, 24]), "mv": a("mv", [128, 2]),
            "rr": a("rr", [128, 1]), "b2": a("b2", [128, 1]),
            "tmp1": a("tmp1", [128, 1024]),
            "amax": a("amax", [128, 1]), "sc": a("sc", [128, 1]),
            "rin": a("rin", [128, 1]),
        }
        bufs["aT"] = bufs["xT"]
        wvh = a("wvh", [128, FT * 1024], BF16)
        wvl = a("wvl", [128, FT * 1024], BF16)
        wtmp = a("wtmp", [128, 1024])
        xnh = a("xnh", [128, FT * 512], BF16)
        xnl = a("xnl", [128, FT * 512], BF16)
        qstg = [a("qstg0", [128, 128], BF16), a("qstg1", [128, 128], BF16)]
        vtile = a("vtile", [128, 512])
        bc_sb = a("bc_sb", [128, 128])
        vh = a("vh", [128, 512], BF16)
        vl = a("vl", [128, 512], BF16)
        csb = a("csb", [128, NH * 512])
        with ExitStack() as ps:
            pp = [ps.enter_context(nc.psum_tensor(f"p2c_pp{i}", [128, 512], F32))
                  for i in range(4)]
            ptr = ps.enter_context(nc.psum_tensor("p2c_ptr", [1, 128], F32))
            pbc = ps.enter_context(nc.psum_tensor("p2c_pbc", [128, 128], F32))
            bufs["pb0"], bufs["pb1"] = pp[0], pp[1]
            for tc in range(3):
                ln_quant_chunk(rec, ctx, sb, bufs, t["xatt"], 512 + tc * 512, 512,
                               "split", out={"hi": xnh, "lo": xnl,
                                             "quant": {"kind": "l0", "dma_to": (t["s_xq0"], tc * 512),
                                                       "stage": qstg,
                                                       "deq_cols": sb["dxq0"][:, tc * 4:tc * 4 + 4]}})
                for h in range(NH):
                    for s in range(4):
                        col = h * 12 + tc * 4 + s
                        bt = _csum_bcast(rec, ctx, sb, col, ptr, pbc)
                        rec.wait("vector", "tns", bt)
                        rec.op("vector", lambda eng, h=h, s=s: eng.tensor_copy(
                            csb[:, h * 512 + s * 128:h * 512 + (s + 1) * 128], pbc[:]),
                            inc="vec")
                        rec.wait("tensor", "vec", ctx.cnt["vec"])
                pp_free = [0, 0, 0, 0]
                for wf in range(2):
                  for ft in range(FT):
                    wsem = ("dma", "dmb")[ft % 2]
                    rec.wait_cur("sync", "vec")
                    rec.wait_cur("sync", "tns")
                    rec.dma("sync", wtmp[:],
                            t["wqkv"][ft * 128:(ft + 1) * 128,
                                      4096 + wf * 1024:4096 + (wf + 1) * 1024],
                            sem=wsem)
                    rec.wait_cur("vector", wsem)
                    co = ft * 1024
                    rec.op("vector", lambda eng, co=co: eng.tensor_copy(wvh[:, co:co + 1024], wtmp[:]))
                    rec.fence("vector")
                    rec.op("vector", lambda eng, co=co: eng.tensor_copy(
                        bufs["tmp1"][:, 0:1024], wvh[:, co:co + 1024]))
                    rec.fence("vector")
                    rec.op("vector", lambda eng, co=co: eng.tensor_sub(
                        wtmp[:], wtmp[:], bufs["tmp1"][:, 0:1024]))
                    rec.fence("vector")
                    rec.op("vector", lambda eng, co=co: eng.tensor_copy(wvl[:, co:co + 1024], wtmp[:]),
                           inc="vec")
                  wsplit_done = ctx.cnt["vec"]
                  rec.wait("tensor", "vec", wsplit_done)
                  for h in range(wf * 8, wf * 8 + 8):
                    bi = h % 4
                    bank = pp[bi]
                    rec.wait("tensor", "vec", pp_free[bi])
                    for ft in range(FT):
                        co = ft * 1024 + (h - wf * 8) * 128
                        mm3(rec, bank[:],
                            wvh[:, co:co + 128], wvl[:, co:co + 128],
                            xnh[:, ft * 512:(ft + 1) * 512],
                            xnl[:, ft * 512:(ft + 1) * 512],
                            start=(ft == 0), stop=(ft == FT - 1))
                    mmv = rec.mark("tensor", "tns")
                    rec.wait("vector", "tns", mmv)
                    rec.wait_cur("vector", "dmag")
                    rec.op("vector", lambda eng, bank=bank, h=h: eng.tensor_mul(
                        vtile[:], bank[:], csb[:, h * 512:(h + 1) * 512]), inc="vec")
                    pp_free[bi] = ctx.cnt["vec"]
                    rec.fence("vector")
                    rec.op("vector", lambda eng: eng.tensor_copy(vh[:], vtile[:]))
                    rec.fence("vector")
                    rec.op("vector", lambda eng: eng.tensor_copy(
                        bufs["tmp1"][:, 0:512], vh[:]))
                    rec.fence("vector")
                    rec.op("vector", lambda eng: eng.tensor_sub(
                        vtile[:], vtile[:], bufs["tmp1"][:, 0:512]))
                    rec.fence("vector")
                    rec.op("vector", lambda eng: eng.tensor_copy(vl[:], vtile[:]), inc="vec")
                    vv = ctx.cnt["vec"]
                    rec.wait("gpsimd", "vec", vv)
                    rec.dma("gpsimd", t["s_vh"][h * 128:(h + 1) * 128, tc * 512:(tc + 1) * 512], vh[:])
                    rec.dma("gpsimd", t["s_vl"][h * 128:(h + 1) * 128, tc * 512:(tc + 1) * 512], vl[:])
    ctx.barrier(rec)
    rec.flush()


def _row_bcast4(rec, ctx, sb, src_cols, ptr, pbc, evac_sb, n=4):
    rec.wait_cur("tensor", "vec")
    for s in range(n):
        rec.op("tensor", lambda eng, s=s: eng.transpose(
            ptr[:], src_cols[:, s:s + 1], sb["ident"][:]), inc="tns")
        rec.wait_cur("vector", "tns")
        rec.op("vector", lambda eng: eng.tensor_copy(sb["csrow"][:], ptr[0:1, 0:128]), inc="vec")
        vv = ctx.cnt["vec"]
        rec.wait("tensor", "vec", vv)
        rec.op("tensor", lambda eng, s=s, n=n: eng.matmul(
            pbc[:, s * 128:(s + 1) * 128], sb["onesrow"][:], sb["csrow"][:],
            start=(s == 0), stop=(s == n - 1)), inc="tns")
    tv = ctx.cnt["tns"]
    rec.wait("vector", "tns", tv)
    rec.op("vector", lambda eng, n=n: eng.tensor_copy(evac_sb[:], pbc[:, 0:n * 128]), inc="vec")
    rec.fence("vector")
    return ctx.cnt["vec"]


def p2d_ffn0(rec, ctx, t, sb):
    nc = ctx.nc
    for tc in range(3):
        with ExitStack() as ph:
            a = lambda n, shape, dt=F32: ph.enter_context(
                nc.sbuf_tensor(f"p2d_{n}_{tc}", shape, dt))
            xqc = a("xqc", [128, FT * 512], BF16)
            wgq = a("wgq", [128, FT * 512], BF16)
            wuq = a("wuq", [128, FT * 512], BF16)
            hfe = a("hfe", [128, 44 * 512])
            hqf = a("hqf", [128, 44 * 512], BF16)
            wdq = a("wdq", [128, 44 * 128], BF16)
            wdqb = wdq
            sxb = a("sxb", [128, 512])
            shb = a("shb", [128, 512])
            g1t = a("g1t", [128, 512])
            g1tb = a("g1tb", [128, 512])
            slt = a("slt", [128, 512])
            sltb = a("sltb", [128, 512])
            g1t2 = [g1t, g1tb]
            slt2 = [slt, sltb]
            pair_free = [0, 0]
            pt_free = [0, 0]
            hamax = a("hamax", [128, 4])
            hsc = a("hsc", [128, 4])
            tmp0 = a("tmp0", [128, 4])
            with ExitStack() as ps:
                ppg2 = [ps.enter_context(nc.psum_tensor(f"p2d_ppg{i}_{tc}", [128, 512], F32))
                        for i in range(2)]
                ppu2 = [ps.enter_context(nc.psum_tensor(f"p2d_ppu{i}_{tc}", [128, 512], F32))
                        for i in range(2)]
                ppg, ppu = ppg2[0], ppu2[0]
                ptA = ps.enter_context(nc.psum_tensor(f"p2d_ptA_{tc}", [128, 128], F32))
                ptB = ps.enter_context(nc.psum_tensor(f"p2d_ptB_{tc}", [128, 128], F32))
                ptr = ps.enter_context(nc.psum_tensor(f"p2d_ptr_{tc}", [1, 128], F32))
                pbc = ps.enter_context(nc.psum_tensor(f"p2d_pbc_{tc}", [128, 512], F32))
                for ft in range(FT):
                    rec.dma("sync", xqc[:, ft * 512:(ft + 1) * 512],
                            t["s_xq0"][ft * 128:(ft + 1) * 128, tc * 512:(tc + 1) * 512])
                xq_dma = ctx.cnt["dma"]
                _row_bcast4(rec, ctx, sb, sb["dxq0"][:, tc * 4:tc * 4 + 4], ptr, pbc, sxb)
                rec.wait("tensor", "dma", xq_dma)
                rec.op("gpsimd", lambda eng: eng.memset(hamax[:], 0.0), inc="gps")
                rec.wait_cur("vector", "gps")
                for oc in range(11):
                    rec.wait_cur("sync", "tns")
                    for ft in range(FT):
                        rec.dma("sync", wgq[:, ft * 512:(ft + 1) * 512],
                                t["s_wq0"][ft * 128:(ft + 1) * 128, oc * 512:(oc + 1) * 512])
                        rec.dma("sync", wuq[:, ft * 512:(ft + 1) * 512],
                                t["s_wq0"][ft * 128:(ft + 1) * 128,
                                           I0 + oc * 512:I0 + (oc + 1) * 512], sem="dmb")
                    rec.wait("tensor", "dma", ctx.cnt["dma"])
                    rec.wait("tensor", "dmb", ctx.cnt["dmb"])
                    for g in range(4):
                        gt = oc * 4 + g
                        pi = gt % 2
                        pg, pu = ppg2[pi], ppu2[pi]
                        rec.wait("tensor", "vec", pair_free[pi])
                        for ft in range(FT):
                            rec.op("tensor", lambda eng, ft=ft, g=g, pg=pg: eng.matmul(
                                pg[:], wgq[:, ft * 512 + g * 128:ft * 512 + g * 128 + 128],
                                xqc[:, ft * 512:(ft + 1) * 512],
                                start=(ft == 0), stop=(ft == FT - 1)))
                        for ft in range(FT):
                            rec.op("tensor", lambda eng, ft=ft, g=g, pu=pu: eng.matmul(
                                pu[:], wuq[:, ft * 512 + g * 128:ft * 512 + g * 128 + 128],
                                xqc[:, ft * 512:(ft + 1) * 512],
                                start=(ft == 0), stop=(ft == FT - 1)))
                        mmv = rec.mark("tensor", "tns")
                        rec.wait("vector", "tns", mmv)
                        rec.wait_cur("vector", "act")
                        rec.op("vector", lambda eng, pg=pg, g1x=g1t2[pi]: eng.tensor_mul(
                            g1x[:], pg[:], sxb[:]), inc="vec")
                        rec.wait_cur("scalar", "vec")
                        rec.op("scalar", lambda eng, gt=gt, g1x=g1t2[pi], slx=slt2[pi]: eng.activation(
                            slx[:], g1x[:], AF.Silu, bias=sb["zero"][:],
                            scale=sb["sgu0"][:, gt:gt + 1]), inc="act")
                        rec.wait_cur("vector", "act")
                        rec.op("vector", lambda eng, pu=pu, g1x=g1t2[pi]: eng.tensor_mul(
                            g1x[:], pu[:], sxb[:]))
                        rec.fence("vector")
                        pair_free[pi] = rec.op("vector", lambda eng, gt=gt, g1x=g1t2[pi], slx=slt2[pi]: eng.scalar_tensor_tensor(
                            hfe[:, gt * 512:(gt + 1) * 512], g1x[:],
                            sb["sgu0"][:, 44 + gt:45 + gt], slx[:],
                            ALU.mult, ALU.mult), inc="vec")
                        hv = ctx.cnt["vec"]
                        rec.wait("tensor", "vec", hv)
                        for s in range(4):
                            pt = ptA if s % 2 == 0 else ptB
                            rec.wait("tensor", "vec", pt_free[s % 2])
                            tv = rec.op("tensor", lambda eng, pt=pt, gt=gt, s=s: eng.transpose(
                                pt[:], hfe[:, gt * 512 + s * 128:gt * 512 + s * 128 + 128],
                                sb["ident"][:]), inc="tns")
                            rec.wait("vector", "tns", tv)
                            rec.op("vector", lambda eng, pt=pt, s=s: eng.reduce_max(
                                tmp0[:, s:s + 1], pt[:], AX.X, apply_absolute_value=True))
                            pt_free[s % 2] = rec.fence("vector")
                            rec.op("vector", lambda eng, s=s: eng.tensor_max(
                                hamax[:, s:s + 1], hamax[:, s:s + 1], tmp0[:, s:s + 1]),
                                inc="vec")
                rec.fence("vector")
                rec.op("vector", lambda eng: eng.tensor_scalar(
                    hsc[:], hamax[:], 1.0 / 127.0, 1e-8, ALU.mult, ALU.max))
                rec.fence("vector")
                rec.op("vector", lambda eng: eng.reciprocal(tmp0[:], hsc[:]), inc="vec")
                _row_bcast4(rec, ctx, sb, tmp0, ptr, pbc, sxb)
                _row_bcast4(rec, ctx, sb, hsc, ptr, pbc, shb)
                for gt in range(44):
                    sl = slice(gt * 512, (gt + 1) * 512)
                    rec.op("vector", lambda eng, sl=sl: eng.tensor_mul(
                        hfe[:, sl], hfe[:, sl], sxb[:]))
                    rec.fence("vector")
                    rec.op("vector", lambda eng, sl=sl: eng.tensor_scalar_add(
                        hfe[:, sl], hfe[:, sl], sb["magicM"][:]))
                    rec.fence("vector")
                    rec.op("vector", lambda eng, sl=sl: eng.tensor_single_scalar(
                        hqf[:, sl], hfe[:, sl], sb["magicM"][:], ALU.subtract), inc="vec")
                hq_done = ctx.cnt["vec"]
                if tc == 0 and "dbg_hqf" in t:
                    rec.wait("gpsimd", "vec", hq_done)
                    rec.dma("gpsimd", t["dbg_hqf"], hqf[:])
                    rec.dma("gpsimd", t["dbg_hamax"], hamax[:])
                    rec.dma("gpsimd", t["dbg_sxb"], sxb[:])
                    rec.dma("gpsimd", t["dbg_shb"], shb[:])
                rec.wait("sync", "vec", hq_done)
                wdq_last = 0
                dn_free = [0, 0]
                for o2 in range(FT):
                    pi = o2 % 2
                    wd = wdq
                    rec.wait("sync", "tns", wdq_last)
                    for it in range(44):
                        rec.dma("sync", wd[:, it * 128:(it + 1) * 128],
                                t["s_wd0"][it * 128:(it + 1) * 128, o2 * 128:(o2 + 1) * 128],
                                sem=("dma", "dmb", "out")[it % 3])
                    rec.wait("tensor", "dma", ctx.cnt["dma"])
                    rec.wait("tensor", "dmb", ctx.cnt["dmb"])
                    rec.wait("tensor", "out", ctx.cnt["out"])
                    rec.wait("tensor", "vec", dn_free[pi])
                    for it in range(44):
                        rec.op("tensor", lambda eng, it=it, pg=ppg2[pi], wd=wd: eng.matmul(
                            pg[:], wd[:, it * 128:(it + 1) * 128],
                            hqf[:, it * 512:(it + 1) * 512],
                            start=(it == 0), stop=(it == 43)))
                    mmv = rec.mark("tensor", "tns")
                    wdq_last = mmv
                    rec.wait("vector", "tns", mmv)
                    rec.wait_cur("vector", "act")
                    dn_free[pi] = rec.op("vector", lambda eng, pg=ppg2[pi], g1x=g1t2[pi]: eng.tensor_mul(
                        g1x[:], pg[:], shb[:]), inc="vec")
                    vv = ctx.cnt["vec"]
                    rec.wait("scalar", "vec", vv)
                    rec.wait_cur("scalar", "dmag")
                    rec.op("scalar", lambda eng, o2=o2, g1x=g1t2[pi], slx=slt2[pi]: eng.activation(
                        slx[:], g1x[:], AF.Silu, bias=sb["zero"][:],
                        scale=sb["sd0"][:, o2:o2 + 1]), inc="act")
                    av = ctx.cnt["act"]
                    rec.wait("gpsimd", "act", av)
                    rec.dma("gpsimd", t["s_g0"][o2 * 128:(o2 + 1) * 128, tc * 512:(tc + 1) * 512],
                            slt2[pi][:])
                ctx.barrier(rec)
                rec.flush()
    ctx.barrier(rec)
    rec.flush()


def p2e_outproj_xl1(rec, ctx, t, sb):
    nc = ctx.nc
    for half in range(2):
        with ExitStack() as ph:
            a = lambda n, shape, dt=F32: ph.enter_context(
                nc.sbuf_tensor(f"p2e_{n}_{half}", shape, dt))
            woh = a("woh", [128, FT * 1024], BF16)
            wol = a("wol", [128, FT * 1024], BF16)
            wtmp = a("wtmp", [128, 1024])
            tmp1 = a("tmp1", [128, 1024])
            vh = a("vh", [128, FT * 512], BF16)
            vl = a("vl", [128, FT * 512], BF16)
            g0c = a("g0c", [128, 8 * 512])
            xfc = a("xfc", [128, 8 * 512])
            s0t = a("s0t", [128, 512])
            xlt = a("xlt", [128, 512])
            with ExitStack() as ps:
                pp = [ps.enter_context(nc.psum_tensor(f"p2e_pp{i}_{half}", [128, 512], F32))
                      for i in range(4)]
                for ft in range(FT):
                    wsem = ("dma", "dmb")[ft % 2]
                    rec.wait_cur("sync", "vec")
                    rec.dma("sync", wtmp[:],
                            t["wo0"][ft * 128:(ft + 1) * 128, half * 1024:(half + 1) * 1024],
                            sem=wsem)
                    rec.wait_cur("vector", wsem)
                    co = ft * 1024
                    rec.op("vector", lambda eng, co=co: eng.tensor_copy(woh[:, co:co + 1024], wtmp[:]))
                    rec.fence("vector")
                    rec.op("vector", lambda eng, co=co: eng.tensor_copy(
                        tmp1[:], woh[:, co:co + 1024]))
                    rec.fence("vector")
                    rec.op("vector", lambda eng: eng.tensor_sub(wtmp[:], wtmp[:], tmp1[:]))
                    rec.fence("vector")
                    rec.op("vector", lambda eng, co=co: eng.tensor_copy(wol[:, co:co + 1024], wtmp[:]),
                           inc="vec")
                wsplit = ctx.cnt["vec"]
                for tc in range(3):
                    rec.wait_cur("sync", "tns")
                    for ft in range(FT):
                        rec.dma("sync", vh[:, ft * 512:(ft + 1) * 512],
                                t["s_vh"][ft * 128:(ft + 1) * 128, tc * 512:(tc + 1) * 512])
                        rec.dma("sync", vl[:, ft * 512:(ft + 1) * 512],
                                t["s_vl"][ft * 128:(ft + 1) * 128, tc * 512:(tc + 1) * 512],
                                sem="dmb")
                    rec.wait_cur("sync", "vec")
                    for ot in range(8):
                        g = half * 8 + ot
                        rec.dma("sync", g0c[:, ot * 512:(ot + 1) * 512],
                                t["s_g0"][g * 128:(g + 1) * 128, tc * 512:(tc + 1) * 512],
                                sem="out")
                        rec.dma("sync", xfc[:, ot * 512:(ot + 1) * 512],
                                t["xatt"][g * 128:(g + 1) * 128,
                                          512 + tc * 512:512 + (tc + 1) * 512],
                                sem="out")
                    rec.wait("tensor", "dma", ctx.cnt["dma"])
                    rec.wait("tensor", "dmb", ctx.cnt["dmb"])
                    rec.wait("tensor", "out", ctx.cnt["out"])
                    rec.wait("tensor", "vec", wsplit)
                    pp_free = [0, 0, 0, 0]
                    for ot in range(8):
                        g = half * 8 + ot
                        bi = ot % 4
                        bank = pp[bi]
                        rec.wait("tensor", "vec", pp_free[bi])
                        for ft in range(FT):
                            co = ft * 1024 + ot * 128
                            mm3(rec, bank[:],
                                woh[:, co:co + 128], wol[:, co:co + 128],
                                vh[:, ft * 512:(ft + 1) * 512],
                                vl[:, ft * 512:(ft + 1) * 512],
                                start=(ft == 0), stop=(ft == FT - 1))
                        mmv = rec.mark("tensor", "tns")
                        rec.wait("vector", "tns", mmv)
                        rec.wait_cur("vector", "dmag")
                        pp_free[bi] = rec.op("vector", lambda eng, bank=bank, ot=ot: eng.tensor_mul(
                            s0t[:], bank[:], g0c[:, ot * 512:(ot + 1) * 512]), inc="vec")
                        rec.fence("vector")
                        rec.op("vector", lambda eng, ot=ot: eng.tensor_add(
                            xlt[:], s0t[:], xfc[:, ot * 512:(ot + 1) * 512]), inc="vec")
                        vv = ctx.cnt["vec"]
                        rec.wait("gpsimd", "vec", vv)
                        rec.dma("gpsimd", t["s_xl1"][g * 128:(g + 1) * 128,
                                                     tc * 512:(tc + 1) * 512], xlt[:])
                ctx.barrier(rec)
                rec.flush()
    ctx.barrier(rec)
    rec.flush()


def p3a_l1_proj(rec, ctx, t, sb):
    nc = ctx.nc
    with ExitStack() as ph:
        a = lambda n, shape, dt=F32: ph.enter_context(nc.sbuf_tensor("p3a_" + n, shape, dt))
        xn1h = a("xn1h", [128, FT * TF], BF16)
        xn1l = a("xn1l", [128, FT * TF], BF16)
        with ExitStack() as ps:
            c = lambda n, shape, dt=F32: ps.enter_context(nc.sbuf_tensor("p3a_" + n, shape, dt))
            bufs = {
                "xch": c("xch", [128, FT * 128]),
                "xT": c("xT", [128, D]), "xnT": c("xnT", [128, D]),
                "stats": c("stats", [128, 24]), "mv": c("mv", [128, 2]),
                "rr": c("rr", [128, 1]), "b2": c("b2", [128, 1]),
                "tmp1": c("tmp1", [128, 1024]),
                "amax": c("amax", [128, 1]), "sc": c("sc", [128, 1]),
                "rin": c("rin", [128, 1]),
                "pb0": ps.enter_context(nc.psum_tensor("p3a_pb0", [128, 512], F32)),
                "pb1": ps.enter_context(nc.psum_tensor("p3a_pb1", [128, 512], F32)),
            }
            bufs["aT"] = bufs["xT"]
            qstg = [ps.enter_context(nc.sbuf_tensor(f"p3a_qstg{i}", [128, 128], BF16))
                    for i in range(2)]

            class _View:
                def __init__(self, slab, tc):
                    self.slab, self.tc = slab, tc

                def __getitem__(self, key):
                    p_sl, c_sl = key
                    ft = c_sl.start // 512
                    off = c_sl.start - ft * 512
                    base = ft * TF + self.tc * 512 + off
                    return self.slab[p_sl, base:base + (c_sl.stop - c_sl.start)]
            for tc in range(3):
                out = {"hi": _View(xn1h, tc), "lo": _View(xn1l, tc)}
                if tc >= 1:
                    out["quant"] = {"kind": "l1", "dma_to": (t["s_xq1"], (tc - 1) * 512),
                                    "stage": qstg,
                                    "deq_cols": sb["dxq1"][:, (tc - 1) * 4:(tc - 1) * 4 + 4]}
                ln_quant_chunk(rec, ctx, sb, bufs, t["s_xl1"], tc * 512, 512, "split",
                               out=out)
            ctx.barrier(rec)
            rec.flush()
        es_r = ExitStack()
        sb["ropeq_c"] = es_r.enter_context(nc.sbuf_tensor("c_ropeq_c", [HD, TOWN], F32))
        sb["ropeq_s"] = es_r.enter_context(nc.sbuf_tensor("c_ropeq_s", [HD, TOWN], F32))
        sb["ropek_c"] = es_r.enter_context(nc.sbuf_tensor("c_ropek_c", [HD, TF], F32))
        sb["ropek_s"] = es_r.enter_context(nc.sbuf_tensor("c_ropek_s", [HD, TF], F32))
        for name in ["ropeq_c", "ropeq_s", "ropek_c", "ropek_s"]:
            rec.dma("sync", sb[name][:], t[name], sem="out")
        rope_dma = ctx.cnt["out"]
        for which, wf in [("k", 0), ("k", 1), ("k", 2), ("k", 3),
                          ("q", 0), ("q", 1), ("q", 2), ("q", 3)]:
            with ExitStack() as ps:
                b = lambda n, shape, dt=F32: ps.enter_context(
                    nc.sbuf_tensor(f"p3a_{n}_{which}{wf}", shape, dt))
                wh = b("wh", [128, FT * 512], BF16)
                wl = b("wl", [128, FT * 512], BF16)
                wtmp = b("wtmp", [128, 512])
                wtm2 = b("wtm2", [128, 512])
                T = TF if which == "k" else TOWN
                col0 = 0 if which == "k" else 512
                kf = b("kf", [128, T])
                ks = b("ks", [128, T])
                kt = b("kt", [128, T])
                kh2 = b("kh2", [128, T], BF16)
                rcs = sb["ropek_c"] if which == "k" else sb["ropeq_c"]
                rss = sb["ropek_s"] if which == "k" else sb["ropeq_s"]
                pj = [ps.enter_context(nc.psum_tensor(f"p3a_pj{i}_{which}{wf}", [128, 512], F32))
                      for i in range(3)]
                wsrc = t["l1wk"] if which == "k" else t["l1wq"]
                for ft in range(FT):
                    wsem = ("dma", "dmb")[ft % 2]
                    rec.wait_cur("sync", "vec")
                    rec.dma("sync", wtmp[:], wsrc[ft * 128:(ft + 1) * 128,
                                                  wf * 512:(wf + 1) * 512], sem=wsem)
                    rec.wait_cur("vector", wsem)
                    co = ft * 512
                    rec.op("vector", lambda eng, co=co: eng.tensor_copy(wh[:, co:co + 512], wtmp[:]))
                    rec.fence("vector")
                    rec.op("vector", lambda eng, co=co: eng.tensor_copy(wtm2[:], wh[:, co:co + 512]))
                    rec.fence("vector")
                    rec.op("vector", lambda eng: eng.tensor_sub(wtmp[:], wtmp[:], wtm2[:]))
                    rec.fence("vector")
                    rec.op("vector", lambda eng, co=co: eng.tensor_copy(wl[:, co:co + 512], wtmp[:]),
                           inc="vec")
                wdone = ctx.cnt["vec"]
                rec.wait("vector", "out", rope_dma)
                rec.wait("tensor", "vec", wdone)
                for h in range(wf * 4, wf * 4 + 4):
                    for tcc in range(T // 512):
                        bank = pj[tcc % 3]
                        rec.wait_cur("tensor", "vec")
                        for ft in range(FT):
                            co = ft * 512 + (h - wf * 4) * 128
                            cx = ft * TF + col0 + tcc * 512
                            mm3(rec, bank[:],
                                wh[:, co:co + 128], wl[:, co:co + 128],
                                xn1h[:, cx:cx + 512], xn1l[:, cx:cx + 512],
                                start=(ft == 0), stop=(ft == FT - 1))
                        mmv = rec.mark("tensor", "tns")
                        rec.wait("vector", "tns", mmv)
                        rec.op("vector", lambda eng, bank=bank, tcc=tcc: eng.tensor_copy(
                            kf[:, tcc * 512:(tcc + 1) * 512], bank[:]), inc="vec")
                    mask = []
                    for i in range(0, 32, 2):
                        mask += [i + 1, i]
                    rec.fence("vector")
                    rec.op("vector", lambda eng: eng.stream_shuffle(ks[:], kf[:], mask))
                    rec.fence("vector")
                    rec.op("vector", lambda eng: eng.tensor_mul(kt[:], kf[:], rcs[:]))
                    rec.op("vector", lambda eng: eng.tensor_mul(ks[:], ks[:], rss[:]))
                    rec.fence("vector")
                    rec.op("vector", lambda eng: eng.tensor_add(kf[:], kt[:], ks[:]))
                    rec.fence("vector")
                    rec.op("vector", lambda eng: eng.tensor_copy(kh2[:], kf[:]))
                    rec.fence("vector")
                    rec.op("vector", lambda eng: eng.tensor_copy(kt[:], kh2[:]))
                    rec.fence("vector")
                    rec.op("vector", lambda eng: eng.tensor_sub(kf[:], kf[:], kt[:]))
                    rec.fence("vector")
                    dsth = t["s_krh"] if which == "k" else t["s_qrh"]
                    dstl = t["s_krl"] if which == "k" else t["s_qrl"]
                    rec.op("vector", lambda eng: eng.tensor_copy(ks[:], kf[:]), inc="vec")
                    rec.fence("vector")
                    vv = ctx.cnt["vec"]
                    rec.wait("gpsimd", "vec", vv)
                    rec.wait_cur("gpsimd", "dmag")
                    rec.dma("gpsimd", dsth[h * 128:(h + 1) * 128, :], kh2[:])
                    rec.wait_cur("vector", "dmag")
                    rec.op("vector", lambda eng: eng.tensor_copy(kh2[:], ks[:]), inc="vec")
                    vv = ctx.cnt["vec"]
                    rec.wait("gpsimd", "vec", vv)
                    rec.dma("gpsimd", dstl[h * 128:(h + 1) * 128, :], kh2[:])
                    rec.wait_cur("vector", "dmag")
                ctx.barrier(rec)
                rec.flush()
        es_r.close()
        for wf in range(4):
          with ExitStack() as ps:
            b = lambda n, shape, dt=F32: ps.enter_context(
                nc.sbuf_tensor(f"p3a_{n}_v{wf}", shape, dt))
            wh = b("wh", [128, FT * 512], BF16)
            wl = b("wl", [128, FT * 512], BF16)
            wtmp = b("wtmp", [128, 512])
            wtm2 = b("wtm2", [128, 512])
            vts = b("vts", [128, 512])
            pj = [ps.enter_context(nc.psum_tensor(f"p3a_pjv{i}{wf}", [128, 512], F32))
                  for i in range(2)]
            for ft in range(FT):
                wsem = ("dma", "dmb")[ft % 2]
                rec.wait_cur("sync", "vec")
                rec.dma("sync", wtmp[:], t["l1wv"][ft * 128:(ft + 1) * 128,
                                                   wf * 512:(wf + 1) * 512], sem=wsem)
                rec.wait_cur("vector", wsem)
                co = ft * 512
                rec.op("vector", lambda eng, co=co: eng.tensor_copy(wh[:, co:co + 512], wtmp[:]))
                rec.fence("vector")
                rec.op("vector", lambda eng, co=co: eng.tensor_copy(wtm2[:], wh[:, co:co + 512]))
                rec.fence("vector")
                rec.op("vector", lambda eng: eng.tensor_sub(wtmp[:], wtmp[:], wtm2[:]))
                rec.fence("vector")
                rec.op("vector", lambda eng, co=co: eng.tensor_copy(wl[:, co:co + 512], wtmp[:]),
                       inc="vec")
            wdone = ctx.cnt["vec"]
            rec.wait("tensor", "vec", wdone)
            for jt in range(12):
                for vo in [wf]:
                    bank = pj[vo % 2]
                    rec.wait_cur("tensor", "vec")
                    for ft in range(FT):
                        cx = ft * TF + jt * 128
                        co = ft * 512
                        mm3(rec, bank[:],
                            xn1h[:, cx:cx + 128], xn1l[:, cx:cx + 128],
                            wh[:, co:co + 512], wl[:, co:co + 512],
                            start=(ft == 0), stop=(ft == FT - 1))
                    mmv = rec.mark("tensor", "tns")
                    rec.wait("vector", "tns", mmv)
                    rec.wait_cur("vector", "dmag")
                    rec.op("vector", lambda eng, bank=bank: eng.tensor_copy(vts[:], bank[:]),
                           inc="vec")
                    vv = ctx.cnt["vec"]
                    rec.wait("gpsimd", "vec", vv)
                    rec.dma("gpsimd", t["s_vT"][jt * 128:(jt + 1) * 128, vo * 512:(vo + 1) * 512],
                            vts[:])
            ctx.barrier(rec)
            rec.flush()
    ctx.barrier(rec)
    rec.flush()


def p3b_l1_attn(rec, ctx, t, sb):
    nc = ctx.nc
    with ExitStack() as ph:
        a = lambda n, shape, dt=F32: ph.enter_context(nc.sbuf_tensor("p3b_" + n, shape, dt))
        for name in ["maskA", "maskB"]:
            sb[name] = ph.enter_context(nc.sbuf_tensor("c_" + name, [128, 8 * 512], BF16))
        for name in ["maskA", "maskB"]:
            for jt in range(8):
                rec.dma("sync", sb[name][:, jt * 512:(jt + 1) * 512],
                        t[name][jt * 128:(jt + 1) * 128, :])
        mdma = ctx.cnt["dma"]
        woh = a("woh", [128, FT * 1024], BF16)
        wol = a("wol", [128, FT * 1024], BF16)
        wtmp = a("wtmp", [128, 1024])
        wtm2 = a("wtm2", [128, 1024])
        khb = a("khb", [128, 1024], BF16)
        klb = a("klb", [128, 1024], BF16)
        qhb = a("qhb", [128, 512], BF16)
        qlb = a("qlb", [128, 512], BF16)
        eb = a("eb", [128, 8 * 512])
        zsb = a("zsb", [128, 4])
        zr = a("zr", [128, 4])
        zbc = a("zbc", [128, 512])
        vts = a("vts", [128, 128])
        onorm = a("onorm", [128, 512])
        oh = a("oh", [128, NH * 512], BF16)
        ol = a("ol", [128, NH * 512], BF16)
        s1e = a("s1e", [128, 512])
        with ExitStack() as ps:
            psc = [ps.enter_context(nc.psum_tensor(f"p3b_psc{i}", [128, 512], F32))
                   for i in range(2)]
            psz = ps.enter_context(nc.psum_tensor("p3b_psz", [128, 512], F32))
            pso = ps.enter_context(nc.psum_tensor("p3b_pso", [128, 512], F32))
            ptr = ps.enter_context(nc.psum_tensor("p3b_ptr", [1, 128], F32))
            pbc = ps.enter_context(nc.psum_tensor("p3b_pbc", [128, 512], F32))
            rec.wait("vector", "dma", mdma)
            for qb in range(2):
                for h in range(NH):
                    rec.wait_cur("sync", "tns")
                    rec.dma("sync", khb[:], t["s_krh"][h * 128:(h + 1) * 128,
                                                       qb * 512:qb * 512 + 1024])
                    rec.dma("sync", klb[:], t["s_krl"][h * 128:(h + 1) * 128,
                                                       qb * 512:qb * 512 + 1024])
                    rec.dma("sync", qhb[:], t["s_qrh"][h * 128:(h + 1) * 128,
                                                       qb * 512:(qb + 1) * 512])
                    rec.dma("sync", qlb[:], t["s_qrl"][h * 128:(h + 1) * 128,
                                                       qb * 512:(qb + 1) * 512])
                    kdma = ctx.cnt["dma"]
                    rec.wait("tensor", "dma", kdma)
                    mname = "maskA" if qb == 0 else "maskB"
                    rec.wait_cur("scalar", "tns")
                    for jt in range(8):
                        bank = psc[jt % 2]
                        rec.wait_cur("tensor", "act")
                        mm3(rec, bank[:],
                            khb[:, jt * 128:(jt + 1) * 128], klb[:, jt * 128:(jt + 1) * 128],
                            qhb[:], qlb[:],
                            start=True, stop=True)
                        mmv = rec.mark("tensor", "tns")
                        rec.wait("vector", "tns", mmv)
                        rec.op("vector", lambda eng, bank=bank, jt=jt, mname=mname: eng.tensor_scalar(
                            bank[:], bank[:], SC_ATT, -C_SM, ALU.mult, ALU.add)
                            if False else eng.tensor_add(
                            bank[:], bank[:], sb[mname][:, jt * 512:(jt + 1) * 512]), inc="vec")
                        vv = ctx.cnt["vec"]
                        rec.wait("scalar", "vec", vv)
                        rec.op("scalar", lambda eng, bank=bank, jt=jt: eng.activation(
                            eb[:, jt * 512:(jt + 1) * 512], bank[:], AF.Exp,
                            bias=sb["negC"][:], scale=SC_ATT), inc="act")
                    edone = ctx.cnt["act"]
                    rec.wait("tensor", "act", edone)
                    for jt in range(8):
                        for qs in range(4):
                            rec.op("tensor", lambda eng, jt=jt, qs=qs: eng.matmul(
                                psz[:, qs:qs + 1],
                                eb[:, jt * 512 + qs * 128:jt * 512 + qs * 128 + 128],
                                sb["onescol"][:],
                                start=(jt == 0 and qs == 0), stop=(jt == 7 and qs == 3)))
                    zt = rec.mark("tensor", "tns")
                    rec.wait("vector", "tns", zt)
                    rec.op("vector", lambda eng: eng.tensor_copy(zsb[:], psz[:, 0:4]))
                    rec.fence("vector")
                    rec.op("vector", lambda eng: eng.reciprocal(zr[:], zsb[:]), inc="vec")
                    _row_bcast4(rec, ctx, sb, zr, ptr, pbc, zbc)
                    for jt in range(8):
                        gjt = qb * 4 + jt
                        rec.wait_cur("sync", "vec" if jt == 0 else "tns")
                        rec.dma("sync", vts[:], t["s_vT"][gjt * 128:(gjt + 1) * 128,
                                                          h * 128:(h + 1) * 128])
                        vdma = ctx.cnt["dma"]
                        rec.wait("tensor", "dma", vdma)
                        for qs in range(2):
                            rec.op("tensor", lambda eng, jt=jt, qs=qs: eng.matmul(
                                pso[:, qs * 256:(qs + 1) * 256], vts[:],
                                eb[:, jt * 512 + qs * 256:jt * 512 + qs * 256 + 256],
                                start=(jt == 0 and qs == 0), stop=(jt == 7 and qs == 1)))
                        rec.mark("tensor", "tns")
                    ot = ctx.cnt["tns"]
                    rec.wait("vector", "tns", ot)
                    rec.op("vector", lambda eng: eng.tensor_mul(onorm[:], pso[:], zbc[:]))
                    rec.fence("vector")
                    co = h * 512
                    rec.op("vector", lambda eng, co=co: eng.tensor_copy(oh[:, co:co + 512], onorm[:]))
                    rec.fence("vector")
                    rec.op("vector", lambda eng, co=co: eng.tensor_copy(zbc[:], oh[:, co:co + 512]))
                    rec.fence("vector")
                    rec.op("vector", lambda eng: eng.tensor_sub(onorm[:], onorm[:], zbc[:]))
                    rec.fence("vector")
                    rec.op("vector", lambda eng, co=co: eng.tensor_copy(ol[:, co:co + 512], onorm[:]),
                           inc="vec")
                rec.wait("tensor", "vec", ctx.cnt["vec"])
                for ohalf in range(2):
                    for ft in range(FT):
                        wsem = ("dma", "dmb")[ft % 2]
                        rec.wait_cur("sync", "vec")
                        rec.wait_cur("sync", "tns")
                        rec.dma("sync", wtmp[:],
                                t["l1wo"][ft * 128:(ft + 1) * 128,
                                          ohalf * 1024:(ohalf + 1) * 1024], sem=wsem)
                        rec.wait_cur("vector", wsem)
                        co = ft * 1024
                        rec.op("vector", lambda eng, co=co: eng.tensor_copy(
                            woh[:, co:co + 1024], wtmp[:]))
                        rec.fence("vector")
                        rec.op("vector", lambda eng, co=co: eng.tensor_copy(
                            wtm2[:], woh[:, co:co + 1024]))
                        rec.fence("vector")
                        rec.op("vector", lambda eng: eng.tensor_sub(wtmp[:], wtmp[:], wtm2[:]))
                        rec.fence("vector")
                        rec.op("vector", lambda eng, co=co: eng.tensor_copy(
                            wol[:, co:co + 1024], wtmp[:]), inc="vec")
                    wodone = ctx.cnt["vec"]
                    rec.wait("tensor", "vec", wodone)
                    for ot2 in range(ohalf * 8, ohalf * 8 + 8):
                        bank = psc[ot2 % 2]
                        rec.wait_cur("tensor", "vec")
                        for hh in range(NH):
                            co = hh * 1024 + (ot2 - ohalf * 8) * 128
                            mm3(rec, bank[:],
                                woh[:, co:co + 128], wol[:, co:co + 128],
                                oh[:, hh * 512:(hh + 1) * 512], ol[:, hh * 512:(hh + 1) * 512],
                                start=(hh == 0), stop=(hh == NH - 1))
                        mmv = rec.mark("tensor", "tns")
                        rec.wait("vector", "tns", mmv)
                        rec.wait_cur("vector", "dmag")
                        rec.op("vector", lambda eng, bank=bank: eng.tensor_copy(s1e[:], bank[:]),
                               inc="vec")
                        vv = ctx.cnt["vec"]
                        rec.wait("gpsimd", "vec", vv)
                        rec.dma("gpsimd", t["s_s1"][ot2 * 128:(ot2 + 1) * 128,
                                                    qb * 512:(qb + 1) * 512], s1e[:])
                ctx.barrier(rec)
                rec.flush()
    ctx.barrier(rec)
    rec.flush()


def p4a_ffn1_gateup(rec, ctx, t, sb):
    nc = ctx.nc
    with ExitStack() as ph:
        a = lambda n, shape, dt=F32: ph.enter_context(nc.sbuf_tensor("p4a_" + n, shape, dt))
        xq1c = a("xq1c", [128, FT * TOWN], BF16)
        wgq = a("wgq", [128, FT * 512], BF16)
        wuq = a("wuq", [128, FT * 512], BF16)
        sxb1 = a("sxb1", [128, 1024])
        g1t = a("g1t", [128, 512])
        slt = a("slt", [128, 512])
        hstr = a("hstr", [128, 512])
        with ExitStack() as ps:
            ppg = ps.enter_context(nc.psum_tensor("p4a_ppg", [128, 512], F32))
            ppu = ps.enter_context(nc.psum_tensor("p4a_ppu", [128, 512], F32))
            ptr = ps.enter_context(nc.psum_tensor("p4a_ptr", [1, 128], F32))
            pbc = ps.enter_context(nc.psum_tensor("p4a_pbc", [128, 512], F32))
            for ft in range(FT):
                rec.dma("sync", xq1c[:, ft * TOWN:(ft + 1) * TOWN],
                        t["s_xq1"][ft * 128:(ft + 1) * 128, :])
            xdma = ctx.cnt["dma"]
            _row_bcast4(rec, ctx, sb, sb["dxq1"][:, 0:4], ptr, pbc, sxb1[:, 0:512])
            _row_bcast4(rec, ctx, sb, sb["dxq1"][:, 4:8], ptr, pbc, sxb1[:, 512:1024])
            rec.wait("tensor", "dma", xdma)
            for oc in range(16):
                rec.wait_cur("sync", "tns")
                for ft in range(FT):
                    rec.dma("sync", wgq[:, ft * 512:(ft + 1) * 512],
                            t["s_wg1"][ft * 128:(ft + 1) * 128, oc * 512:(oc + 1) * 512])
                    rec.dma("sync", wuq[:, ft * 512:(ft + 1) * 512],
                            t["s_wu1"][ft * 128:(ft + 1) * 128, oc * 512:(oc + 1) * 512],
                            sem="dmb")
                rec.wait("tensor", "dma", ctx.cnt["dma"])
                rec.wait("tensor", "dmb", ctx.cnt["dmb"])
                for g in range(4):
                    gt = oc * 4 + g
                    for th in range(2):
                        rec.wait_cur("tensor", "vec")
                        for ft in range(FT):
                            rec.op("tensor", lambda eng, ft=ft, g=g, th=th: eng.matmul(
                                ppg[:], wgq[:, ft * 512 + g * 128:ft * 512 + g * 128 + 128],
                                xq1c[:, ft * TOWN + th * 512:ft * TOWN + (th + 1) * 512],
                                start=(ft == 0), stop=(ft == FT - 1)))
                        for ft in range(FT):
                            rec.op("tensor", lambda eng, ft=ft, g=g, th=th: eng.matmul(
                                ppu[:], wuq[:, ft * 512 + g * 128:ft * 512 + g * 128 + 128],
                                xq1c[:, ft * TOWN + th * 512:ft * TOWN + (th + 1) * 512],
                                start=(ft == 0), stop=(ft == FT - 1)))
                        mmv = rec.mark("tensor", "tns")
                        rec.wait("vector", "tns", mmv)
                        rec.wait_cur("vector", "dmag")
                        rec.wait_cur("vector", "act")
                        rec.op("vector", lambda eng, th=th: eng.tensor_mul(
                            g1t[:], ppg[:], sxb1[:, th * 512:(th + 1) * 512]), inc="vec")
                        rec.wait_cur("scalar", "vec")
                        rec.op("scalar", lambda eng: eng.activation(
                            slt[:], g1t[:], AF.Silu, bias=sb["zero"][:],
                            scale=sb["deq_g1"][:]), inc="act")
                        rec.wait_cur("vector", "act")
                        rec.op("vector", lambda eng, th=th: eng.tensor_mul(
                            g1t[:], ppu[:], sxb1[:, th * 512:(th + 1) * 512]))
                        rec.fence("vector")
                        rec.op("vector", lambda eng: eng.scalar_tensor_tensor(
                            hstr[:], g1t[:], sb["deq_u1"][:], slt[:],
                            ALU.mult, ALU.mult), inc="vec")
                        vv = ctx.cnt["vec"]
                        rec.wait("gpsimd", "vec", vv)
                        rec.dma("gpsimd", t["s_h1"][gt * 128:(gt + 1) * 128,
                                                    th * 512:(th + 1) * 512], hstr[:])
    ctx.barrier(rec)
    rec.flush()


def p4b_fwht_quant(rec, ctx, t, sb):
    nc = ctx.nc
    NTOK = 256
    for tc in range(4):
        with ExitStack() as ph:
            a = lambda n, shape, dt=F32: ph.enter_context(
                nc.sbuf_tensor(f"p4b_{n}_{tc}", shape, dt))
            hfe = a("hfe", [128, 64 * NTOK])
            bt = a("bt", [128, 32 * NTOK])
            hh = a("hh", [128, 64 * NTOK], BF16)
            hl = a("hl", [128, 64 * NTOK], BF16)
            rb = a("rb", [128, NTOK])
            yamax = a("yamax", [128, 2])
            ysc = a("ysc", [128, 2])
            tmp2 = a("tmp2", [128, 2])
            stg = [a("stg0", [128, NTOK], BF16), a("stg1", [128, NTOK], BF16)]
            with ExitStack() as ps:
                pm = ps.enter_context(nc.psum_tensor(f"p4b_pm_{tc}", [128, NTOK], F32))
                ptA = ps.enter_context(nc.psum_tensor(f"p4b_ptA_{tc}", [128, 128], F32))
                ptB = ps.enter_context(nc.psum_tensor(f"p4b_ptB_{tc}", [128, 128], F32))
                ptr = ps.enter_context(nc.psum_tensor(f"p4b_ptr_{tc}", [1, 128], F32))
                pbc = ps.enter_context(nc.psum_tensor(f"p4b_pbc_{tc}", [128, 512], F32))
                rec.wait_cur("sync", "vec")
                rec.wait_cur("gpsimd", "vec")
                for it in range(64):
                    if it % 4 == 3:
                        rec.dma("gpsimd", hfe[:, it * NTOK:(it + 1) * NTOK],
                                t["s_h1"][it * 128:(it + 1) * 128, tc * NTOK:(tc + 1) * NTOK],
                                sem="dmah")
                    else:
                        rec.dma("sync", hfe[:, it * NTOK:(it + 1) * NTOK],
                                t["s_h1"][it * 128:(it + 1) * 128, tc * NTOK:(tc + 1) * NTOK],
                                sem=("dma", "dmb", "out")[it % 4])
                rec.wait("vector", "dma", ctx.cnt["dma"])
                rec.wait("vector", "dmb", ctx.cnt["dmb"])
                rec.wait("vector", "out", ctx.cnt["out"])
                rec.wait("vector", "dmah", ctx.cnt["dmah"])
                for lvl in range(6):
                    d = 32 >> lvl
                    c = d * NTOK
                    va = hfe[:].rearrange("p (b two c) -> p b two c", two=2, c=c)
                    av = va[:, :, 0:1, :]
                    bv = va[:, :, 1:2, :]
                    btv = bt[:, 0:32 * NTOK]
                    rec.op("vector", lambda eng, av=av, bv=bv, btv=btv: eng.tensor_sub(
                        btv, av, bv))
                    rec.fence("vector")
                    rec.op("vector", lambda eng, av=av, bv=bv: eng.tensor_add(av, av, bv))
                    rec.fence("vector")
                    rec.op("vector", lambda eng, bv=bv, btv=btv: eng.tensor_copy(bv, btv))
                    rec.fence("vector")
                rec.op("vector", lambda eng: eng.tensor_single_scalar(
                    hfe[:], hfe[:], float(2.0 ** -6.5), ALU.mult))
                rec.fence("vector")
                rec.op("vector", lambda eng: eng.tensor_copy(hh[:], hfe[:]))
                rec.fence("vector")
                rec.op("vector", lambda eng: eng.tensor_sub(hfe[:], hfe[:], hh[:]))
                rec.fence("vector")
                rec.op("vector", lambda eng: eng.tensor_copy(hl[:], hfe[:]), inc="vec")
                spl = ctx.cnt["vec"]
                rec.wait("tensor", "vec", spl)
                rec.op("gpsimd", lambda eng: eng.memset(yamax[:], 0.0), inc="gps")
                rec.wait_cur("vector", "gps")
                for it in range(64):
                    rec.op("tensor", lambda eng, it=it: eng.matmul(
                        pm[:], sb["h128b"][:], hh[:, it * NTOK:(it + 1) * NTOK],
                        start=True, stop=False))
                    rec.op("tensor", lambda eng, it=it: eng.matmul(
                        pm[:], sb["h128b"][:], hl[:, it * NTOK:(it + 1) * NTOK],
                        start=False, stop=True))
                    mmv = rec.mark("tensor", "tns")
                    rec.wait("vector", "tns", mmv)
                    rec.op("vector", lambda eng, it=it: eng.tensor_copy(
                        hfe[:, it * NTOK:(it + 1) * NTOK], pm[:]), inc="vec")
                    ev = ctx.cnt["vec"]
                    rec.wait("tensor", "vec", ev)
                    for s in range(2):
                        pt = ptA if s % 2 == 0 else ptB
                        rec.wait_cur("tensor", "vec")
                        rec.op("tensor", lambda eng, pt=pt, it=it, s=s: eng.transpose(
                            pt[:], hfe[:, it * NTOK + s * 128:it * NTOK + s * 128 + 128],
                            sb["ident"][:]), inc="tns")
                        rec.wait_cur("vector", "tns")
                        rec.op("vector", lambda eng, pt=pt, s=s: eng.reduce_max(
                            tmp2[:, s:s + 1], pt[:], AX.X, apply_absolute_value=True))
                        rec.fence("vector")
                        rec.op("vector", lambda eng, s=s: eng.tensor_max(
                            yamax[:, s:s + 1], yamax[:, s:s + 1], tmp2[:, s:s + 1]),
                            inc="vec")
                rec.fence("vector")
                rec.op("vector", lambda eng: eng.tensor_single_scalar(
                    ysc[:], yamax[:], 1e-5, ALU.max))
                rec.fence("vector")
                rec.op("vector", lambda eng: eng.reciprocal(tmp2[:], ysc[:]))
                rec.fence("vector")
                rec.op("vector", lambda eng: eng.tensor_single_scalar(
                    tmp2[:], tmp2[:], 128.0, ALU.mult))
                rec.op("vector", lambda eng, tc=tc: eng.tensor_single_scalar(
                    sb["dyq1"][:, tc * 2:tc * 2 + 2], ysc[:], 1.0 / 128.0, ALU.mult),
                    inc="vec")
                _row_bcast4(rec, ctx, sb, tmp2, ptr, pbc, rb, n=2)
                for it in range(64):
                    sl = slice(it * NTOK, (it + 1) * NTOK)
                    sg = stg[it % 2]
                    rec.op("vector", lambda eng, sl=sl: eng.tensor_mul(hfe[:, sl], hfe[:, sl], rb[:]))
                    rec.fence("vector")
                    rec.op("vector", lambda eng, sl=sl: eng.tensor_scalar_add(
                        hfe[:, sl], hfe[:, sl], sb["magicM"][:]))
                    rec.fence("vector")
                    rec.op("vector", lambda eng, sl=sl: eng.tensor_single_scalar(
                        hfe[:, sl], hfe[:, sl], sb["magicM"][:], ALU.subtract))
                    rec.fence("vector")
                    rec.wait_cur("vector", "dmag")
                    rec.op("vector", lambda eng, sl=sl, sg=sg: eng.tensor_scalar(
                        sg[:], hfe[:, sl], 127.0, -128.0, ALU.min, ALU.max), inc="vec")
                    vv = ctx.cnt["vec"]
                    rec.wait("gpsimd", "vec", vv)
                    rec.dma("gpsimd", t["s_yq1"][it * 128:(it + 1) * 128,
                                                 tc * NTOK:(tc + 1) * NTOK], sg[:])
                ctx.barrier(rec)
                rec.flush()
    ctx.barrier(rec)
    rec.flush()


def p4c_down_out(rec, ctx, t, sb):
    nc = ctx.nc
    for tc2 in range(2):
        with ExitStack() as ph:
            a = lambda n, shape, dt=F32: ph.enter_context(
                nc.sbuf_tensor(f"p4c_{n}_{tc2}", shape, dt))
            yqc = a("yqc", [128, 64 * 512], BF16)
            wdq = a("wdq", [128, 64 * 128], BF16)
            syb = a("syb", [128, 512])
            d1 = a("d1", [128, 512])
            g1c = a("g1c", [128, 512])
            s1c = a("s1c", [128, 512])
            xl1c = a("xl1c", [128, 512])
            with ExitStack() as ps:
                pm = ps.enter_context(nc.psum_tensor(f"p4c_pm_{tc2}", [128, 512], F32))
                ptr = ps.enter_context(nc.psum_tensor(f"p4c_ptr_{tc2}", [1, 128], F32))
                pbc = ps.enter_context(nc.psum_tensor(f"p4c_pbc_{tc2}", [128, 512], F32))
                rec.wait_cur("sync", "tns")
                rec.wait_cur("gpsimd", "tns")
                for it in range(64):
                    if it % 4 == 3:
                        rec.dma("gpsimd", yqc[:, it * 512:(it + 1) * 512],
                                t["s_yq1"][it * 128:(it + 1) * 128, tc2 * 512:(tc2 + 1) * 512],
                                sem="dmah")
                    else:
                        rec.dma("sync", yqc[:, it * 512:(it + 1) * 512],
                                t["s_yq1"][it * 128:(it + 1) * 128, tc2 * 512:(tc2 + 1) * 512],
                                sem=("dma", "dmb", "out")[it % 4])
                _row_bcast4(rec, ctx, sb, sb["dyq1"][:, tc2 * 4:tc2 * 4 + 4], ptr, pbc, syb)
                rec.wait("tensor", "dma", ctx.cnt["dma"])
                rec.wait("tensor", "dmb", ctx.cnt["dmb"])
                rec.wait("tensor", "out", ctx.cnt["out"])
                rec.wait("tensor", "dmah", ctx.cnt["dmah"])
                for o2 in range(FT):
                    rec.wait_cur("sync", "tns")
                    for it in range(64):
                        rec.dma("sync", wdq[:, it * 128:(it + 1) * 128],
                                t["s_wd1"][it * 128:(it + 1) * 128, o2 * 128:(o2 + 1) * 128],
                                sem=("dma", "dmb", "out")[it % 3])
                    rec.wait_cur("sync", "vec")
                    rec.dma("sync", s1c[:], t["s_s1"][o2 * 128:(o2 + 1) * 128,
                                                      tc2 * 512:(tc2 + 1) * 512])
                    rec.dma("sync", xl1c[:], t["s_xl1"][o2 * 128:(o2 + 1) * 128,
                                                        512 + tc2 * 512:512 + (tc2 + 1) * 512],
                            sem="dmb")
                    rec.wait("tensor", "dma", ctx.cnt["dma"])
                    rec.wait("tensor", "dmb", ctx.cnt["dmb"])
                    rec.wait("tensor", "out", ctx.cnt["out"])
                    rec.wait_cur("tensor", "vec")
                    for it in range(64):
                        rec.op("tensor", lambda eng, it=it: eng.matmul(
                            pm[:], wdq[:, it * 128:(it + 1) * 128],
                            yqc[:, it * 512:(it + 1) * 512],
                            start=(it == 0), stop=(it == 63)))
                    mmv = rec.mark("tensor", "tns")
                    rec.wait("vector", "tns", mmv)
                    rec.op("vector", lambda eng: eng.tensor_mul(d1[:], pm[:], syb[:]), inc="vec")
                    vv = ctx.cnt["vec"]
                    rec.wait("scalar", "vec", vv)
                    rec.wait_cur("scalar", "dmah")
                    rec.op("scalar", lambda eng: eng.activation(
                        g1c[:], d1[:], AF.Silu, bias=sb["zero"][:],
                        scale=sb["deq_d1"][:]), inc="act")
                    av = ctx.cnt["act"]
                    rec.wait("vector", "act", av)
                    rec.wait_cur("vector", "dmah")
                    rec.op("vector", lambda eng: eng.tensor_mul(g1c[:], g1c[:], s1c[:]))
                    rec.fence("vector")
                    rec.op("vector", lambda eng: eng.tensor_add(g1c[:], g1c[:], xl1c[:]),
                           inc="vec")
                    vv = ctx.cnt["vec"]
                    rec.wait("gpsimd", "vec", vv)
                    rec.dma("gpsimd", t["yout"][o2 * 128:(o2 + 1) * 128,
                                                tc2 * 512:(tc2 + 1) * 512], g1c[:],
                            sem="dmah")
                ctx.barrier(rec)
                rec.flush()
    rec.wait("gpsimd", "dmah", ctx.cnt["dmah"])
    ctx.barrier(rec)
    rec.flush()


def _dump_dram(rec, ctx, t, dbg_name, src_ap, rows, cols, dt):
    nc = ctx.nc
    with nc.sbuf_tensor(f"dumpbuf_{dbg_name}", [128, cols], dt) as buf:
        for rt in range(rows // 128):
            dv = rec.dma("sync", buf[:], src_ap[rt * 128:(rt + 1) * 128, :])
            rec.wait("gpsimd", "dma", dv)
            dv2 = rec.dma("gpsimd", t[dbg_name][rt * 128:(rt + 1) * 128, :], buf[:])
            rec.wait("sync", "dmag", dv2)
        rec.wait("gpsimd", "dmag", ctx.cnt["dmag"])
        rec.flush()


def _dbg_dump(rec, ctx, t, pairs):
    for name, ap in pairs:
        rec.dma("gpsimd", t["dbg_" + name], ap)
    dv = ctx.cnt["dmag"]
    rec.wait("gpsimd", "dmag", dv)
    rec.flush()


def build(debug=None, stop_after=None, p1_rounds=4):
    nc = bass.Bass("TRN2", target_bir_lowering=False)
    t = declare_io(nc)
    if debug:
        for name, shape, dt in debug:
            t["dbg_" + name] = nc.dram_tensor("dbg_" + name, shape, dt,
                                              kind="ExternalOutput").ap()
    ctx = Ctx(nc)
    sb = alloc_persistent(ctx, nc)
    rec = Rec(ctx)
    p0_consts(rec, ctx, t, sb)
    ctx.barrier(rec)
    rec.flush()
    if stop_after == "p0":
        _dbg_dump(rec, ctx, t, [("canary", sb["canary"][:]), ("magicM", sb["magicM"][:]),
                                ("can1", sb["can1"][:]), ("m64t", sb["m64t"][:]),
                                ("m32t", sb["m32t"][:]), ("c075", sb["c075"][:])])
        ctx.es.close()
        return nc
    if stop_after == "ln":
        with ExitStack() as ph:
            a = lambda n, shape, dt=F32: ph.enter_context(nc.sbuf_tensor("lnt_" + n, shape, dt))
            bufs = {
                "xch": a("xch", [128, FT * 128]),
                "xT": a("xT", [128, D]), "xnT": a("xnT", [128, D]),
                "stats": a("stats", [128, 24]), "mv": a("mv", [128, 2]),
                "rr": a("rr", [128, 1]), "b2": a("b2", [128, 1]),
                "tmp1": a("tmp1", [128, 1024]), "aT": a("aT", [128, D]),
                "amax": a("amax", [128, 1]), "sc": a("sc", [128, 1]),
                "rin": a("rin", [128, 1]),
                "pb0": ph.enter_context(nc.psum_tensor("lnt_pb0", [128, 512], F32)),
                "pb1": ph.enter_context(nc.psum_tensor("lnt_pb1", [128, 512], F32)),
            }
            xnh = a("xnh", [128, FT * 128], BF16)
            xnl = a("xnl", [128, FT * 128], BF16)
            qsl = a("qsl", [128, FT * 128], BF16)
            deq = a("deq", [128, 1])
            ln_quant_chunk(rec, ctx, sb, bufs, t["xatt"], 0, 128, "split",
                           out={"hi": xnh, "lo": xnl,
                                "quant": {"kind": "l0", "q_slab": qsl, "deq_cols": deq}})
            ctx.barrier(rec)
            hi32 = a("hi32", [128, FT * 128])
            lo32 = a("lo32", [128, FT * 128])
            q32 = a("q32", [128, FT * 128])
            rec.op("vector", lambda eng: eng.tensor_copy(hi32[:], xnh[:]))
            rec.op("vector", lambda eng: eng.tensor_copy(lo32[:], xnl[:]))
            rec.op("vector", lambda eng: eng.tensor_copy(q32[:], qsl[:]), inc="vec")
            rec.fence("vector")
            _dbg_dump(rec, ctx, t, [("xT", bufs["xT"][:]), ("xnT", bufs["xnT"][:]),
                                    ("mv", bufs["mv"][:]), ("hi32", hi32[:]),
                                    ("lo32", lo32[:]), ("q32", q32[:]), ("deq", deq[:]),
                                    ("stats", bufs["stats"][:])])
        ctx.es.close()
        return nc
    p1_l0_attention(rec, ctx, t, sb, rounds=p1_rounds)
    if stop_after == "p1":
        _dbg_dump(rec, ctx, t, [("csum", sb["csum"][:])])
        ctx.es.close()
        return nc
    p2a_quant_w0(rec, ctx, t, sb)
    p2b_quant_w1(rec, ctx, t, sb)
    p2c_values_xq(rec, ctx, t, sb)
    p2d_ffn0(rec, ctx, t, sb)
    p2e_outproj_xl1(rec, ctx, t, sb)
    if stop_after == "p2":
        _dump_dram(rec, ctx, t, "dbg_xl1", t["s_xl1"], D, TF, F32)
        if "dbg_g0" in t:
            _dump_dram(rec, ctx, t, "dbg_g0", t["s_g0"], D, TF, F32)
        if "dbg_vh" in t:
            _dump_dram(rec, ctx, t, "dbg_vh", t["s_vh"], D, TF, BF16)
        if "dbg_vl" in t:
            _dump_dram(rec, ctx, t, "dbg_vl", t["s_vl"], D, TF, BF16)
        if "dbg_xq" in t:
            _dump_dram(rec, ctx, t, "dbg_xq", t["s_xq0"], D, TF, BF16)
        ctx.es.close()
        return nc
    p3a_l1_proj(rec, ctx, t, sb)
    p3b_l1_attn(rec, ctx, t, sb)
    if stop_after == "p3":
        _dump_dram(rec, ctx, t, "dbg_s1", t["s_s1"], D, TOWN, F32)
        ctx.es.close()
        return nc
    p4a_ffn1_gateup(rec, ctx, t, sb)
    p4b_fwht_quant(rec, ctx, t, sb)
    p4c_down_out(rec, ctx, t, sb)
    ctx.es.close()
    return nc


def kernel(**inputs):
    in_maps = _stage_inputs(inputs)
    nc = build()
    res = run_bass_kernel_spmd(nc, in_maps, core_ids=list(range(8)))
    out = np.empty((4, 2048, 2048), dtype=np.float32)
    for c in range(8):
        b, r = c // 2, c % 2
        y = np.asarray(res.results[c]["yout"])
        out[b, r * 1024:(r + 1) * 1024, :] = y.T
    return out


# version 06
# speedup vs baseline: 1.1116x; 1.1116x over previous
import numpy as np
from contextlib import ExitStack

import concourse.bass as bass
import concourse.mybir as mybir
from concourse.bass_utils import run_bass_kernel_spmd

F32 = mybir.dt.float32
BF16 = mybir.dt.bfloat16
AF = mybir.ActivationFunctionType
ALU = mybir.AluOpType
AX = mybir.AxisListType

D = 2048
FT = D // 128
NH = 16
HD = 128
TATT = 2048
TF = 1536
TOWN = 1024
I0 = 5632
I1 = 8192
C_SM = 8.0
EPS_LN = 1e-5
M32C = 12582912.0
M64C = float(np.float32(1.5 * 2 ** 52))
SC_ATT = float(1.0 / np.sqrt(np.float32(128.0)))

ENGINES = ("sync", "tensor", "vector", "scalar", "gpsimd")



class Rec:
    def __init__(self, ctx):
        self.ctx = ctx
        self.steps = {en: [] for en in ENGINES}
        self._drain = {}

    def op(self, en, fn, inc=None, inc_by=None):
        ctx = self.ctx
        entry = {"fn": fn, "incs": []}
        self.steps[en].append(entry)
        if inc is not None:
            v = inc_by if inc_by is not None else (16 if inc in ("dma", "dmb", "dmc", "dmag", "dmah", "out") else 1)
            ctx.cnt[inc] += v
            entry["incs"].append((inc, v))
            return ctx.cnt[inc]
        return None

    def wait(self, en, sem, val):
        h = self.ctx.wait_hwm
        if val > h.get(sem, 0):
            h[sem] = val
        self.steps[en].append({"wait": (sem, val)})

    def wait_cur(self, en, sem):
        self.wait(en, sem, self.ctx.cnt[sem])

    def _last_op(self, en):
        for e in reversed(self.steps[en]):
            if "fn" in e and not e.get("shim"):
                return e
        return None

    def mark(self, en, sem):
        ctx = self.ctx
        e = self._last_op(en)
        assert e is not None, f"mark({en}) with no prior instruction"
        if e["incs"]:
            s0, v0 = e["incs"][0]
            assert s0 == sem, f"last {en} op already incs {s0}, wanted {sem}"
            return ctx.cnt[s0]
        v = 16 if sem in ("dma", "dmb", "dmc", "dmag", "dmah", "out") else 1
        ctx.cnt[sem] += v
        e["incs"].append((sem, v))
        return ctx.cnt[sem]

    def sem_inc_raw(self, en, sem, v=1):
        self.ctx.cnt[sem] += v
        val = self.ctx.cnt[sem]
        self.steps[en].append({"fn": lambda eng, s=self.ctx.sem[sem], v=v: _SemIncShim(eng, s, v),
                               "incs": [], "shim": True})
        return val

    def fence(self, en, sem=None):
        e = self._last_op(en)
        assert e is not None, f"fence({en}) with no prior instruction"
        if e["incs"]:
            s0, _ = e["incs"][0]
            v = self.ctx.cnt[s0]
            self.wait(en, s0, v)
            return v
        sem = sem or {"vector": "vec", "scalar": "act", "gpsimd": "gps",
                      "tensor": "tns", "sync": "ph"}[en]
        v = self.mark(en, sem)
        self.wait(en, sem, v)
        return v

    def dma(self, q, out, in_, sem=None):
        if sem is None:
            sem = "dma" if q == "sync" else "dmag"
        key = (q, sem)
        hwm = self.ctx.wait_hwm.get(sem, 0)
        if hwm > self._drain.get(key, 0):
            self.steps[q].append({"wait": (sem, hwm)})
            self._drain[key] = hwm
        v = self.op(q, lambda eng, o=out, i=in_: eng.dma_start(out=o, in_=i), inc=sem)
        if v - self._drain.get(key, 0) >= 12 * 16:
            self.wait(q, sem, v)
            self._drain[key] = v
        return v

    def flush(self):
        nc = self.ctx.nc
        ctx = self.ctx

        def runner(eng, entries):
            for e in entries:
                if "wait" in e:
                    sem, val = e["wait"]
                    eng.wait_ge(ctx.sem[sem], val)
                else:
                    inst = e["fn"](eng)
                    for sem, v in e["incs"]:
                        inst.then_inc(ctx.sem[sem], v)
        with nc.Block() as block:
            for en in ENGINES:
                entries = self.steps[en]
                if entries:
                    getattr(block, en)(lambda eng, entries=entries: runner(eng, entries))
        self.steps = {en: [] for en in ENGINES}


class Ctx:
    def __init__(self, nc):
        self.nc = nc
        self.es = ExitStack()
        self.sem = {}
        self.cnt = {}
        self.wait_hwm = {}
        for name in ["dma", "dmb", "dmc", "dmag", "dmah", "tns", "vec", "act", "gps", "ph", "out"]:
            self.sem[name] = self.es.enter_context(nc.semaphore(name))
            self.cnt[name] = 0

    def barrier(self, rec):
        for en in ENGINES:
            if rec._last_op(en) is not None:
                rec.fence(en)
        for en, sems in (("sync", ("dma", "dmb", "dmc", "out")),
                         ("gpsimd", ("dmag", "dmah"))):
            for s in sems:
                if self.cnt[s] > 0:
                    rec.wait(en, s, self.cnt[s])
        for en in ENGINES:
            rec.sem_inc_raw(en, "ph", 1)
        tgt = self.cnt["ph"]
        for en in ENGINES:
            rec.wait(en, "ph", tgt)


class _SemIncShim:
    def __init__(self, eng, sem, v):
        eng.sem_inc(sem, v)

    def then_inc(self, sem, v):
        raise RuntimeError("raw sem_inc cannot carry further incs")



def _stage_inputs(inputs):
    import ml_dtypes
    x = np.asarray(inputs["L0_attn_x"], dtype=np.float32)
    w_qkv = np.ascontiguousarray(np.asarray(inputs["L0_attn_w_qkv"], dtype=np.float32))
    w_o = np.ascontiguousarray(np.asarray(inputs["L0_attn_w_o"], dtype=np.float32))
    wgu0 = np.ascontiguousarray(np.asarray(inputs["L0_ffn_w_gate_up"], dtype=np.float32))
    wd0 = np.ascontiguousarray(np.asarray(inputs["L0_ffn_w_down"], dtype=np.float32))
    l1wq = np.ascontiguousarray(np.asarray(inputs["L1_attn_Wq"], dtype=np.float32).T)
    l1wk = np.ascontiguousarray(np.asarray(inputs["L1_attn_Wk"], dtype=np.float32).T)
    l1wv = np.ascontiguousarray(np.asarray(inputs["L1_attn_Wv"], dtype=np.float32).T)
    l1wo = np.ascontiguousarray(np.asarray(inputs["L1_attn_Wo"], dtype=np.float32).T)
    l1wg = np.ascontiguousarray(np.asarray(inputs["L1_ffn_w_gate"], dtype=np.float32).T)
    l1wu = np.ascontiguousarray(np.asarray(inputs["L1_ffn_w_up"], dtype=np.float32).T)
    l1wd = np.ascontiguousarray(np.asarray(inputs["L1_ffn_w_down"], dtype=np.float32).T)

    ident = np.eye(128, dtype=np.float32)
    onescol = np.ones((128, 1), dtype=np.float32)
    onesrow = np.ones((1, 128), dtype=np.float32)
    H = np.array([[1.0]], dtype=np.float32)
    while H.shape[0] < 128:
        H = np.block([[H, H], [H, -H]])
    H128 = np.ascontiguousarray(H.astype(np.float32))

    inv_freq = 1.0 / (10000.0 ** (np.arange(0, HD, 2, dtype=np.float32) / HD))

    def rope_tables(positions):
        freqs = positions[None, :].astype(np.float32) * inv_freq[:, None]
        c = np.cos(freqs).astype(np.float32)
        s_ = np.sin(freqs).astype(np.float32)
        ct = np.empty((HD, len(positions)), dtype=np.float32)
        st = np.empty((HD, len(positions)), dtype=np.float32)
        ct[0::2] = c
        ct[1::2] = c
        st[0::2] = -s_
        st[1::2] = s_
        return np.ascontiguousarray(ct), np.ascontiguousarray(st)

    jj = np.arange(1024)[:, None]
    qq = np.arange(512)[None, :] + 512
    base_valid = (jj > qq - 512) & (jj <= qq)
    mask_full = np.where(base_valid, 0.0, -1e30).astype(np.float32)
    mask_pad = mask_full.copy()
    mask_pad[0:512, :] = -1e30
    mask_full16 = np.ascontiguousarray(mask_full.astype(ml_dtypes.bfloat16))
    mask_pad16 = np.ascontiguousarray(mask_pad.astype(ml_dtypes.bfloat16))

    in_maps = []
    for c in range(8):
        b, r = c // 2, c % 2
        xs = np.ascontiguousarray(x[b].T)
        xatt = np.ascontiguousarray(np.roll(xs, 1024, axis=1)) if r == 0 else xs
        own0 = 0 if r == 0 else 1024
        qpos = np.arange(own0, own0 + TOWN)
        if r == 1:
            kpos = np.arange(512, 2048)
        else:
            kpos = np.concatenate([np.arange(1536, 2048), np.arange(0, 1024)])
        cq, sq = rope_tables(qpos)
        ck, sk = rope_tables(kpos)
        mA = mask_pad16 if r == 0 else mask_full16
        in_maps.append({
            "xatt": xatt,
            "wqkv": w_qkv, "wo0": w_o, "wgu0": wgu0, "wd0": wd0,
            "l1wq": l1wq, "l1wk": l1wk, "l1wv": l1wv, "l1wo": l1wo,
            "l1wg": l1wg, "l1wu": l1wu, "l1wd": l1wd,
            "ident": ident, "onescol": onescol, "onesrow": onesrow, "h128": H128,
            "ropeq_c": cq, "ropeq_s": sq, "ropek_c": ck, "ropek_s": sk,
            "maskA": mA, "maskB": mask_full16,
        })
    return in_maps


def declare_io(nc):
    t = {}
    t["xatt"] = nc.dram_tensor("xatt", [D, TATT], F32, kind="ExternalInput").ap()
    t["wqkv"] = nc.dram_tensor("wqkv", [D, 3 * D], F32, kind="ExternalInput").ap()
    t["wo0"] = nc.dram_tensor("wo0", [D, D], F32, kind="ExternalInput").ap()
    t["wgu0"] = nc.dram_tensor("wgu0", [2 * I0, D], F32, kind="ExternalInput").ap()
    t["wd0"] = nc.dram_tensor("wd0", [D, I0], F32, kind="ExternalInput").ap()
    for n in ["l1wq", "l1wk", "l1wv", "l1wo"]:
        t[n] = nc.dram_tensor(n, [D, D], F32, kind="ExternalInput").ap()
    t["l1wg"] = nc.dram_tensor("l1wg", [D, I1], F32, kind="ExternalInput").ap()
    t["l1wu"] = nc.dram_tensor("l1wu", [D, I1], F32, kind="ExternalInput").ap()
    t["l1wd"] = nc.dram_tensor("l1wd", [I1, D], F32, kind="ExternalInput").ap()
    t["ident"] = nc.dram_tensor("ident", [128, 128], F32, kind="ExternalInput").ap()
    t["onescol"] = nc.dram_tensor("onescol", [128, 1], F32, kind="ExternalInput").ap()
    t["onesrow"] = nc.dram_tensor("onesrow", [1, 128], F32, kind="ExternalInput").ap()
    t["h128"] = nc.dram_tensor("h128", [128, 128], F32, kind="ExternalInput").ap()
    t["ropeq_c"] = nc.dram_tensor("ropeq_c", [HD, TOWN], F32, kind="ExternalInput").ap()
    t["ropeq_s"] = nc.dram_tensor("ropeq_s", [HD, TOWN], F32, kind="ExternalInput").ap()
    t["ropek_c"] = nc.dram_tensor("ropek_c", [HD, TF], F32, kind="ExternalInput").ap()
    t["ropek_s"] = nc.dram_tensor("ropek_s", [HD, TF], F32, kind="ExternalInput").ap()
    t["maskA"] = nc.dram_tensor("maskA", [1024, 512], BF16, kind="ExternalInput").ap()
    t["maskB"] = nc.dram_tensor("maskB", [1024, 512], BF16, kind="ExternalInput").ap()
    t["yout"] = nc.dram_tensor("yout", [D, TOWN], F32, kind="ExternalOutput").ap()
    t["s_wq0"] = nc.dram_tensor("s_wq0", [D, 2 * I0], BF16).ap()
    t["s_wd0"] = nc.dram_tensor("s_wd0", [I0, D], BF16).ap()
    t["s_wg1"] = nc.dram_tensor("s_wg1", [D, I1], BF16).ap()
    t["s_wu1"] = nc.dram_tensor("s_wu1", [D, I1], BF16).ap()
    t["s_wd1"] = nc.dram_tensor("s_wd1", [I1, D], BF16).ap()
    t["s_xq0"] = nc.dram_tensor("s_xq0", [D, TF], BF16).ap()
    t["s_vh"] = nc.dram_tensor("s_vh", [D, TF], BF16).ap()
    t["s_vl"] = nc.dram_tensor("s_vl", [D, TF], BF16).ap()
    t["s_g0"] = nc.dram_tensor("s_g0", [D, TF], F32).ap()
    t["s_xl1"] = nc.dram_tensor("s_xl1", [D, TF], F32).ap()
    t["s_krh"] = nc.dram_tensor("s_krh", [D, TF], BF16).ap()
    t["s_krl"] = nc.dram_tensor("s_krl", [D, TF], BF16).ap()
    t["s_vT"] = nc.dram_tensor("s_vT", [TF, D], F32).ap()
    t["s_s1"] = nc.dram_tensor("s_s1", [D, TOWN], F32).ap()
    t["s_xq1"] = nc.dram_tensor("s_xq1", [D, TOWN], BF16).ap()
    t["s_qrh"] = nc.dram_tensor("s_qrh", [D, TOWN], BF16).ap()
    t["s_qrl"] = nc.dram_tensor("s_qrl", [D, TOWN], BF16).ap()
    t["s_h1"] = nc.dram_tensor("s_h1", [I1, TOWN], F32).ap()
    t["s_yq1"] = nc.dram_tensor("s_yq1", [I1, TOWN], BF16).ap()
    return t




def alloc_persistent(ctx, nc):
    sb = {}
    a = lambda n, shape, dt=F32: ctx.es.enter_context(nc.sbuf_tensor(n, shape, dt))
    sb["ident"] = a("c_ident", [128, 128])
    sb["onescol"] = a("c_onescol", [128, 1])
    sb["onesrow"] = a("c_onesrow", [1, 128])
    sb["h128f"] = a("c_h128f", [128, 128])
    sb["h128b"] = a("c_h128b", [128, 128], BF16)
    sb["identb"] = a("c_identb", [128, 128], BF16)
    for n, v in [("c075", 0.75), ("negC", -C_SM), ("epsln", EPS_LN), ("zero", 0.0),
                 ("m64t", M64C), ("m32t", M32C)]:
        sb[n] = a("c_" + n, [128, 1])
        sb["_val_" + n] = v
    sb["canary"] = a("c_canary", [128, 1])
    sb["can1"] = a("c_can1", [128, 1])
    sb["magicM"] = a("c_magicM", [128, 1])
    sb["negM"] = a("c_negM", [128, 1])
    sb["csum"] = a("g_csum", [128, NH * 12])
    sb["sgu0"] = a("g_sgu0", [128, 88])
    sb["sd0"] = a("g_sd0", [128, 16])
    for n in ["deq_g1", "deq_u1", "deq_d1"]:
        sb[n] = a("g_" + n, [128, 1])
    for n in ["thp_g1", "thn_g1", "thp_u1", "thn_u1", "thp_d1", "thn_d1"]:
        sb[n] = a("g_" + n, [128, 1])
    sb["dxq0"] = a("g_dxq0", [128, 12])
    sb["dxq1"] = a("g_dxq1", [128, 8])
    sb["dyq1"] = a("g_dyq1", [128, 8])
    sb["csrow"] = a("g_csrow", [1, 128])
    return sb


def p0_consts(rec, ctx, t, sb):
    for name in ["ident", "onescol", "onesrow", "h128f"]:
        src = {"h128f": "h128"}.get(name, name)
        rec.dma("sync", sb[name][:], t[src])
    dmadone = ctx.cnt["dma"]
    for n in ["c075", "negC", "epsln", "zero", "m64t", "m32t"]:
        rec.op("gpsimd", lambda eng, ap=sb[n][:], v=sb["_val_" + n]: eng.memset(ap, v),
               inc="gps")
    gv = ctx.cnt["gps"]
    rec.wait("vector", "gps", gv)
    rec.wait("vector", "dma", dmadone)
    rec.op("vector", lambda eng: eng.tensor_add(sb["can1"][:], sb["c075"][:], sb["m64t"][:]))
    rec.fence("vector")
    rec.op("vector", lambda eng: eng.tensor_sub(sb["canary"][:], sb["can1"][:], sb["m64t"][:]))
    rec.fence("vector")
    rec.op("vector", lambda eng: eng.tensor_mul(sb["magicM"][:], sb["canary"][:], sb["m64t"][:]))
    rec.fence("vector")
    rec.op("vector", lambda eng: eng.tensor_add(sb["magicM"][:], sb["magicM"][:], sb["m32t"][:]))
    rec.fence("vector")
    rec.op("vector", lambda eng: eng.tensor_sub(sb["negM"][:], sb["zero"][:], sb["magicM"][:]))
    rec.op("vector", lambda eng: eng.tensor_copy(sb["h128b"][:], sb["h128f"][:]))
    rec.op("vector", lambda eng: eng.tensor_copy(sb["identb"][:], sb["ident"][:]), inc="vec")


def mm3(rec, out_ap, lT_hi, lT_lo, r_hi, r_lo, start, stop):
    rec.op("tensor", lambda eng: eng.matmul(out_ap, lT_hi, r_hi, start=start, stop=False))
    rec.op("tensor", lambda eng: eng.matmul(out_ap, lT_hi, r_lo, start=False, stop=False))
    rec.op("tensor", lambda eng: eng.matmul(out_ap, lT_lo, r_hi, start=False, stop=stop))


def ln_quant_chunk(rec, ctx, sb, bufs, src_hbm, col0, TC, mode, out=None):
    nsub = TC // 128
    for s in range(nsub):
        rec.wait_cur("sync", "tns")
        for ft in range(FT):
            rec.dma("sync", bufs["xch"][:, ft * 128:(ft + 1) * 128],
                    src_hbm[ft * 128:(ft + 1) * 128, col0 + s * 128:col0 + s * 128 + 128],
                    sem="dmc")
        dma_done = ctx.cnt["dmc"]
        rec.wait("tensor", "dmc", dma_done)
        for ft in range(FT):
            pb = bufs["pb0"] if ft % 2 == 0 else bufs["pb1"]
            rec.wait_cur("tensor", "vec")
            rec.op("tensor", lambda eng, pb=pb, ft=ft: eng.transpose(
                pb[:, 0:128], bufs["xch"][:, ft * 128:(ft + 1) * 128],
                sb["ident"][:]), inc="tns")
            rec.wait_cur("vector", "tns")
            rec.op("vector", lambda eng, pb=pb, ft=ft: eng.tensor_copy(
                bufs["xT"][:, ft * 128:(ft + 1) * 128], pb[:, 0:128]), inc="vec")
        rec.fence("vector")
        for i in range(4):
            rec.op("vector", lambda eng, i=i: eng.bn_stats(
                bufs["stats"][:, 6 * i:6 * i + 6], bufs["xT"][:, 512 * i:512 * (i + 1)]))
        rec.fence("vector")
        rec.op("vector", lambda eng: eng.bn_aggr(bufs["mv"][:], bufs["stats"][:]), inc="vec")
        rec.fence("vector")
        rec.wait_cur("scalar", "vec")
        rec.op("scalar", lambda eng: eng.activation(
            bufs["rr"][:], bufs["mv"][:, 1:2], AF.Sqrt, bias=sb["epsln"][:], scale=1.0), inc="act")
        rec.wait_cur("vector", "act")
        rec.op("vector", lambda eng: eng.reciprocal(bufs["rr"][:], bufs["rr"][:]))
        rec.fence("vector")
        rec.op("vector", lambda eng: eng.tensor_mul(bufs["b2"][:], bufs["mv"][:, 0:1], bufs["rr"][:]))
        rec.fence("vector")
        rec.op("vector", lambda eng: eng.tensor_single_scalar(
            bufs["b2"][:], bufs["b2"][:], -1.0, ALU.mult), inc="vec")
        rec.wait_cur("scalar", "vec")
        rec.op("scalar", lambda eng: eng.activation(
            bufs["xnT"][:], bufs["xT"][:], AF.Identity, bias=bufs["b2"][:],
            scale=bufs["rr"][:]), inc="act")
        act_ln = ctx.cnt["act"]
        if out is not None and "hi" in out:
            rec.wait("tensor", "act", act_ln)
            for ft in range(FT):
                pb = bufs["pb0"] if ft % 2 == 0 else bufs["pb1"]
                rec.wait_cur("tensor", "vec")
                rec.op("tensor", lambda eng, pb=pb, ft=ft: eng.transpose(
                    pb[:, 0:128], bufs["xnT"][:, ft * 128:(ft + 1) * 128],
                    sb["ident"][:]), inc="tns")
                rec.wait_cur("vector", "tns")
                co = ft * TC + s * 128
                hi = out["hi"][:, co:co + 128]
                lo = out["lo"][:, co:co + 128]
                tmp = bufs["tmp1"][:, 0:128]
                rec.op("vector", lambda eng, pb=pb, hi=hi: eng.tensor_copy(hi, pb[:, 0:128]))
                rec.fence("vector")
                rec.op("vector", lambda eng, pb=pb, hi=hi, tmp=tmp: eng.tensor_sub(tmp, pb[:, 0:128], hi))
                rec.fence("vector")
                rec.op("vector", lambda eng, lo=lo, tmp=tmp: eng.tensor_copy(lo, tmp), inc="vec")
        if out is not None and "quant" in out:
            q = out["quant"]
            rec.wait("vector", "act", act_ln)
            rec.op("vector", lambda eng: eng.reduce_max(
                bufs["amax"][:], bufs["xnT"][:], AX.X, apply_absolute_value=True))
            rec.fence("vector")
            if q["kind"] == "l0":
                rec.op("vector", lambda eng: eng.tensor_scalar(
                    bufs["sc"][:], bufs["amax"][:], 1.0 / 127.0, 1e-8, ALU.mult, ALU.max))
                rec.fence("vector")
                rec.op("vector", lambda eng: eng.reciprocal(bufs["rin"][:], bufs["sc"][:]), inc="vec")
                deq_src = bufs["sc"]
            else:
                rec.op("vector", lambda eng: eng.tensor_single_scalar(
                    bufs["sc"][:], bufs["amax"][:], 1e-5, ALU.max))
                rec.fence("vector")
                rec.op("vector", lambda eng: eng.reciprocal(bufs["rin"][:], bufs["sc"][:]))
                rec.fence("vector")
                rec.op("vector", lambda eng: eng.tensor_single_scalar(
                    bufs["rin"][:], bufs["rin"][:], 128.0, ALU.mult))
                rec.op("vector", lambda eng: eng.tensor_single_scalar(
                    bufs["sc"][:], bufs["sc"][:], 1.0 / 128.0, ALU.mult), inc="vec")
                deq_src = bufs["sc"]
            rec.fence("vector")
            rec.op("vector", lambda eng, s=s: eng.tensor_copy(
                q["deq_cols"][:, s:s + 1], deq_src[:]))
            rec.wait_cur("scalar", "vec")
            rec.op("scalar", lambda eng: eng.activation(
                bufs["aT"][:], bufs["xnT"][:], AF.Identity, bias=sb["magicM"][:],
                scale=bufs["rin"][:]), inc="act")
            rec.wait_cur("vector", "act")
            rec.op("vector", lambda eng: eng.tensor_single_scalar(
                bufs["aT"][:], bufs["aT"][:], sb["magicM"][:], ALU.subtract))
            if q["kind"] == "l1":
                rec.fence("vector")
                rec.op("vector", lambda eng: eng.tensor_scalar(
                    bufs["aT"][:], bufs["aT"][:], 127.0, -128.0, ALU.min, ALU.max))
            qv = rec.mark("vector", "vec")
            rec.wait("tensor", "vec", qv)
            for ft in range(FT):
                pb = bufs["pb0"] if ft % 2 == 0 else bufs["pb1"]
                rec.wait_cur("tensor", "vec")
                rec.op("tensor", lambda eng, pb=pb, ft=ft: eng.transpose(
                    pb[:, 0:128], bufs["aT"][:, ft * 128:(ft + 1) * 128],
                    sb["ident"][:]), inc="tns")
                rec.wait_cur("vector", "tns")
                if "dma_to" in q:
                    dst, colb = q["dma_to"]
                    stg = q["stage"][ft % 2]
                    rec.wait_cur("vector", "dmag")
                    rec.op("vector", lambda eng, pb=pb, stg=stg: eng.tensor_copy(
                        stg[:], pb[:, 0:128]), inc="vec")
                    vv = ctx.cnt["vec"]
                    rec.wait("gpsimd", "vec", vv)
                    rec.dma("gpsimd", dst[ft * 128:(ft + 1) * 128,
                                          colb + s * 128:colb + s * 128 + 128], stg[:])
                else:
                    co = ft * TC + s * 128
                    rec.op("vector", lambda eng, pb=pb, co=co: eng.tensor_copy(
                        q["q_slab"][:, co:co + 128], pb[:, 0:128]), inc="vec")


def p1_l0_attention(rec, ctx, t, sb, rounds=4):
    nc = ctx.nc
    NQH = 4
    for rnd in range(rounds):
        with ExitStack() as ph:
            a = lambda n, shape, dt=F32: ph.enter_context(
                nc.sbuf_tensor(f"p1_{n}_{rnd}", shape, dt))
            kh = a("kh", [128, NQH * TATT], BF16)
            kl = a("kl", [128, NQH * TATT], BF16)
            qh = a("qh", [128, NQH * TATT], BF16)
            ql = a("ql", [128, NQH * TATT], BF16)
            bufs = {
                "xch": a("xch", [128, FT * 128]),
                "xT": a("xT", [128, D]), "xnT": a("xnT", [128, D]),
                "stats": a("stats", [128, 24]), "mv": a("mv", [128, 2]),
                "rr": a("rr", [128, 1]), "b2": a("b2", [128, 1]),
                "tmp1": a("tmp1", [128, 1024]),
                "amax": a("amax", [128, 1]), "sc": a("sc", [128, 1]),
                "rin": a("rin", [128, 1]),
            }
            with ExitStack() as ps:
                pj = [ps.enter_context(nc.psum_tensor(f"p1_pj{i}_{rnd}", [128, 512], F32))
                      for i in range(8)]
                bufs["pb0"], bufs["pb1"] = pj[0], pj[1]
                wh = ps.enter_context(nc.sbuf_tensor(f"p1_wh_{rnd}", [128, FT * 1024], BF16))
                wl = ps.enter_context(nc.sbuf_tensor(f"p1_wl_{rnd}", [128, FT * 1024], BF16))
                wtmp = ps.enter_context(nc.sbuf_tensor(f"p1_wt_{rnd}", [128, 1024], F32))
                xnh = ps.enter_context(nc.sbuf_tensor(f"p1_xnh_{rnd}", [128, FT * 512], BF16))
                xnl = ps.enter_context(nc.sbuf_tensor(f"p1_xnl_{rnd}", [128, FT * 512], BF16))
                for ft in range(FT):
                    wsem = ("dma", "dmb")[ft % 2]
                    rec.wait_cur("sync", "vec")
                    rec.dma("sync", wtmp[:, 0:512],
                            t["wqkv"][ft * 128:(ft + 1) * 128, rnd * 512:rnd * 512 + 512],
                            sem=wsem)
                    rec.dma("sync", wtmp[:, 512:1024],
                            t["wqkv"][ft * 128:(ft + 1) * 128, 2048 + rnd * 512:2048 + rnd * 512 + 512],
                            sem=wsem)
                    rec.wait_cur("vector", wsem)
                    co = ft * 1024
                    rec.op("vector", lambda eng, co=co: eng.tensor_copy(wh[:, co:co + 1024], wtmp[:]))
                    rec.fence("vector")
                    rec.op("vector", lambda eng, co=co: eng.tensor_copy(
                        bufs["tmp1"][:, 0:1024], wh[:, co:co + 1024]))
                    rec.fence("vector")
                    rec.op("vector", lambda eng: eng.tensor_sub(wtmp[:], wtmp[:], bufs["tmp1"][:, 0:1024]))
                    rec.fence("vector")
                    rec.op("vector", lambda eng, co=co: eng.tensor_copy(wl[:, co:co + 1024], wtmp[:]),
                           inc="vec")
                wsplit_done = ctx.cnt["vec"]
                pj_free = [0] * 8
                for tc in range(4):
                    rec.wait("tensor", "vec", max(pj_free[0], pj_free[1]))
                    ln_quant_chunk(rec, ctx, sb, bufs, t["xatt"], tc * 512, 512,
                                   "split", out={"hi": xnh, "lo": xnl})
                    rec.wait("tensor", "vec", wsplit_done)
                    rec.wait("tensor", "vec", ctx.cnt["vec"])
                    for grp in range(2):
                        for hh in range(NQH):
                            bi = grp * NQH + hh
                            bank = pj[bi]
                            rec.wait("tensor", "vec", pj_free[bi])
                            for ft in range(FT):
                                co = ft * 1024 + grp * 512 + hh * 128
                                mm3(rec, bank[:],
                                    wh[:, co:co + 128], wl[:, co:co + 128],
                                    xnh[:, ft * 512:(ft + 1) * 512],
                                    xnl[:, ft * 512:(ft + 1) * 512],
                                    start=(ft == 0), stop=(ft == FT - 1))
                            tv = rec.mark("tensor", "tns")
                            rec.wait("vector", "tns", tv)
                            dst_h = (qh, kh)[grp]
                            dst_l = (ql, kl)[grp]
                            co = hh * TATT + tc * 512
                            tmp = bufs["tmp1"][:, 512 * (bi % 2):512 * (bi % 2) + 512]
                            rec.op("vector", lambda eng, b=bank, d=dst_h, co=co: eng.tensor_copy(
                                d[:, co:co + 512], b[:]))
                            rec.fence("vector")
                            rec.op("vector", lambda eng, b=bank, d=dst_h, co=co, tmp=tmp: eng.tensor_sub(
                                tmp, b[:], d[:, co:co + 512]))
                            rec.fence("vector")
                            pj_free[bi] = rec.op("vector", lambda eng, d=dst_l, co=co, tmp=tmp: eng.tensor_copy(
                                d[:, co:co + 512], tmp), inc="vec")
            ctx.barrier(rec)
            with ExitStack() as ps:
                ps_sc = [ps.enter_context(nc.psum_tensor(f"p1_sc{i}_{rnd}", [128, 1024], F32))
                         for i in range(2)]
                ps_cs = ps.enter_context(nc.psum_tensor(f"p1_cs_{rnd}", [128, 512], F32))
                eb = [ps.enter_context(nc.sbuf_tensor(f"p1_eb{i}_{rnd}", [128, 2048], F32))
                      for i in range(2)]
                zpart = ps.enter_context(nc.sbuf_tensor(f"p1_zp_{rnd}", [128, 4], F32))
                zi2 = ps.enter_context(nc.sbuf_tensor(f"p1_zi_{rnd}", [128, 2], F32))
                eb_free = [0, 0]
                z_guard = [0, 0]
                z_done = [0, 0]
                sc_free = 0
                cv = 0
                pend = None
                def emit_colsum(pend):
                    nonlocal cv
                    pe, phh, pqt, pj_ = pend
                    if pqt == 0:
                        rec.wait("tensor", "vec", cv)
                    rec.wait("tensor", "vec", z_done[pj_])
                    for kt in range(4, 16):
                        rec.op("tensor", lambda eng, e=pe, kt=kt, pqt=pqt, pj_=pj_: eng.matmul(
                            ps_cs[:, kt - 4:kt - 3], e[:, kt * 128:(kt + 1) * 128],
                            zi2[:, pj_:pj_ + 1],
                            start=(pqt == 0 and kt == 4), stop=(pqt == 15 and kt == 15)))
                    m = rec.mark("tensor", "tns")
                    eb_free[pj_] = m
                    z_guard[pj_] = m
                    if pqt == 15:
                        rec.wait("vector", "tns", m)
                        hg = rnd * NQH + phh
                        cv = rec.op("vector", lambda eng, hg=hg: eng.tensor_copy(
                            sb["csum"][:, hg * 12:hg * 12 + 12], ps_cs[:, 0:12]), inc="vec")
                for i in range(NQH * 16):
                    hh, qt = divmod(i, 16)
                    j = i % 2
                    e = eb[j]
                    rec.wait("tensor", "act", sc_free)
                    for hlf in range(2):
                        for kc in range(2):
                            kco = hh * TATT + hlf * 1024 + kc * 512
                            qco = hh * TATT + qt * 128
                            mm3(rec, ps_sc[hlf][:, kc * 512:(kc + 1) * 512],
                                qh[:, qco:qco + 128], ql[:, qco:qco + 128],
                                kh[:, kco:kco + 512], kl[:, kco:kco + 512],
                                start=True, stop=True)
                    tmm = rec.mark("tensor", "tns")
                    if pend is not None:
                        emit_colsum(pend)
                    rec.wait("scalar", "tns", tmm)
                    rec.wait("scalar", "tns", eb_free[j])
                    rec.wait("scalar", "vec", z_done[j])
                    a0 = rec.op("scalar", lambda eng, e=e, j=j: eng.activation(
                        e[:, 0:1024], ps_sc[0][:], AF.Exp, bias=sb["negC"][:],
                        scale=SC_ATT, accum_out=zpart[:, 2 * j:2 * j + 1]), inc="act")
                    a1 = rec.op("scalar", lambda eng, e=e, j=j: eng.activation(
                        e[:, 1024:2048], ps_sc[1][:], AF.Exp, bias=sb["negC"][:],
                        scale=SC_ATT, accum_out=zpart[:, 2 * j + 1:2 * j + 2]), inc="act")
                    sc_free = a1
                    rec.wait("vector", "act", a1)
                    rec.wait("vector", "tns", z_guard[j])
                    rec.op("vector", lambda eng, j=j: eng.tensor_add(
                        zi2[:, j:j + 1], zpart[:, 2 * j:2 * j + 1], zpart[:, 2 * j + 1:2 * j + 2]))
                    rec.fence("vector")
                    z_done[j] = rec.op("vector", lambda eng, j=j: eng.reciprocal(
                        zi2[:, j:j + 1], zi2[:, j:j + 1]), inc="vec")
                    pend = (e, hh, qt, j)
                emit_colsum(pend)
                rec.wait("tensor", "vec", cv)
            ctx.barrier(rec)
            rec.flush()
    ctx.barrier(rec)
    rec.flush()


def p2a_quant_w0(rec, ctx, t, sb):
    nc = ctx.nc
    for (src_t, dst_t, n_in, n_rows, sc_dst) in [
            ("wgu0", "s_wq0", D, 2 * I0, "sgu0"),
            ("wd0", "s_wd0", I0, D, "sd0")]:
        with ExitStack() as ph:
            a = lambda n, shape, dt=F32: ph.enter_context(
                nc.sbuf_tensor(f"p2a_{n}_{src_t}", shape, dt))
            nat = [a(f"nat{i}", [128, n_in]) for i in range(2)]
            aa = [a(f"aa{i}", [128, n_in]) for i in range(2)]
            ib = [a(f"ib{i}", [128, n_in], BF16) for i in range(2)]
            amax = a("amax", [128, 2])
            sc = a("sc", [128, 2])
            rin = a("rin", [128, 2])
            ob = [a(f"ob{i}", [128, 128], BF16) for i in range(2)]
            pt = [ph.enter_context(nc.psum_tensor(f"p2a_pt{i}_{src_t}", [128, 128], BF16))
                  for i in range(4)]
            nt = n_in // 128
            nat_free = [0, 0]
            ib_free = [0, 0]
            pt_free = [0, 0, 0, 0]
            ob_free = [0, 0]
            for g in range(n_rows // 128):
                s = g % 2
                nsem = "dma" if s == 0 else "dmb"
                rec.wait("sync", "act", nat_free[s])
                dv = rec.dma("sync", nat[s][:], t[src_t][g * 128:(g + 1) * 128, :], sem=nsem)
                rec.wait("vector", nsem, dv)
                rec.op("vector", lambda eng, s=s: eng.reduce_max(
                    amax[:, s:s + 1], nat[s][:], AX.X, apply_absolute_value=True))
                rec.fence("vector")
                rec.op("vector", lambda eng, s=s: eng.tensor_scalar(
                    sc[:, s:s + 1], amax[:, s:s + 1], 1.0 / 127.0, 1e-8, ALU.mult, ALU.max))
                rec.fence("vector")
                rec.op("vector", lambda eng, s=s: eng.reciprocal(rin[:, s:s + 1], sc[:, s:s + 1]))
                rec.fence("vector")
                v1 = rec.op("vector", lambda eng, g=g, s=s: eng.tensor_copy(
                    sb[sc_dst][:, g:g + 1], sc[:, s:s + 1]), inc="vec")
                rec.wait("scalar", "vec", v1)
                rec.wait("scalar", "tns", ib_free[s])
                rec.op("scalar", lambda eng, s=s: eng.activation(
                    aa[s][:], nat[s][:], AF.Identity, bias=sb["magicM"][:],
                    scale=rin[:, s:s + 1]))
                rec.fence("scalar")
                av = rec.op("scalar", lambda eng, s=s: eng.activation(
                    ib[s][:], aa[s][:], AF.Identity, bias=sb["negM"][:], scale=1.0),
                    inc="act")
                nat_free[s] = av
                rec.wait("tensor", "act", av)
                for ft in range(nt):
                    i4 = ft % 4
                    i2 = ft % 2
                    osem = "dmag" if i2 == 0 else "dmah"
                    rec.wait("tensor", "vec", pt_free[i4])
                    tv = rec.op("tensor", lambda eng, i4=i4, s=s, ft=ft: eng.transpose(
                        pt[i4][:], ib[s][:, ft * 128:(ft + 1) * 128], sb["identb"][:]),
                        inc="tns")
                    rec.wait("vector", "tns", tv)
                    rec.wait("vector", osem, ob_free[i2])
                    ev = rec.op("vector", lambda eng, i4=i4, i2=i2: eng.tensor_copy(
                        ob[i2][:], pt[i4][:]), inc="vec")
                    pt_free[i4] = ev
                    rec.wait("gpsimd", "vec", ev)
                    ob_free[i2] = rec.dma(
                        "gpsimd", t[dst_t][ft * 128:(ft + 1) * 128, g * 128:(g + 1) * 128],
                        ob[i2][:], sem=osem)
                ib_free[s] = ctx.cnt["tns"]
            ctx.barrier(rec)
            rec.flush()
    ctx.barrier(rec)
    rec.flush()


def p2b_quant_w1(rec, ctx, t, sb):
    nc = ctx.nc
    for (src_t, dst_t, n_k, n_o, deq, thp, thn) in [
            ("l1wg", "s_wg1", D, I1, "deq_g1", "thp_g1", "thn_g1"),
            ("l1wu", "s_wu1", D, I1, "deq_u1", "thp_u1", "thn_u1"),
            ("l1wd", "s_wd1", I1, D, "deq_d1", "thp_d1", "thn_d1")]:
        with ExitStack() as ph:
            a = lambda n, shape, dt=F32: ph.enter_context(
                nc.sbuf_tensor(f"p2b_{n}_{src_t}", shape, dt))
            wt = [a(f"wt{i}", [128, n_o]) for i in range(2)]
            s1 = a("s1", [128, n_o])
            s2 = a("s2", [128, n_o])
            qb = [a(f"qb{i}", [128, n_o], BF16) for i in range(2)]
            acc = a("acc", [128, 16 if n_k == D else 64])
            tot = a("tot", [128, 1])
            pm = ph.enter_context(nc.psum_tensor(f"p2b_pm_{src_t}", [128, 512], F32))
            nkt = n_k // 128
            wt_free = [0, 0]
            for it in range(nkt):
                s = it % 2
                nsem = "dma" if s == 0 else "dmb"
                rec.wait("sync", "vec", wt_free[s])
                dv = rec.dma("sync", wt[s][:], t[src_t][it * 128:(it + 1) * 128, :], sem=nsem)
                rec.wait("vector", nsem, dv)
                wt_free[s] = rec.op("vector", lambda eng, it=it, s=s: eng.reduce_sum(
                    acc[:, it:it + 1], wt[s][:], AX.X, apply_absolute_value=True), inc="vec")
            rec.fence("vector")
            rec.op("vector", lambda eng: eng.reduce_sum(tot[:], acc[:], AX.X), inc="vec")
            tv = ctx.cnt["vec"]
            rec.wait("tensor", "vec", tv)
            rec.op("tensor", lambda eng: eng.matmul(
                pm[0:1, 0:1], tot[:], sb["onescol"][:], start=True, stop=True), inc="tns")
            rec.wait_cur("vector", "tns")
            rec.op("vector", lambda eng, n_k=n_k, n_o=n_o: eng.tensor_single_scalar(
                tot[0:1, 0:1], pm[0:1, 0:1], 1.0 / float(n_k * n_o), ALU.mult))
            rec.fence("vector")
            rec.op("vector", lambda eng: eng.tensor_single_scalar(
                tot[0:1, 0:1], tot[0:1, 0:1], 1e-5, ALU.max), inc="vec")
            tv = ctx.cnt["vec"]
            rec.wait("tensor", "vec", tv)
            rec.op("tensor", lambda eng: eng.matmul(
                pm[:, 1:2], sb["onesrow"][:], tot[0:1, 0:1],
                start=True, stop=True), inc="tns")
            rec.wait_cur("vector", "tns")
            rec.op("vector", lambda eng, deq=deq: eng.tensor_single_scalar(
                sb[deq][:], pm[:, 1:2], 0.5, ALU.mult))
            rec.fence("vector")
            rec.op("vector", lambda eng, deq=deq, thn=thn: eng.tensor_sub(
                sb[thn][:], sb["zero"][:], sb[deq][:]))
            rec.op("vector", lambda eng, deq=deq, thp=thp: eng.tensor_copy(
                sb[thp][:], sb[deq][:]), inc="vec")
            tv = ctx.cnt["vec"]
            rec.wait("scalar", "vec", tv)
            rec.wait_cur("sync", "vec")
            wt_free = [0, 0]
            qb_free = [0, 0]
            for it in range(nkt):
                s = it % 2
                nsem = "dma" if s == 0 else "dmb"
                osem = "dmag" if s == 0 else "dmah"
                rec.wait("sync", "act", wt_free[s])
                dv = rec.dma("sync", wt[s][:], t[src_t][it * 128:(it + 1) * 128, :], sem=nsem)
                rec.wait("scalar", nsem, dv)
                rec.wait_cur("scalar", "vec")
                rec.op("scalar", lambda eng, s=s, thn=thn: eng.activation(
                    s1[:], wt[s][:], AF.Sign, bias=sb[thn][:], scale=1.0))
                av = rec.op("scalar", lambda eng, s=s, thp=thp: eng.activation(
                    s2[:], wt[s][:], AF.Sign, bias=sb[thp][:], scale=1.0), inc="act")
                wt_free[s] = av
                rec.wait("vector", "act", av)
                rec.wait("vector", osem, qb_free[s])
                vv = rec.op("vector", lambda eng, s=s: eng.tensor_add(qb[s][:], s1[:], s2[:]),
                            inc="vec")
                rec.wait("gpsimd", "vec", vv)
                qb_free[s] = rec.dma("gpsimd", t[dst_t][it * 128:(it + 1) * 128, :], qb[s][:],
                                     sem=osem)
            ctx.barrier(rec)
            rec.flush()
    ctx.barrier(rec)
    rec.flush()


def _csum_bcast(rec, ctx, sb, col, ptr, pbc):
    rec.op("tensor", lambda eng: eng.transpose(
        ptr[:], sb["csum"][:, col:col + 1], sb["ident"][:]), inc="tns")
    rec.wait_cur("vector", "tns")
    rec.op("vector", lambda eng: eng.tensor_copy(sb["csrow"][:], ptr[0:1, 0:128]), inc="vec")
    vv = ctx.cnt["vec"]
    rec.wait("tensor", "vec", vv)
    rec.op("tensor", lambda eng: eng.matmul(
        pbc[:], sb["onesrow"][:], sb["csrow"][:], start=True, stop=True), inc="tns")
    return ctx.cnt["tns"]


def p2c_values_xq(rec, ctx, t, sb):
    nc = ctx.nc
    with ExitStack() as ph:
        a = lambda n, shape, dt=F32: ph.enter_context(nc.sbuf_tensor("p2c_" + n, shape, dt))
        bufs = {
            "xch": a("xch", [128, FT * 128]),
            "xT": a("xT", [128, D]), "xnT": a("xnT", [128, D]),
            "stats": a("stats", [128, 24]), "mv": a("mv", [128, 2]),
            "rr": a("rr", [128, 1]), "b2": a("b2", [128, 1]),
            "tmp1": a("tmp1", [128, 1024]),
            "amax": a("amax", [128, 1]), "sc": a("sc", [128, 1]),
            "rin": a("rin", [128, 1]),
        }
        bufs["aT"] = bufs["xT"]
        wvh = a("wvh", [128, FT * 1024], BF16)
        wvl = a("wvl", [128, FT * 1024], BF16)
        wtmp = a("wtmp", [128, 1024])
        xnh = a("xnh", [128, FT * 512], BF16)
        xnl = a("xnl", [128, FT * 512], BF16)
        qstg = [a("qstg0", [128, 128], BF16), a("qstg1", [128, 128], BF16)]
        vtile = a("vtile", [128, 512])
        bc_sb = a("bc_sb", [128, 128])
        vh = a("vh", [128, 512], BF16)
        vl = a("vl", [128, 512], BF16)
        csb = a("csb", [128, NH * 512])
        with ExitStack() as ps:
            pp = [ps.enter_context(nc.psum_tensor(f"p2c_pp{i}", [128, 512], F32))
                  for i in range(4)]
            ptr = ps.enter_context(nc.psum_tensor("p2c_ptr", [1, 128], F32))
            pbc = ps.enter_context(nc.psum_tensor("p2c_pbc", [128, 128], F32))
            bufs["pb0"], bufs["pb1"] = pp[0], pp[1]
            for tc in range(3):
                ln_quant_chunk(rec, ctx, sb, bufs, t["xatt"], 512 + tc * 512, 512,
                               "split", out={"hi": xnh, "lo": xnl,
                                             "quant": {"kind": "l0", "dma_to": (t["s_xq0"], tc * 512),
                                                       "stage": qstg,
                                                       "deq_cols": sb["dxq0"][:, tc * 4:tc * 4 + 4]}})
                for h in range(NH):
                    for s in range(4):
                        col = h * 12 + tc * 4 + s
                        bt = _csum_bcast(rec, ctx, sb, col, ptr, pbc)
                        rec.wait("vector", "tns", bt)
                        rec.op("vector", lambda eng, h=h, s=s: eng.tensor_copy(
                            csb[:, h * 512 + s * 128:h * 512 + (s + 1) * 128], pbc[:]),
                            inc="vec")
                        rec.wait("tensor", "vec", ctx.cnt["vec"])
                pp_free = [0, 0, 0, 0]
                for wf in range(2):
                  for ft in range(FT):
                    wsem = ("dma", "dmb")[ft % 2]
                    rec.wait_cur("sync", "vec")
                    rec.wait_cur("sync", "tns")
                    rec.dma("sync", wtmp[:],
                            t["wqkv"][ft * 128:(ft + 1) * 128,
                                      4096 + wf * 1024:4096 + (wf + 1) * 1024],
                            sem=wsem)
                    rec.wait_cur("vector", wsem)
                    co = ft * 1024
                    rec.op("vector", lambda eng, co=co: eng.tensor_copy(wvh[:, co:co + 1024], wtmp[:]))
                    rec.fence("vector")
                    rec.op("vector", lambda eng, co=co: eng.tensor_copy(
                        bufs["tmp1"][:, 0:1024], wvh[:, co:co + 1024]))
                    rec.fence("vector")
                    rec.op("vector", lambda eng, co=co: eng.tensor_sub(
                        wtmp[:], wtmp[:], bufs["tmp1"][:, 0:1024]))
                    rec.fence("vector")
                    rec.op("vector", lambda eng, co=co: eng.tensor_copy(wvl[:, co:co + 1024], wtmp[:]),
                           inc="vec")
                  wsplit_done = ctx.cnt["vec"]
                  rec.wait("tensor", "vec", wsplit_done)
                  for h in range(wf * 8, wf * 8 + 8):
                    bi = h % 4
                    bank = pp[bi]
                    rec.wait("tensor", "vec", pp_free[bi])
                    for ft in range(FT):
                        co = ft * 1024 + (h - wf * 8) * 128
                        mm3(rec, bank[:],
                            wvh[:, co:co + 128], wvl[:, co:co + 128],
                            xnh[:, ft * 512:(ft + 1) * 512],
                            xnl[:, ft * 512:(ft + 1) * 512],
                            start=(ft == 0), stop=(ft == FT - 1))
                    mmv = rec.mark("tensor", "tns")
                    rec.wait("vector", "tns", mmv)
                    rec.wait_cur("vector", "dmag")
                    rec.op("vector", lambda eng, bank=bank, h=h: eng.tensor_mul(
                        vtile[:], bank[:], csb[:, h * 512:(h + 1) * 512]), inc="vec")
                    pp_free[bi] = ctx.cnt["vec"]
                    rec.fence("vector")
                    rec.op("vector", lambda eng: eng.tensor_copy(vh[:], vtile[:]))
                    rec.fence("vector")
                    rec.op("vector", lambda eng: eng.tensor_copy(
                        bufs["tmp1"][:, 0:512], vh[:]))
                    rec.fence("vector")
                    rec.op("vector", lambda eng: eng.tensor_sub(
                        vtile[:], vtile[:], bufs["tmp1"][:, 0:512]))
                    rec.fence("vector")
                    rec.op("vector", lambda eng: eng.tensor_copy(vl[:], vtile[:]), inc="vec")
                    vv = ctx.cnt["vec"]
                    rec.wait("gpsimd", "vec", vv)
                    rec.dma("gpsimd", t["s_vh"][h * 128:(h + 1) * 128, tc * 512:(tc + 1) * 512], vh[:])
                    rec.dma("gpsimd", t["s_vl"][h * 128:(h + 1) * 128, tc * 512:(tc + 1) * 512], vl[:])
    ctx.barrier(rec)
    rec.flush()


def _row_bcast4(rec, ctx, sb, src_cols, ptr, pbc, evac_sb, n=4):
    rec.wait_cur("tensor", "vec")
    for s in range(n):
        rec.op("tensor", lambda eng, s=s: eng.transpose(
            ptr[:], src_cols[:, s:s + 1], sb["ident"][:]), inc="tns")
        rec.wait_cur("vector", "tns")
        rec.op("vector", lambda eng: eng.tensor_copy(sb["csrow"][:], ptr[0:1, 0:128]), inc="vec")
        vv = ctx.cnt["vec"]
        rec.wait("tensor", "vec", vv)
        rec.op("tensor", lambda eng, s=s, n=n: eng.matmul(
            pbc[:, s * 128:(s + 1) * 128], sb["onesrow"][:], sb["csrow"][:],
            start=(s == 0), stop=(s == n - 1)), inc="tns")
    tv = ctx.cnt["tns"]
    rec.wait("vector", "tns", tv)
    rec.op("vector", lambda eng, n=n: eng.tensor_copy(evac_sb[:], pbc[:, 0:n * 128]), inc="vec")
    rec.fence("vector")
    return ctx.cnt["vec"]


def p2d_ffn0(rec, ctx, t, sb):
    nc = ctx.nc
    for tc in range(3):
        with ExitStack() as ph:
            a = lambda n, shape, dt=F32: ph.enter_context(
                nc.sbuf_tensor(f"p2d_{n}_{tc}", shape, dt))
            xqc = a("xqc", [128, FT * 512], BF16)
            wq2 = [a(f"wq2{i}", [128, FT * 512], BF16) for i in range(2)]
            hfe = a("hfe", [128, 44 * 512])
            hqf = a("hqf", [128, 44 * 512], BF16)
            wdh = [a(f"wdq{i}", [128, 22 * 128], BF16) for i in range(2)]
            sxb = a("sxb", [128, 512])
            shb = a("shb", [128, 512])
            g1t = a("g1t", [128, 512])
            g1tb = a("g1tb", [128, 512])
            slt = a("slt", [128, 512])
            sltb = a("sltb", [128, 512])
            g1t2 = [g1t, g1tb]
            slt2 = [slt, sltb]
            pair_free = [0, 0]
            pt_free = [0, 0]
            hamax = a("hamax", [128, 4])
            hsc = a("hsc", [128, 4])
            tmp0 = a("tmp0", [128, 4])
            with ExitStack() as ps:
                ppg2 = [ps.enter_context(nc.psum_tensor(f"p2d_ppg{i}_{tc}", [128, 512], F32))
                        for i in range(2)]
                ppu2 = [ps.enter_context(nc.psum_tensor(f"p2d_ppu{i}_{tc}", [128, 512], F32))
                        for i in range(2)]
                ppg, ppu = ppg2[0], ppu2[0]
                ptA = ps.enter_context(nc.psum_tensor(f"p2d_ptA_{tc}", [128, 128], F32))
                ptB = ps.enter_context(nc.psum_tensor(f"p2d_ptB_{tc}", [128, 128], F32))
                ptr = ps.enter_context(nc.psum_tensor(f"p2d_ptr_{tc}", [1, 128], F32))
                pbc = ps.enter_context(nc.psum_tensor(f"p2d_pbc_{tc}", [128, 512], F32))
                for ft in range(FT):
                    rec.dma("sync", xqc[:, ft * 512:(ft + 1) * 512],
                            t["s_xq0"][ft * 128:(ft + 1) * 128, tc * 512:(tc + 1) * 512])
                xq_dma = ctx.cnt["dma"]
                _row_bcast4(rec, ctx, sb, sb["dxq0"][:, tc * 4:tc * 4 + 4], ptr, pbc, sxb)
                rec.wait("tensor", "dma", xq_dma)
                rec.op("gpsimd", lambda eng: eng.memset(hamax[:], 0.0), inc="gps")
                rec.wait_cur("vector", "gps")
                step_mm = [0, 0]
                pend_tr = None

                def emit_tr(gt, hv):
                    rec.wait("tensor", "vec", hv)
                    for s in range(4):
                        pt = ptA if s % 2 == 0 else ptB
                        rec.wait("tensor", "vec", pt_free[s % 2])
                        tv = rec.op("tensor", lambda eng, pt=pt, gt=gt, s=s: eng.transpose(
                            pt[:], hfe[:, gt * 512 + s * 128:gt * 512 + s * 128 + 128],
                            sb["ident"][:]), inc="tns")
                        rec.wait("vector", "tns", tv)
                        rec.op("vector", lambda eng, pt=pt, s=s: eng.reduce_max(
                            tmp0[:, s:s + 1], pt[:], AX.X, apply_absolute_value=True))
                        pt_free[s % 2] = rec.fence("vector")
                        rec.op("vector", lambda eng, s=s: eng.tensor_max(
                            hamax[:, s:s + 1], hamax[:, s:s + 1], tmp0[:, s:s + 1]),
                            inc="vec")
                for st in range(22):
                    sbi = st % 2
                    wq = wq2[sbi]
                    rec.wait("sync", "tns", step_mm[sbi])
                    for ft in range(FT):
                        rec.dma("sync", wq[:, ft * 512:ft * 512 + 256],
                                t["s_wq0"][ft * 128:(ft + 1) * 128, st * 256:(st + 1) * 256])
                        rec.dma("sync", wq[:, ft * 512 + 256:ft * 512 + 512],
                                t["s_wq0"][ft * 128:(ft + 1) * 128,
                                           I0 + st * 256:I0 + (st + 1) * 256], sem="dmb")
                    ld_a, ld_b = ctx.cnt["dma"], ctx.cnt["dmb"]
                    rec.wait("tensor", "dma", ld_a)
                    rec.wait("tensor", "dmb", ld_b)
                    for g in range(2):
                        gt = st * 2 + g
                        pi = gt % 2
                        pg, pu = ppg2[pi], ppu2[pi]
                        rec.wait("tensor", "vec", pair_free[pi])
                        for ft in range(FT):
                            rec.op("tensor", lambda eng, ft=ft, g=g, pg=pg, wq=wq: eng.matmul(
                                pg[:], wq[:, ft * 512 + g * 128:ft * 512 + g * 128 + 128],
                                xqc[:, ft * 512:(ft + 1) * 512],
                                start=(ft == 0), stop=(ft == FT - 1)))
                        for ft in range(FT):
                            rec.op("tensor", lambda eng, ft=ft, g=g, pu=pu, wq=wq: eng.matmul(
                                pu[:], wq[:, ft * 512 + 256 + g * 128:ft * 512 + 256 + g * 128 + 128],
                                xqc[:, ft * 512:(ft + 1) * 512],
                                start=(ft == 0), stop=(ft == FT - 1)))
                        mmv = rec.mark("tensor", "tns")
                        if g == 1:
                            step_mm[sbi] = mmv
                        if pend_tr is not None:
                            emit_tr(*pend_tr)
                        rec.wait("vector", "tns", mmv)
                        rec.wait_cur("vector", "act")
                        rec.op("vector", lambda eng, pg=pg, g1x=g1t2[pi]: eng.tensor_mul(
                            g1x[:], pg[:], sxb[:]), inc="vec")
                        rec.wait_cur("scalar", "vec")
                        rec.op("scalar", lambda eng, gt=gt, g1x=g1t2[pi], slx=slt2[pi]: eng.activation(
                            slx[:], g1x[:], AF.Silu, bias=sb["zero"][:],
                            scale=sb["sgu0"][:, gt:gt + 1]), inc="act")
                        rec.wait_cur("vector", "act")
                        rec.op("vector", lambda eng, pu=pu, g1x=g1t2[pi]: eng.tensor_mul(
                            g1x[:], pu[:], sxb[:]))
                        rec.fence("vector")
                        pair_free[pi] = rec.op("vector", lambda eng, gt=gt, g1x=g1t2[pi], slx=slt2[pi]: eng.scalar_tensor_tensor(
                            hfe[:, gt * 512:(gt + 1) * 512], g1x[:],
                            sb["sgu0"][:, 44 + gt:45 + gt], slx[:],
                            ALU.mult, ALU.mult), inc="vec")
                        hv = ctx.cnt["vec"]
                        pend_tr = (gt, hv)
                emit_tr(*pend_tr)
                pend_tr = None
                rec.fence("vector")
                rec.op("vector", lambda eng: eng.tensor_scalar(
                    hsc[:], hamax[:], 1.0 / 127.0, 1e-8, ALU.mult, ALU.max))
                rec.fence("vector")
                rec.op("vector", lambda eng: eng.reciprocal(tmp0[:], hsc[:]), inc="vec")
                _row_bcast4(rec, ctx, sb, tmp0, ptr, pbc, sxb)
                _row_bcast4(rec, ctx, sb, hsc, ptr, pbc, shb)
                for gt in range(44):
                    sl = slice(gt * 512, (gt + 1) * 512)
                    rec.op("vector", lambda eng, sl=sl: eng.tensor_mul(
                        hfe[:, sl], hfe[:, sl], sxb[:]))
                    rec.fence("vector")
                    rec.op("vector", lambda eng, sl=sl: eng.tensor_scalar_add(
                        hfe[:, sl], hfe[:, sl], sb["magicM"][:]))
                    rec.fence("vector")
                    rec.op("vector", lambda eng, sl=sl: eng.tensor_single_scalar(
                        hqf[:, sl], hfe[:, sl], sb["magicM"][:], ALU.subtract), inc="vec")
                hq_done = ctx.cnt["vec"]
                if tc == 0 and "dbg_hqf" in t:
                    rec.wait("gpsimd", "vec", hq_done)
                    rec.dma("gpsimd", t["dbg_hqf"], hqf[:])
                    rec.dma("gpsimd", t["dbg_hamax"], hamax[:])
                    rec.dma("gpsimd", t["dbg_sxb"], sxb[:])
                    rec.dma("gpsimd", t["dbg_shb"], shb[:])
                rec.wait("sync", "vec", hq_done)
                wdq_mm = [0, 0]
                dn_free = [0, 0]
                for o2 in range(FT):
                    pi = o2 % 2
                    for hf in range(2):
                        wd = wdh[hf]
                        rec.wait("sync", "tns", wdq_mm[hf])
                        for it2 in range(22):
                            it = hf * 22 + it2
                            rec.dma("sync", wd[:, it2 * 128:(it2 + 1) * 128],
                                    t["s_wd0"][it * 128:(it + 1) * 128,
                                               o2 * 128:(o2 + 1) * 128],
                                    sem=("dma", "dmb", "out")[it2 % 3])
                        rec.wait("tensor", "dma", ctx.cnt["dma"])
                        rec.wait("tensor", "dmb", ctx.cnt["dmb"])
                        rec.wait("tensor", "out", ctx.cnt["out"])
                        if hf == 0:
                            rec.wait("tensor", "vec", dn_free[pi])
                        for it2 in range(22):
                            it = hf * 22 + it2
                            rec.op("tensor", lambda eng, it=it, it2=it2, pg=ppg2[pi], wd=wd:
                                   eng.matmul(pg[:], wd[:, it2 * 128:(it2 + 1) * 128],
                                              hqf[:, it * 512:(it + 1) * 512],
                                              start=(it == 0), stop=(it == 43)))
                        wdq_mm[hf] = rec.mark("tensor", "tns")
                    mmv = wdq_mm[1]
                    rec.wait("vector", "tns", mmv)
                    rec.wait_cur("vector", "act")
                    dn_free[pi] = rec.op("vector", lambda eng, pg=ppg2[pi], g1x=g1t2[pi]: eng.tensor_mul(
                        g1x[:], pg[:], shb[:]), inc="vec")
                    vv = ctx.cnt["vec"]
                    rec.wait("scalar", "vec", vv)
                    rec.wait_cur("scalar", "dmag")
                    rec.op("scalar", lambda eng, o2=o2, g1x=g1t2[pi], slx=slt2[pi]: eng.activation(
                        slx[:], g1x[:], AF.Silu, bias=sb["zero"][:],
                        scale=sb["sd0"][:, o2:o2 + 1]), inc="act")
                    av = ctx.cnt["act"]
                    rec.wait("gpsimd", "act", av)
                    rec.dma("gpsimd", t["s_g0"][o2 * 128:(o2 + 1) * 128, tc * 512:(tc + 1) * 512],
                            slt2[pi][:])
                ctx.barrier(rec)
                rec.flush()
    ctx.barrier(rec)
    rec.flush()


def p2e_outproj_xl1(rec, ctx, t, sb):
    nc = ctx.nc
    for half in range(2):
        with ExitStack() as ph:
            a = lambda n, shape, dt=F32: ph.enter_context(
                nc.sbuf_tensor(f"p2e_{n}_{half}", shape, dt))
            woh = a("woh", [128, FT * 1024], BF16)
            wol = a("wol", [128, FT * 1024], BF16)
            wtmp = a("wtmp", [128, 1024])
            tmp1 = a("tmp1", [128, 1024])
            vh = a("vh", [128, FT * 512], BF16)
            vl = a("vl", [128, FT * 512], BF16)
            g0c = a("g0c", [128, 8 * 512])
            xfc = a("xfc", [128, 8 * 512])
            s0t = a("s0t", [128, 512])
            xlt = a("xlt", [128, 512])
            with ExitStack() as ps:
                pp = [ps.enter_context(nc.psum_tensor(f"p2e_pp{i}_{half}", [128, 512], F32))
                      for i in range(4)]
                for ft in range(FT):
                    wsem = ("dma", "dmb")[ft % 2]
                    rec.wait_cur("sync", "vec")
                    rec.dma("sync", wtmp[:],
                            t["wo0"][ft * 128:(ft + 1) * 128, half * 1024:(half + 1) * 1024],
                            sem=wsem)
                    rec.wait_cur("vector", wsem)
                    co = ft * 1024
                    rec.op("vector", lambda eng, co=co: eng.tensor_copy(woh[:, co:co + 1024], wtmp[:]))
                    rec.fence("vector")
                    rec.op("vector", lambda eng, co=co: eng.tensor_copy(
                        tmp1[:], woh[:, co:co + 1024]))
                    rec.fence("vector")
                    rec.op("vector", lambda eng: eng.tensor_sub(wtmp[:], wtmp[:], tmp1[:]))
                    rec.fence("vector")
                    rec.op("vector", lambda eng, co=co: eng.tensor_copy(wol[:, co:co + 1024], wtmp[:]),
                           inc="vec")
                wsplit = ctx.cnt["vec"]
                for tc in range(3):
                    rec.wait_cur("sync", "tns")
                    for ft in range(FT):
                        rec.dma("sync", vh[:, ft * 512:(ft + 1) * 512],
                                t["s_vh"][ft * 128:(ft + 1) * 128, tc * 512:(tc + 1) * 512])
                        rec.dma("sync", vl[:, ft * 512:(ft + 1) * 512],
                                t["s_vl"][ft * 128:(ft + 1) * 128, tc * 512:(tc + 1) * 512],
                                sem="dmb")
                    rec.wait_cur("sync", "vec")
                    for ot in range(8):
                        g = half * 8 + ot
                        rec.dma("sync", g0c[:, ot * 512:(ot + 1) * 512],
                                t["s_g0"][g * 128:(g + 1) * 128, tc * 512:(tc + 1) * 512],
                                sem="out")
                        rec.dma("sync", xfc[:, ot * 512:(ot + 1) * 512],
                                t["xatt"][g * 128:(g + 1) * 128,
                                          512 + tc * 512:512 + (tc + 1) * 512],
                                sem="out")
                    rec.wait("tensor", "dma", ctx.cnt["dma"])
                    rec.wait("tensor", "dmb", ctx.cnt["dmb"])
                    rec.wait("tensor", "out", ctx.cnt["out"])
                    rec.wait("tensor", "vec", wsplit)
                    pp_free = [0, 0, 0, 0]
                    for ot in range(8):
                        g = half * 8 + ot
                        bi = ot % 4
                        bank = pp[bi]
                        rec.wait("tensor", "vec", pp_free[bi])
                        for ft in range(FT):
                            co = ft * 1024 + ot * 128
                            mm3(rec, bank[:],
                                woh[:, co:co + 128], wol[:, co:co + 128],
                                vh[:, ft * 512:(ft + 1) * 512],
                                vl[:, ft * 512:(ft + 1) * 512],
                                start=(ft == 0), stop=(ft == FT - 1))
                        mmv = rec.mark("tensor", "tns")
                        rec.wait("vector", "tns", mmv)
                        rec.wait_cur("vector", "dmag")
                        pp_free[bi] = rec.op("vector", lambda eng, bank=bank, ot=ot: eng.tensor_mul(
                            s0t[:], bank[:], g0c[:, ot * 512:(ot + 1) * 512]), inc="vec")
                        rec.fence("vector")
                        rec.op("vector", lambda eng, ot=ot: eng.tensor_add(
                            xlt[:], s0t[:], xfc[:, ot * 512:(ot + 1) * 512]), inc="vec")
                        vv = ctx.cnt["vec"]
                        rec.wait("gpsimd", "vec", vv)
                        rec.dma("gpsimd", t["s_xl1"][g * 128:(g + 1) * 128,
                                                     tc * 512:(tc + 1) * 512], xlt[:])
                ctx.barrier(rec)
                rec.flush()
    ctx.barrier(rec)
    rec.flush()


def p3a_l1_proj(rec, ctx, t, sb):
    nc = ctx.nc
    with ExitStack() as ph:
        a = lambda n, shape, dt=F32: ph.enter_context(nc.sbuf_tensor("p3a_" + n, shape, dt))
        xn1h = a("xn1h", [128, FT * TF], BF16)
        xn1l = a("xn1l", [128, FT * TF], BF16)
        with ExitStack() as ps:
            c = lambda n, shape, dt=F32: ps.enter_context(nc.sbuf_tensor("p3a_" + n, shape, dt))
            bufs = {
                "xch": c("xch", [128, FT * 128]),
                "xT": c("xT", [128, D]), "xnT": c("xnT", [128, D]),
                "stats": c("stats", [128, 24]), "mv": c("mv", [128, 2]),
                "rr": c("rr", [128, 1]), "b2": c("b2", [128, 1]),
                "tmp1": c("tmp1", [128, 1024]),
                "amax": c("amax", [128, 1]), "sc": c("sc", [128, 1]),
                "rin": c("rin", [128, 1]),
                "pb0": ps.enter_context(nc.psum_tensor("p3a_pb0", [128, 512], F32)),
                "pb1": ps.enter_context(nc.psum_tensor("p3a_pb1", [128, 512], F32)),
            }
            bufs["aT"] = bufs["xT"]
            qstg = [ps.enter_context(nc.sbuf_tensor(f"p3a_qstg{i}", [128, 128], BF16))
                    for i in range(2)]

            class _View:
                def __init__(self, slab, tc):
                    self.slab, self.tc = slab, tc

                def __getitem__(self, key):
                    p_sl, c_sl = key
                    ft = c_sl.start // 512
                    off = c_sl.start - ft * 512
                    base = ft * TF + self.tc * 512 + off
                    return self.slab[p_sl, base:base + (c_sl.stop - c_sl.start)]
            for tc in range(3):
                out = {"hi": _View(xn1h, tc), "lo": _View(xn1l, tc)}
                if tc >= 1:
                    out["quant"] = {"kind": "l1", "dma_to": (t["s_xq1"], (tc - 1) * 512),
                                    "stage": qstg,
                                    "deq_cols": sb["dxq1"][:, (tc - 1) * 4:(tc - 1) * 4 + 4]}
                ln_quant_chunk(rec, ctx, sb, bufs, t["s_xl1"], tc * 512, 512, "split",
                               out=out)
            ctx.barrier(rec)
            rec.flush()
        es_r = ExitStack()
        sb["ropeq_c"] = es_r.enter_context(nc.sbuf_tensor("c_ropeq_c", [HD, TOWN], F32))
        sb["ropeq_s"] = es_r.enter_context(nc.sbuf_tensor("c_ropeq_s", [HD, TOWN], F32))
        sb["ropek_c"] = es_r.enter_context(nc.sbuf_tensor("c_ropek_c", [HD, TF], F32))
        sb["ropek_s"] = es_r.enter_context(nc.sbuf_tensor("c_ropek_s", [HD, TF], F32))
        for name in ["ropeq_c", "ropeq_s", "ropek_c", "ropek_s"]:
            rec.dma("sync", sb[name][:], t[name], sem="out")
        rope_dma = ctx.cnt["out"]
        for which, wf in [("k", 0), ("k", 1), ("k", 2), ("k", 3),
                          ("q", 0), ("q", 1), ("q", 2), ("q", 3)]:
            with ExitStack() as ps:
                b = lambda n, shape, dt=F32: ps.enter_context(
                    nc.sbuf_tensor(f"p3a_{n}_{which}{wf}", shape, dt))
                wh = b("wh", [128, FT * 512], BF16)
                wl = b("wl", [128, FT * 512], BF16)
                wtmp = b("wtmp", [128, 512])
                wtm2 = b("wtm2", [128, 512])
                T = TF if which == "k" else TOWN
                col0 = 0 if which == "k" else 512
                kf = b("kf", [128, T])
                ks = b("ks", [128, T])
                kt = b("kt", [128, T])
                kh2 = b("kh2", [128, T], BF16)
                rcs = sb["ropek_c"] if which == "k" else sb["ropeq_c"]
                rss = sb["ropek_s"] if which == "k" else sb["ropeq_s"]
                pj = [ps.enter_context(nc.psum_tensor(f"p3a_pj{i}_{which}{wf}", [128, 512], F32))
                      for i in range(3)]
                wsrc = t["l1wk"] if which == "k" else t["l1wq"]
                for ft in range(FT):
                    wsem = ("dma", "dmb")[ft % 2]
                    rec.wait_cur("sync", "vec")
                    rec.dma("sync", wtmp[:], wsrc[ft * 128:(ft + 1) * 128,
                                                  wf * 512:(wf + 1) * 512], sem=wsem)
                    rec.wait_cur("vector", wsem)
                    co = ft * 512
                    rec.op("vector", lambda eng, co=co: eng.tensor_copy(wh[:, co:co + 512], wtmp[:]))
                    rec.fence("vector")
                    rec.op("vector", lambda eng, co=co: eng.tensor_copy(wtm2[:], wh[:, co:co + 512]))
                    rec.fence("vector")
                    rec.op("vector", lambda eng: eng.tensor_sub(wtmp[:], wtmp[:], wtm2[:]))
                    rec.fence("vector")
                    rec.op("vector", lambda eng, co=co: eng.tensor_copy(wl[:, co:co + 512], wtmp[:]),
                           inc="vec")
                wdone = ctx.cnt["vec"]
                rec.wait("vector", "out", rope_dma)
                rec.wait("tensor", "vec", wdone)
                for h in range(wf * 4, wf * 4 + 4):
                    for tcc in range(T // 512):
                        bank = pj[tcc % 3]
                        rec.wait_cur("tensor", "vec")
                        for ft in range(FT):
                            co = ft * 512 + (h - wf * 4) * 128
                            cx = ft * TF + col0 + tcc * 512
                            mm3(rec, bank[:],
                                wh[:, co:co + 128], wl[:, co:co + 128],
                                xn1h[:, cx:cx + 512], xn1l[:, cx:cx + 512],
                                start=(ft == 0), stop=(ft == FT - 1))
                        mmv = rec.mark("tensor", "tns")
                        rec.wait("vector", "tns", mmv)
                        rec.op("vector", lambda eng, bank=bank, tcc=tcc: eng.tensor_copy(
                            kf[:, tcc * 512:(tcc + 1) * 512], bank[:]), inc="vec")
                    mask = []
                    for i in range(0, 32, 2):
                        mask += [i + 1, i]
                    rec.fence("vector")
                    rec.op("vector", lambda eng: eng.stream_shuffle(ks[:], kf[:], mask))
                    rec.fence("vector")
                    rec.op("vector", lambda eng: eng.tensor_mul(kt[:], kf[:], rcs[:]))
                    rec.op("vector", lambda eng: eng.tensor_mul(ks[:], ks[:], rss[:]))
                    rec.fence("vector")
                    rec.op("vector", lambda eng: eng.tensor_add(kf[:], kt[:], ks[:]))
                    rec.fence("vector")
                    rec.op("vector", lambda eng: eng.tensor_copy(kh2[:], kf[:]))
                    rec.fence("vector")
                    rec.op("vector", lambda eng: eng.tensor_copy(kt[:], kh2[:]))
                    rec.fence("vector")
                    rec.op("vector", lambda eng: eng.tensor_sub(kf[:], kf[:], kt[:]))
                    rec.fence("vector")
                    dsth = t["s_krh"] if which == "k" else t["s_qrh"]
                    dstl = t["s_krl"] if which == "k" else t["s_qrl"]
                    rec.op("vector", lambda eng: eng.tensor_copy(ks[:], kf[:]), inc="vec")
                    rec.fence("vector")
                    vv = ctx.cnt["vec"]
                    rec.wait("gpsimd", "vec", vv)
                    rec.wait_cur("gpsimd", "dmag")
                    rec.dma("gpsimd", dsth[h * 128:(h + 1) * 128, :], kh2[:])
                    rec.wait_cur("vector", "dmag")
                    rec.op("vector", lambda eng: eng.tensor_copy(kh2[:], ks[:]), inc="vec")
                    vv = ctx.cnt["vec"]
                    rec.wait("gpsimd", "vec", vv)
                    rec.dma("gpsimd", dstl[h * 128:(h + 1) * 128, :], kh2[:])
                    rec.wait_cur("vector", "dmag")
                ctx.barrier(rec)
                rec.flush()
        es_r.close()
        for wf in range(4):
          with ExitStack() as ps:
            b = lambda n, shape, dt=F32: ps.enter_context(
                nc.sbuf_tensor(f"p3a_{n}_v{wf}", shape, dt))
            wh = b("wh", [128, FT * 512], BF16)
            wl = b("wl", [128, FT * 512], BF16)
            wtmp = b("wtmp", [128, 512])
            wtm2 = b("wtm2", [128, 512])
            vts = b("vts", [128, 512])
            pj = [ps.enter_context(nc.psum_tensor(f"p3a_pjv{i}{wf}", [128, 512], F32))
                  for i in range(2)]
            for ft in range(FT):
                wsem = ("dma", "dmb")[ft % 2]
                rec.wait_cur("sync", "vec")
                rec.dma("sync", wtmp[:], t["l1wv"][ft * 128:(ft + 1) * 128,
                                                   wf * 512:(wf + 1) * 512], sem=wsem)
                rec.wait_cur("vector", wsem)
                co = ft * 512
                rec.op("vector", lambda eng, co=co: eng.tensor_copy(wh[:, co:co + 512], wtmp[:]))
                rec.fence("vector")
                rec.op("vector", lambda eng, co=co: eng.tensor_copy(wtm2[:], wh[:, co:co + 512]))
                rec.fence("vector")
                rec.op("vector", lambda eng: eng.tensor_sub(wtmp[:], wtmp[:], wtm2[:]))
                rec.fence("vector")
                rec.op("vector", lambda eng, co=co: eng.tensor_copy(wl[:, co:co + 512], wtmp[:]),
                       inc="vec")
            wdone = ctx.cnt["vec"]
            rec.wait("tensor", "vec", wdone)
            for jt in range(12):
                for vo in [wf]:
                    bank = pj[vo % 2]
                    rec.wait_cur("tensor", "vec")
                    for ft in range(FT):
                        cx = ft * TF + jt * 128
                        co = ft * 512
                        mm3(rec, bank[:],
                            xn1h[:, cx:cx + 128], xn1l[:, cx:cx + 128],
                            wh[:, co:co + 512], wl[:, co:co + 512],
                            start=(ft == 0), stop=(ft == FT - 1))
                    mmv = rec.mark("tensor", "tns")
                    rec.wait("vector", "tns", mmv)
                    rec.wait_cur("vector", "dmag")
                    rec.op("vector", lambda eng, bank=bank: eng.tensor_copy(vts[:], bank[:]),
                           inc="vec")
                    vv = ctx.cnt["vec"]
                    rec.wait("gpsimd", "vec", vv)
                    rec.dma("gpsimd", t["s_vT"][jt * 128:(jt + 1) * 128, vo * 512:(vo + 1) * 512],
                            vts[:])
            ctx.barrier(rec)
            rec.flush()
    ctx.barrier(rec)
    rec.flush()


def p3b_l1_attn(rec, ctx, t, sb):
    nc = ctx.nc
    with ExitStack() as ph:
        a = lambda n, shape, dt=F32: ph.enter_context(nc.sbuf_tensor("p3b_" + n, shape, dt))
        for name in ["maskA", "maskB"]:
            sb[name] = ph.enter_context(nc.sbuf_tensor("c_" + name, [128, 8 * 512], BF16))
        for name in ["maskA", "maskB"]:
            for jt in range(8):
                rec.dma("sync", sb[name][:, jt * 512:(jt + 1) * 512],
                        t[name][jt * 128:(jt + 1) * 128, :])
        mdma = ctx.cnt["dma"]
        woh = a("woh", [128, FT * 1024], BF16)
        wol = a("wol", [128, FT * 1024], BF16)
        wtmp = a("wtmp", [128, 1024])
        wtm2 = a("wtm2", [128, 1024])
        khb = a("khb", [128, 1024], BF16)
        klb = a("klb", [128, 1024], BF16)
        qhb = a("qhb", [128, 512], BF16)
        qlb = a("qlb", [128, 512], BF16)
        eb = a("eb", [128, 8 * 512])
        zsb = a("zsb", [128, 4])
        zr = a("zr", [128, 4])
        zbc = a("zbc", [128, 512])
        vts = a("vts", [128, 128])
        onorm = a("onorm", [128, 512])
        oh = a("oh", [128, NH * 512], BF16)
        ol = a("ol", [128, NH * 512], BF16)
        s1e = a("s1e", [128, 512])
        with ExitStack() as ps:
            psc = [ps.enter_context(nc.psum_tensor(f"p3b_psc{i}", [128, 512], F32))
                   for i in range(2)]
            psz = ps.enter_context(nc.psum_tensor("p3b_psz", [128, 512], F32))
            pso = ps.enter_context(nc.psum_tensor("p3b_pso", [128, 512], F32))
            ptr = ps.enter_context(nc.psum_tensor("p3b_ptr", [1, 128], F32))
            pbc = ps.enter_context(nc.psum_tensor("p3b_pbc", [128, 512], F32))
            rec.wait("vector", "dma", mdma)
            for qb in range(2):
                for h in range(NH):
                    rec.wait_cur("sync", "tns")
                    rec.dma("sync", khb[:], t["s_krh"][h * 128:(h + 1) * 128,
                                                       qb * 512:qb * 512 + 1024])
                    rec.dma("sync", klb[:], t["s_krl"][h * 128:(h + 1) * 128,
                                                       qb * 512:qb * 512 + 1024])
                    rec.dma("sync", qhb[:], t["s_qrh"][h * 128:(h + 1) * 128,
                                                       qb * 512:(qb + 1) * 512])
                    rec.dma("sync", qlb[:], t["s_qrl"][h * 128:(h + 1) * 128,
                                                       qb * 512:(qb + 1) * 512])
                    kdma = ctx.cnt["dma"]
                    rec.wait("tensor", "dma", kdma)
                    mname = "maskA" if qb == 0 else "maskB"
                    rec.wait_cur("scalar", "tns")
                    for jt in range(8):
                        bank = psc[jt % 2]
                        rec.wait_cur("tensor", "act")
                        mm3(rec, bank[:],
                            khb[:, jt * 128:(jt + 1) * 128], klb[:, jt * 128:(jt + 1) * 128],
                            qhb[:], qlb[:],
                            start=True, stop=True)
                        mmv = rec.mark("tensor", "tns")
                        rec.wait("vector", "tns", mmv)
                        rec.op("vector", lambda eng, bank=bank, jt=jt, mname=mname: eng.tensor_scalar(
                            bank[:], bank[:], SC_ATT, -C_SM, ALU.mult, ALU.add)
                            if False else eng.tensor_add(
                            bank[:], bank[:], sb[mname][:, jt * 512:(jt + 1) * 512]), inc="vec")
                        vv = ctx.cnt["vec"]
                        rec.wait("scalar", "vec", vv)
                        rec.op("scalar", lambda eng, bank=bank, jt=jt: eng.activation(
                            eb[:, jt * 512:(jt + 1) * 512], bank[:], AF.Exp,
                            bias=sb["negC"][:], scale=SC_ATT), inc="act")
                    edone = ctx.cnt["act"]
                    rec.wait("tensor", "act", edone)
                    for jt in range(8):
                        for qs in range(4):
                            rec.op("tensor", lambda eng, jt=jt, qs=qs: eng.matmul(
                                psz[:, qs:qs + 1],
                                eb[:, jt * 512 + qs * 128:jt * 512 + qs * 128 + 128],
                                sb["onescol"][:],
                                start=(jt == 0 and qs == 0), stop=(jt == 7 and qs == 3)))
                    zt = rec.mark("tensor", "tns")
                    rec.wait("vector", "tns", zt)
                    rec.op("vector", lambda eng: eng.tensor_copy(zsb[:], psz[:, 0:4]))
                    rec.fence("vector")
                    rec.op("vector", lambda eng: eng.reciprocal(zr[:], zsb[:]), inc="vec")
                    _row_bcast4(rec, ctx, sb, zr, ptr, pbc, zbc)
                    for jt in range(8):
                        gjt = qb * 4 + jt
                        rec.wait_cur("sync", "vec" if jt == 0 else "tns")
                        rec.dma("sync", vts[:], t["s_vT"][gjt * 128:(gjt + 1) * 128,
                                                          h * 128:(h + 1) * 128])
                        vdma = ctx.cnt["dma"]
                        rec.wait("tensor", "dma", vdma)
                        for qs in range(2):
                            rec.op("tensor", lambda eng, jt=jt, qs=qs: eng.matmul(
                                pso[:, qs * 256:(qs + 1) * 256], vts[:],
                                eb[:, jt * 512 + qs * 256:jt * 512 + qs * 256 + 256],
                                start=(jt == 0 and qs == 0), stop=(jt == 7 and qs == 1)))
                        rec.mark("tensor", "tns")
                    ot = ctx.cnt["tns"]
                    rec.wait("vector", "tns", ot)
                    rec.op("vector", lambda eng: eng.tensor_mul(onorm[:], pso[:], zbc[:]))
                    rec.fence("vector")
                    co = h * 512
                    rec.op("vector", lambda eng, co=co: eng.tensor_copy(oh[:, co:co + 512], onorm[:]))
                    rec.fence("vector")
                    rec.op("vector", lambda eng, co=co: eng.tensor_copy(zbc[:], oh[:, co:co + 512]))
                    rec.fence("vector")
                    rec.op("vector", lambda eng: eng.tensor_sub(onorm[:], onorm[:], zbc[:]))
                    rec.fence("vector")
                    rec.op("vector", lambda eng, co=co: eng.tensor_copy(ol[:, co:co + 512], onorm[:]),
                           inc="vec")
                rec.wait("tensor", "vec", ctx.cnt["vec"])
                for ohalf in range(2):
                    for ft in range(FT):
                        wsem = ("dma", "dmb")[ft % 2]
                        rec.wait_cur("sync", "vec")
                        rec.wait_cur("sync", "tns")
                        rec.dma("sync", wtmp[:],
                                t["l1wo"][ft * 128:(ft + 1) * 128,
                                          ohalf * 1024:(ohalf + 1) * 1024], sem=wsem)
                        rec.wait_cur("vector", wsem)
                        co = ft * 1024
                        rec.op("vector", lambda eng, co=co: eng.tensor_copy(
                            woh[:, co:co + 1024], wtmp[:]))
                        rec.fence("vector")
                        rec.op("vector", lambda eng, co=co: eng.tensor_copy(
                            wtm2[:], woh[:, co:co + 1024]))
                        rec.fence("vector")
                        rec.op("vector", lambda eng: eng.tensor_sub(wtmp[:], wtmp[:], wtm2[:]))
                        rec.fence("vector")
                        rec.op("vector", lambda eng, co=co: eng.tensor_copy(
                            wol[:, co:co + 1024], wtmp[:]), inc="vec")
                    wodone = ctx.cnt["vec"]
                    rec.wait("tensor", "vec", wodone)
                    for ot2 in range(ohalf * 8, ohalf * 8 + 8):
                        bank = psc[ot2 % 2]
                        rec.wait_cur("tensor", "vec")
                        for hh in range(NH):
                            co = hh * 1024 + (ot2 - ohalf * 8) * 128
                            mm3(rec, bank[:],
                                woh[:, co:co + 128], wol[:, co:co + 128],
                                oh[:, hh * 512:(hh + 1) * 512], ol[:, hh * 512:(hh + 1) * 512],
                                start=(hh == 0), stop=(hh == NH - 1))
                        mmv = rec.mark("tensor", "tns")
                        rec.wait("vector", "tns", mmv)
                        rec.wait_cur("vector", "dmag")
                        rec.op("vector", lambda eng, bank=bank: eng.tensor_copy(s1e[:], bank[:]),
                               inc="vec")
                        vv = ctx.cnt["vec"]
                        rec.wait("gpsimd", "vec", vv)
                        rec.dma("gpsimd", t["s_s1"][ot2 * 128:(ot2 + 1) * 128,
                                                    qb * 512:(qb + 1) * 512], s1e[:])
                ctx.barrier(rec)
                rec.flush()
    ctx.barrier(rec)
    rec.flush()


def p4a_ffn1_gateup(rec, ctx, t, sb):
    nc = ctx.nc
    with ExitStack() as ph:
        a = lambda n, shape, dt=F32: ph.enter_context(nc.sbuf_tensor("p4a_" + n, shape, dt))
        xq1c = a("xq1c", [128, FT * TOWN], BF16)
        wgq = a("wgq", [128, FT * 512], BF16)
        wuq = a("wuq", [128, FT * 512], BF16)
        sxb1 = a("sxb1", [128, 1024])
        g1t = a("g1t", [128, 512])
        slt = a("slt", [128, 512])
        hstr = a("hstr", [128, 512])
        with ExitStack() as ps:
            ppg = ps.enter_context(nc.psum_tensor("p4a_ppg", [128, 512], F32))
            ppu = ps.enter_context(nc.psum_tensor("p4a_ppu", [128, 512], F32))
            ptr = ps.enter_context(nc.psum_tensor("p4a_ptr", [1, 128], F32))
            pbc = ps.enter_context(nc.psum_tensor("p4a_pbc", [128, 512], F32))
            for ft in range(FT):
                rec.dma("sync", xq1c[:, ft * TOWN:(ft + 1) * TOWN],
                        t["s_xq1"][ft * 128:(ft + 1) * 128, :])
            xdma = ctx.cnt["dma"]
            _row_bcast4(rec, ctx, sb, sb["dxq1"][:, 0:4], ptr, pbc, sxb1[:, 0:512])
            _row_bcast4(rec, ctx, sb, sb["dxq1"][:, 4:8], ptr, pbc, sxb1[:, 512:1024])
            rec.wait("tensor", "dma", xdma)
            for oc in range(16):
                rec.wait_cur("sync", "tns")
                for ft in range(FT):
                    rec.dma("sync", wgq[:, ft * 512:(ft + 1) * 512],
                            t["s_wg1"][ft * 128:(ft + 1) * 128, oc * 512:(oc + 1) * 512])
                    rec.dma("sync", wuq[:, ft * 512:(ft + 1) * 512],
                            t["s_wu1"][ft * 128:(ft + 1) * 128, oc * 512:(oc + 1) * 512],
                            sem="dmb")
                rec.wait("tensor", "dma", ctx.cnt["dma"])
                rec.wait("tensor", "dmb", ctx.cnt["dmb"])
                for g in range(4):
                    gt = oc * 4 + g
                    for th in range(2):
                        rec.wait_cur("tensor", "vec")
                        for ft in range(FT):
                            rec.op("tensor", lambda eng, ft=ft, g=g, th=th: eng.matmul(
                                ppg[:], wgq[:, ft * 512 + g * 128:ft * 512 + g * 128 + 128],
                                xq1c[:, ft * TOWN + th * 512:ft * TOWN + (th + 1) * 512],
                                start=(ft == 0), stop=(ft == FT - 1)))
                        for ft in range(FT):
                            rec.op("tensor", lambda eng, ft=ft, g=g, th=th: eng.matmul(
                                ppu[:], wuq[:, ft * 512 + g * 128:ft * 512 + g * 128 + 128],
                                xq1c[:, ft * TOWN + th * 512:ft * TOWN + (th + 1) * 512],
                                start=(ft == 0), stop=(ft == FT - 1)))
                        mmv = rec.mark("tensor", "tns")
                        rec.wait("vector", "tns", mmv)
                        rec.wait_cur("vector", "dmag")
                        rec.wait_cur("vector", "act")
                        rec.op("vector", lambda eng, th=th: eng.tensor_mul(
                            g1t[:], ppg[:], sxb1[:, th * 512:(th + 1) * 512]), inc="vec")
                        rec.wait_cur("scalar", "vec")
                        rec.op("scalar", lambda eng: eng.activation(
                            slt[:], g1t[:], AF.Silu, bias=sb["zero"][:],
                            scale=sb["deq_g1"][:]), inc="act")
                        rec.wait_cur("vector", "act")
                        rec.op("vector", lambda eng, th=th: eng.tensor_mul(
                            g1t[:], ppu[:], sxb1[:, th * 512:(th + 1) * 512]))
                        rec.fence("vector")
                        rec.op("vector", lambda eng: eng.scalar_tensor_tensor(
                            hstr[:], g1t[:], sb["deq_u1"][:], slt[:],
                            ALU.mult, ALU.mult), inc="vec")
                        vv = ctx.cnt["vec"]
                        rec.wait("gpsimd", "vec", vv)
                        rec.dma("gpsimd", t["s_h1"][gt * 128:(gt + 1) * 128,
                                                    th * 512:(th + 1) * 512], hstr[:])
    ctx.barrier(rec)
    rec.flush()


def p4b_fwht_quant(rec, ctx, t, sb):
    nc = ctx.nc
    NTOK = 256
    for tc in range(4):
        with ExitStack() as ph:
            a = lambda n, shape, dt=F32: ph.enter_context(
                nc.sbuf_tensor(f"p4b_{n}_{tc}", shape, dt))
            hfe = a("hfe", [128, 64 * NTOK])
            bt = a("bt", [128, 32 * NTOK])
            hh = a("hh", [128, 64 * NTOK], BF16)
            hl = a("hl", [128, 64 * NTOK], BF16)
            rb = a("rb", [128, NTOK])
            yamax = a("yamax", [128, 2])
            ysc = a("ysc", [128, 2])
            tmp2 = a("tmp2", [128, 2])
            stg = [a("stg0", [128, NTOK], BF16), a("stg1", [128, NTOK], BF16)]
            with ExitStack() as ps:
                pm = ps.enter_context(nc.psum_tensor(f"p4b_pm_{tc}", [128, NTOK], F32))
                ptA = ps.enter_context(nc.psum_tensor(f"p4b_ptA_{tc}", [128, 128], F32))
                ptB = ps.enter_context(nc.psum_tensor(f"p4b_ptB_{tc}", [128, 128], F32))
                ptr = ps.enter_context(nc.psum_tensor(f"p4b_ptr_{tc}", [1, 128], F32))
                pbc = ps.enter_context(nc.psum_tensor(f"p4b_pbc_{tc}", [128, 512], F32))
                rec.wait_cur("sync", "vec")
                rec.wait_cur("gpsimd", "vec")
                for it in range(64):
                    if it % 4 == 3:
                        rec.dma("gpsimd", hfe[:, it * NTOK:(it + 1) * NTOK],
                                t["s_h1"][it * 128:(it + 1) * 128, tc * NTOK:(tc + 1) * NTOK],
                                sem="dmah")
                    else:
                        rec.dma("sync", hfe[:, it * NTOK:(it + 1) * NTOK],
                                t["s_h1"][it * 128:(it + 1) * 128, tc * NTOK:(tc + 1) * NTOK],
                                sem=("dma", "dmb", "out")[it % 4])
                rec.wait("vector", "dma", ctx.cnt["dma"])
                rec.wait("vector", "dmb", ctx.cnt["dmb"])
                rec.wait("vector", "out", ctx.cnt["out"])
                rec.wait("vector", "dmah", ctx.cnt["dmah"])
                for lvl in range(6):
                    d = 32 >> lvl
                    c = d * NTOK
                    va = hfe[:].rearrange("p (b two c) -> p b two c", two=2, c=c)
                    av = va[:, :, 0:1, :]
                    bv = va[:, :, 1:2, :]
                    btv = bt[:, 0:32 * NTOK]
                    rec.op("vector", lambda eng, av=av, bv=bv, btv=btv: eng.tensor_sub(
                        btv, av, bv))
                    rec.fence("vector")
                    rec.op("vector", lambda eng, av=av, bv=bv: eng.tensor_add(av, av, bv))
                    rec.fence("vector")
                    rec.op("vector", lambda eng, bv=bv, btv=btv: eng.tensor_copy(bv, btv))
                    rec.fence("vector")
                rec.op("vector", lambda eng: eng.tensor_single_scalar(
                    hfe[:], hfe[:], float(2.0 ** -6.5), ALU.mult))
                rec.fence("vector")
                rec.op("vector", lambda eng: eng.tensor_copy(hh[:], hfe[:]))
                rec.fence("vector")
                rec.op("vector", lambda eng: eng.tensor_sub(hfe[:], hfe[:], hh[:]))
                rec.fence("vector")
                rec.op("vector", lambda eng: eng.tensor_copy(hl[:], hfe[:]), inc="vec")
                spl = ctx.cnt["vec"]
                rec.wait("tensor", "vec", spl)
                rec.op("gpsimd", lambda eng: eng.memset(yamax[:], 0.0), inc="gps")
                rec.wait_cur("vector", "gps")
                for it in range(64):
                    rec.op("tensor", lambda eng, it=it: eng.matmul(
                        pm[:], sb["h128b"][:], hh[:, it * NTOK:(it + 1) * NTOK],
                        start=True, stop=False))
                    rec.op("tensor", lambda eng, it=it: eng.matmul(
                        pm[:], sb["h128b"][:], hl[:, it * NTOK:(it + 1) * NTOK],
                        start=False, stop=True))
                    mmv = rec.mark("tensor", "tns")
                    rec.wait("vector", "tns", mmv)
                    rec.op("vector", lambda eng, it=it: eng.tensor_copy(
                        hfe[:, it * NTOK:(it + 1) * NTOK], pm[:]), inc="vec")
                    ev = ctx.cnt["vec"]
                    rec.wait("tensor", "vec", ev)
                    for s in range(2):
                        pt = ptA if s % 2 == 0 else ptB
                        rec.wait_cur("tensor", "vec")
                        rec.op("tensor", lambda eng, pt=pt, it=it, s=s: eng.transpose(
                            pt[:], hfe[:, it * NTOK + s * 128:it * NTOK + s * 128 + 128],
                            sb["ident"][:]), inc="tns")
                        rec.wait_cur("vector", "tns")
                        rec.op("vector", lambda eng, pt=pt, s=s: eng.reduce_max(
                            tmp2[:, s:s + 1], pt[:], AX.X, apply_absolute_value=True))
                        rec.fence("vector")
                        rec.op("vector", lambda eng, s=s: eng.tensor_max(
                            yamax[:, s:s + 1], yamax[:, s:s + 1], tmp2[:, s:s + 1]),
                            inc="vec")
                rec.fence("vector")
                rec.op("vector", lambda eng: eng.tensor_single_scalar(
                    ysc[:], yamax[:], 1e-5, ALU.max))
                rec.fence("vector")
                rec.op("vector", lambda eng: eng.reciprocal(tmp2[:], ysc[:]))
                rec.fence("vector")
                rec.op("vector", lambda eng: eng.tensor_single_scalar(
                    tmp2[:], tmp2[:], 128.0, ALU.mult))
                rec.op("vector", lambda eng, tc=tc: eng.tensor_single_scalar(
                    sb["dyq1"][:, tc * 2:tc * 2 + 2], ysc[:], 1.0 / 128.0, ALU.mult),
                    inc="vec")
                _row_bcast4(rec, ctx, sb, tmp2, ptr, pbc, rb, n=2)
                for it in range(64):
                    sl = slice(it * NTOK, (it + 1) * NTOK)
                    sg = stg[it % 2]
                    rec.op("vector", lambda eng, sl=sl: eng.tensor_mul(hfe[:, sl], hfe[:, sl], rb[:]))
                    rec.fence("vector")
                    rec.op("vector", lambda eng, sl=sl: eng.tensor_scalar_add(
                        hfe[:, sl], hfe[:, sl], sb["magicM"][:]))
                    rec.fence("vector")
                    rec.op("vector", lambda eng, sl=sl: eng.tensor_single_scalar(
                        hfe[:, sl], hfe[:, sl], sb["magicM"][:], ALU.subtract))
                    rec.fence("vector")
                    rec.wait_cur("vector", "dmag")
                    rec.op("vector", lambda eng, sl=sl, sg=sg: eng.tensor_scalar(
                        sg[:], hfe[:, sl], 127.0, -128.0, ALU.min, ALU.max), inc="vec")
                    vv = ctx.cnt["vec"]
                    rec.wait("gpsimd", "vec", vv)
                    rec.dma("gpsimd", t["s_yq1"][it * 128:(it + 1) * 128,
                                                 tc * NTOK:(tc + 1) * NTOK], sg[:])
                ctx.barrier(rec)
                rec.flush()
    ctx.barrier(rec)
    rec.flush()


def p4c_down_out(rec, ctx, t, sb):
    nc = ctx.nc
    for tc2 in range(2):
        with ExitStack() as ph:
            a = lambda n, shape, dt=F32: ph.enter_context(
                nc.sbuf_tensor(f"p4c_{n}_{tc2}", shape, dt))
            yqc = a("yqc", [128, 64 * 512], BF16)
            wdq2 = [a(f"wdq{i}", [128, 64 * 128], BF16) for i in range(2)]
            syb = a("syb", [128, 512])
            d12 = [a(f"d1{i}", [128, 512]) for i in range(2)]
            g1c2 = [a(f"g1c{i}", [128, 512]) for i in range(2)]
            s1c2 = [a(f"s1c{i}", [128, 512]) for i in range(2)]
            xl1c2 = [a(f"xl1c{i}", [128, 512]) for i in range(2)]
            with ExitStack() as ps:
                pm2 = [ps.enter_context(nc.psum_tensor(f"p4c_pm{i}_{tc2}", [128, 512], F32))
                       for i in range(2)]
                ptr = ps.enter_context(nc.psum_tensor(f"p4c_ptr_{tc2}", [1, 128], F32))
                pbc = ps.enter_context(nc.psum_tensor(f"p4c_pbc_{tc2}", [128, 512], F32))
                rec.wait_cur("sync", "tns")
                rec.wait_cur("gpsimd", "tns")
                for it in range(64):
                    if it % 4 == 3:
                        rec.dma("gpsimd", yqc[:, it * 512:(it + 1) * 512],
                                t["s_yq1"][it * 128:(it + 1) * 128, tc2 * 512:(tc2 + 1) * 512],
                                sem="dmah")
                    else:
                        rec.dma("sync", yqc[:, it * 512:(it + 1) * 512],
                                t["s_yq1"][it * 128:(it + 1) * 128, tc2 * 512:(tc2 + 1) * 512],
                                sem=("dma", "dmb", "out")[it % 4])
                _row_bcast4(rec, ctx, sb, sb["dyq1"][:, tc2 * 4:tc2 * 4 + 4], ptr, pbc, syb)
                rec.wait("tensor", "dma", ctx.cnt["dma"])
                rec.wait("tensor", "dmb", ctx.cnt["dmb"])
                rec.wait("tensor", "out", ctx.cnt["out"])
                rec.wait("tensor", "dmah", ctx.cnt["dmah"])
                wdq_mm = [0, 0]
                pm_free = [0, 0]
                sx_used = [0, 0]
                d1_read = [0, 0]
                g1_ship = [0, 0]
                for o2 in range(FT):
                    pi = o2 % 2
                    wd = wdq2[pi]
                    rec.wait("sync", "tns", wdq_mm[pi])
                    for it in range(64):
                        rec.dma("sync", wd[:, it * 128:(it + 1) * 128],
                                t["s_wd1"][it * 128:(it + 1) * 128, o2 * 128:(o2 + 1) * 128],
                                sem=("dma", "dmb", "out")[it % 3])
                    rec.wait("sync", "vec", sx_used[pi])
                    rec.dma("sync", s1c2[pi][:], t["s_s1"][o2 * 128:(o2 + 1) * 128,
                                                           tc2 * 512:(tc2 + 1) * 512])
                    rec.dma("sync", xl1c2[pi][:],
                            t["s_xl1"][o2 * 128:(o2 + 1) * 128,
                                       512 + tc2 * 512:512 + (tc2 + 1) * 512],
                            sem="dmb")
                    rec.wait("tensor", "dma", ctx.cnt["dma"])
                    rec.wait("tensor", "dmb", ctx.cnt["dmb"])
                    rec.wait("tensor", "out", ctx.cnt["out"])
                    rec.wait("tensor", "vec", pm_free[pi])
                    for it in range(64):
                        rec.op("tensor", lambda eng, it=it, pm=pm2[pi], wd=wd: eng.matmul(
                            pm[:], wd[:, it * 128:(it + 1) * 128],
                            yqc[:, it * 512:(it + 1) * 512],
                            start=(it == 0), stop=(it == 63)))
                    mmv = rec.mark("tensor", "tns")
                    wdq_mm[pi] = mmv
                    rec.wait("vector", "tns", mmv)
                    rec.wait("vector", "act", d1_read[pi])
                    pm_free[pi] = rec.op("vector", lambda eng, pm=pm2[pi], d1=d12[pi]:
                                         eng.tensor_mul(d1[:], pm[:], syb[:]), inc="vec")
                    vv = ctx.cnt["vec"]
                    rec.wait("scalar", "vec", vv)
                    rec.wait("scalar", "dmah", g1_ship[pi])
                    d1_read[pi] = rec.op("scalar", lambda eng, d1=d12[pi], g1c=g1c2[pi]:
                                         eng.activation(g1c[:], d1[:], AF.Silu,
                                                        bias=sb["zero"][:],
                                                        scale=sb["deq_d1"][:]), inc="act")
                    av = ctx.cnt["act"]
                    rec.wait("vector", "act", av)
                    rec.wait("vector", "dmah", g1_ship[pi])
                    rec.op("vector", lambda eng, g1c=g1c2[pi], s1c=s1c2[pi]:
                           eng.tensor_mul(g1c[:], g1c[:], s1c[:]))
                    rec.fence("vector")
                    rec.op("vector", lambda eng, g1c=g1c2[pi], xl1c=xl1c2[pi]:
                           eng.tensor_add(g1c[:], g1c[:], xl1c[:]), inc="vec")
                    vv = ctx.cnt["vec"]
                    sx_used[pi] = vv
                    rec.wait("gpsimd", "vec", vv)
                    g1_ship[pi] = rec.dma("gpsimd",
                                          t["yout"][o2 * 128:(o2 + 1) * 128,
                                                    tc2 * 512:(tc2 + 1) * 512],
                                          g1c2[pi][:], sem="dmah")
                    rec.wait("gpsimd", "dmah", g1_ship[pi])
                ctx.barrier(rec)
                rec.flush()
    rec.wait("gpsimd", "dmah", ctx.cnt["dmah"])
    ctx.barrier(rec)
    rec.flush()


def _dump_dram(rec, ctx, t, dbg_name, src_ap, rows, cols, dt):
    nc = ctx.nc
    with nc.sbuf_tensor(f"dumpbuf_{dbg_name}", [128, cols], dt) as buf:
        for rt in range(rows // 128):
            dv = rec.dma("sync", buf[:], src_ap[rt * 128:(rt + 1) * 128, :])
            rec.wait("gpsimd", "dma", dv)
            dv2 = rec.dma("gpsimd", t[dbg_name][rt * 128:(rt + 1) * 128, :], buf[:])
            rec.wait("sync", "dmag", dv2)
        rec.wait("gpsimd", "dmag", ctx.cnt["dmag"])
        rec.flush()


def _dbg_dump(rec, ctx, t, pairs):
    for name, ap in pairs:
        rec.dma("gpsimd", t["dbg_" + name], ap)
    dv = ctx.cnt["dmag"]
    rec.wait("gpsimd", "dmag", dv)
    rec.flush()


def build(debug=None, stop_after=None, p1_rounds=4):
    nc = bass.Bass("TRN2", target_bir_lowering=False)
    t = declare_io(nc)
    if debug:
        for name, shape, dt in debug:
            t["dbg_" + name] = nc.dram_tensor("dbg_" + name, shape, dt,
                                              kind="ExternalOutput").ap()
    ctx = Ctx(nc)
    sb = alloc_persistent(ctx, nc)
    rec = Rec(ctx)
    p0_consts(rec, ctx, t, sb)
    ctx.barrier(rec)
    rec.flush()
    if stop_after == "p0":
        _dbg_dump(rec, ctx, t, [("canary", sb["canary"][:]), ("magicM", sb["magicM"][:]),
                                ("can1", sb["can1"][:]), ("m64t", sb["m64t"][:]),
                                ("m32t", sb["m32t"][:]), ("c075", sb["c075"][:])])
        ctx.es.close()
        return nc
    if stop_after == "ln":
        with ExitStack() as ph:
            a = lambda n, shape, dt=F32: ph.enter_context(nc.sbuf_tensor("lnt_" + n, shape, dt))
            bufs = {
                "xch": a("xch", [128, FT * 128]),
                "xT": a("xT", [128, D]), "xnT": a("xnT", [128, D]),
                "stats": a("stats", [128, 24]), "mv": a("mv", [128, 2]),
                "rr": a("rr", [128, 1]), "b2": a("b2", [128, 1]),
                "tmp1": a("tmp1", [128, 1024]), "aT": a("aT", [128, D]),
                "amax": a("amax", [128, 1]), "sc": a("sc", [128, 1]),
                "rin": a("rin", [128, 1]),
                "pb0": ph.enter_context(nc.psum_tensor("lnt_pb0", [128, 512], F32)),
                "pb1": ph.enter_context(nc.psum_tensor("lnt_pb1", [128, 512], F32)),
            }
            xnh = a("xnh", [128, FT * 128], BF16)
            xnl = a("xnl", [128, FT * 128], BF16)
            qsl = a("qsl", [128, FT * 128], BF16)
            deq = a("deq", [128, 1])
            ln_quant_chunk(rec, ctx, sb, bufs, t["xatt"], 0, 128, "split",
                           out={"hi": xnh, "lo": xnl,
                                "quant": {"kind": "l0", "q_slab": qsl, "deq_cols": deq}})
            ctx.barrier(rec)
            hi32 = a("hi32", [128, FT * 128])
            lo32 = a("lo32", [128, FT * 128])
            q32 = a("q32", [128, FT * 128])
            rec.op("vector", lambda eng: eng.tensor_copy(hi32[:], xnh[:]))
            rec.op("vector", lambda eng: eng.tensor_copy(lo32[:], xnl[:]))
            rec.op("vector", lambda eng: eng.tensor_copy(q32[:], qsl[:]), inc="vec")
            rec.fence("vector")
            _dbg_dump(rec, ctx, t, [("xT", bufs["xT"][:]), ("xnT", bufs["xnT"][:]),
                                    ("mv", bufs["mv"][:]), ("hi32", hi32[:]),
                                    ("lo32", lo32[:]), ("q32", q32[:]), ("deq", deq[:]),
                                    ("stats", bufs["stats"][:])])
        ctx.es.close()
        return nc
    p1_l0_attention(rec, ctx, t, sb, rounds=p1_rounds)
    if stop_after == "p1":
        _dbg_dump(rec, ctx, t, [("csum", sb["csum"][:])])
        ctx.es.close()
        return nc
    p2a_quant_w0(rec, ctx, t, sb)
    p2b_quant_w1(rec, ctx, t, sb)
    p2c_values_xq(rec, ctx, t, sb)
    p2d_ffn0(rec, ctx, t, sb)
    p2e_outproj_xl1(rec, ctx, t, sb)
    if stop_after == "p2":
        _dump_dram(rec, ctx, t, "dbg_xl1", t["s_xl1"], D, TF, F32)
        if "dbg_g0" in t:
            _dump_dram(rec, ctx, t, "dbg_g0", t["s_g0"], D, TF, F32)
        if "dbg_vh" in t:
            _dump_dram(rec, ctx, t, "dbg_vh", t["s_vh"], D, TF, BF16)
        if "dbg_vl" in t:
            _dump_dram(rec, ctx, t, "dbg_vl", t["s_vl"], D, TF, BF16)
        if "dbg_xq" in t:
            _dump_dram(rec, ctx, t, "dbg_xq", t["s_xq0"], D, TF, BF16)
        ctx.es.close()
        return nc
    p3a_l1_proj(rec, ctx, t, sb)
    p3b_l1_attn(rec, ctx, t, sb)
    if stop_after == "p3":
        _dump_dram(rec, ctx, t, "dbg_s1", t["s_s1"], D, TOWN, F32)
        ctx.es.close()
        return nc
    p4a_ffn1_gateup(rec, ctx, t, sb)
    p4b_fwht_quant(rec, ctx, t, sb)
    p4c_down_out(rec, ctx, t, sb)
    ctx.es.close()
    return nc


def kernel(**inputs):
    in_maps = _stage_inputs(inputs)
    nc = build()
    res = run_bass_kernel_spmd(nc, in_maps, core_ids=list(range(8)))
    out = np.empty((4, 2048, 2048), dtype=np.float32)
    for c in range(8):
        b, r = c // 2, c % 2
        y = np.asarray(res.results[c]["yout"])
        out[b, r * 1024:(r + 1) * 1024, :] = y.T
    return out


# version 07
# speedup vs baseline: 1.2384x; 1.1141x over previous
import numpy as np
from contextlib import ExitStack

import concourse.bass as bass
import concourse.mybir as mybir
from concourse.bass_utils import run_bass_kernel_spmd

F32 = mybir.dt.float32
BF16 = mybir.dt.bfloat16
AF = mybir.ActivationFunctionType
ALU = mybir.AluOpType
AX = mybir.AxisListType

D = 2048
FT = D // 128
NH = 16
HD = 128
TATT = 2048
TF = 1536
TOWN = 1024
I0 = 5632
I1 = 8192
C_SM = 8.0
EPS_LN = 1e-5
M32C = 12582912.0
M64C = float(np.float32(1.5 * 2 ** 52))
SC_ATT = float(1.0 / np.sqrt(np.float32(128.0)))

ENGINES = ("sync", "tensor", "vector", "scalar", "gpsimd")



class Rec:
    def __init__(self, ctx):
        self.ctx = ctx
        self.steps = {en: [] for en in ENGINES}
        self._drain = {}

    def op(self, en, fn, inc=None, inc_by=None):
        ctx = self.ctx
        entry = {"fn": fn, "incs": []}
        self.steps[en].append(entry)
        if inc is not None:
            v = inc_by if inc_by is not None else (16 if inc in ("dma", "dmb", "dmc", "dmag", "dmah", "out") else 1)
            ctx.cnt[inc] += v
            entry["incs"].append((inc, v))
            return ctx.cnt[inc]
        return None

    def wait(self, en, sem, val):
        h = self.ctx.wait_hwm
        if val > h.get(sem, 0):
            h[sem] = val
        self.steps[en].append({"wait": (sem, val)})

    def wait_cur(self, en, sem):
        self.wait(en, sem, self.ctx.cnt[sem])

    def _last_op(self, en):
        for e in reversed(self.steps[en]):
            if "fn" in e and not e.get("shim"):
                return e
        return None

    def mark(self, en, sem):
        ctx = self.ctx
        e = self._last_op(en)
        assert e is not None, f"mark({en}) with no prior instruction"
        if e["incs"]:
            s0, v0 = e["incs"][0]
            assert s0 == sem, f"last {en} op already incs {s0}, wanted {sem}"
            return ctx.cnt[s0]
        v = 16 if sem in ("dma", "dmb", "dmc", "dmag", "dmah", "out") else 1
        ctx.cnt[sem] += v
        e["incs"].append((sem, v))
        return ctx.cnt[sem]

    def sem_inc_raw(self, en, sem, v=1):
        self.ctx.cnt[sem] += v
        val = self.ctx.cnt[sem]
        self.steps[en].append({"fn": lambda eng, s=self.ctx.sem[sem], v=v: _SemIncShim(eng, s, v),
                               "incs": [], "shim": True})
        return val

    def fence(self, en, sem=None):
        e = self._last_op(en)
        assert e is not None, f"fence({en}) with no prior instruction"
        if e["incs"]:
            s0, _ = e["incs"][0]
            v = self.ctx.cnt[s0]
            self.wait(en, s0, v)
            return v
        sem = sem or {"vector": "vec", "scalar": "act", "gpsimd": "gps",
                      "tensor": "tns", "sync": "ph"}[en]
        v = self.mark(en, sem)
        self.wait(en, sem, v)
        return v

    def dma(self, q, out, in_, sem=None):
        if sem is None:
            sem = "dma" if q == "sync" else "dmag"
        key = (q, sem)
        hwm = self.ctx.wait_hwm.get(sem, 0)
        if hwm > self._drain.get(key, 0):
            self.steps[q].append({"wait": (sem, hwm)})
            self._drain[key] = hwm
        v = self.op(q, lambda eng, o=out, i=in_: eng.dma_start(out=o, in_=i), inc=sem)
        if v - self._drain.get(key, 0) >= 12 * 16:
            self.wait(q, sem, v)
            self._drain[key] = v
        return v

    def flush(self):
        nc = self.ctx.nc
        ctx = self.ctx

        def runner(eng, entries):
            for e in entries:
                if "wait" in e:
                    sem, val = e["wait"]
                    eng.wait_ge(ctx.sem[sem], val)
                else:
                    inst = e["fn"](eng)
                    for sem, v in e["incs"]:
                        inst.then_inc(ctx.sem[sem], v)
        with nc.Block() as block:
            for en in ENGINES:
                entries = self.steps[en]
                if entries:
                    getattr(block, en)(lambda eng, entries=entries: runner(eng, entries))
        self.steps = {en: [] for en in ENGINES}


class Ctx:
    def __init__(self, nc):
        self.nc = nc
        self.es = ExitStack()
        self.sem = {}
        self.cnt = {}
        self.wait_hwm = {}
        for name in ["dma", "dmb", "dmc", "dmag", "dmah", "tns", "vec", "act", "gps", "ph", "out"]:
            self.sem[name] = self.es.enter_context(nc.semaphore(name))
            self.cnt[name] = 0

    def barrier(self, rec):
        for en in ENGINES:
            if rec._last_op(en) is not None:
                rec.fence(en)
        for en, sems in (("sync", ("dma", "dmb", "dmc", "out")),
                         ("gpsimd", ("dmag", "dmah"))):
            for s in sems:
                if self.cnt[s] > 0:
                    rec.wait(en, s, self.cnt[s])
        for en in ENGINES:
            rec.sem_inc_raw(en, "ph", 1)
        tgt = self.cnt["ph"]
        for en in ENGINES:
            rec.wait(en, "ph", tgt)


class _SemIncShim:
    def __init__(self, eng, sem, v):
        eng.sem_inc(sem, v)

    def then_inc(self, sem, v):
        raise RuntimeError("raw sem_inc cannot carry further incs")



def _stage_inputs(inputs):
    import ml_dtypes
    x = np.asarray(inputs["L0_attn_x"], dtype=np.float32)
    w_qkv = np.ascontiguousarray(np.asarray(inputs["L0_attn_w_qkv"], dtype=np.float32))
    w_o = np.ascontiguousarray(np.asarray(inputs["L0_attn_w_o"], dtype=np.float32))
    wgu0 = np.ascontiguousarray(np.asarray(inputs["L0_ffn_w_gate_up"], dtype=np.float32))
    wd0 = np.ascontiguousarray(np.asarray(inputs["L0_ffn_w_down"], dtype=np.float32))
    l1wq = np.ascontiguousarray(np.asarray(inputs["L1_attn_Wq"], dtype=np.float32).T)
    l1wk = np.ascontiguousarray(np.asarray(inputs["L1_attn_Wk"], dtype=np.float32).T)
    l1wv = np.ascontiguousarray(np.asarray(inputs["L1_attn_Wv"], dtype=np.float32).T)
    l1wo = np.ascontiguousarray(np.asarray(inputs["L1_attn_Wo"], dtype=np.float32).T)
    l1wg = np.ascontiguousarray(np.asarray(inputs["L1_ffn_w_gate"], dtype=np.float32).T)
    l1wu = np.ascontiguousarray(np.asarray(inputs["L1_ffn_w_up"], dtype=np.float32).T)
    l1wd = np.ascontiguousarray(np.asarray(inputs["L1_ffn_w_down"], dtype=np.float32).T)

    ident = np.eye(128, dtype=np.float32)
    onescol = np.ones((128, 1), dtype=np.float32)
    onesrow = np.ones((1, 128), dtype=np.float32)
    H = np.array([[1.0]], dtype=np.float32)
    while H.shape[0] < 128:
        H = np.block([[H, H], [H, -H]])
    H128 = np.ascontiguousarray(H.astype(np.float32))

    inv_freq = 1.0 / (10000.0 ** (np.arange(0, HD, 2, dtype=np.float32) / HD))

    def rope_tables(positions):
        freqs = positions[None, :].astype(np.float32) * inv_freq[:, None]
        c = np.cos(freqs).astype(np.float32)
        s_ = np.sin(freqs).astype(np.float32)
        ct = np.empty((HD, len(positions)), dtype=np.float32)
        st = np.empty((HD, len(positions)), dtype=np.float32)
        ct[0::2] = c
        ct[1::2] = c
        st[0::2] = -s_
        st[1::2] = s_
        return np.ascontiguousarray(ct), np.ascontiguousarray(st)

    jj = np.arange(1024)[:, None]
    qq = np.arange(512)[None, :] + 512
    base_valid = (jj > qq - 512) & (jj <= qq)
    mask_full = np.where(base_valid, 0.0, -1e30).astype(np.float32)
    mask_pad = mask_full.copy()
    mask_pad[0:512, :] = -1e30
    mask_full16 = np.ascontiguousarray(mask_full.astype(ml_dtypes.bfloat16))
    mask_pad16 = np.ascontiguousarray(mask_pad.astype(ml_dtypes.bfloat16))

    in_maps = []
    for c in range(8):
        b, r = c // 2, c % 2
        xs = np.ascontiguousarray(x[b].T)
        xatt = np.ascontiguousarray(np.roll(xs, 1024, axis=1)) if r == 0 else xs
        own0 = 0 if r == 0 else 1024
        qpos = np.arange(own0, own0 + TOWN)
        if r == 1:
            kpos = np.arange(512, 2048)
        else:
            kpos = np.concatenate([np.arange(1536, 2048), np.arange(0, 1024)])
        cq, sq = rope_tables(qpos)
        ck, sk = rope_tables(kpos)
        mA = mask_pad16 if r == 0 else mask_full16
        in_maps.append({
            "xatt": xatt,
            "wqkv": w_qkv, "wo0": w_o, "wgu0": wgu0, "wd0": wd0,
            "l1wq": l1wq, "l1wk": l1wk, "l1wv": l1wv, "l1wo": l1wo,
            "l1wg": l1wg, "l1wu": l1wu, "l1wd": l1wd,
            "ident": ident, "onescol": onescol, "onesrow": onesrow, "h128": H128,
            "ropeq_c": cq, "ropeq_s": sq, "ropek_c": ck, "ropek_s": sk,
            "maskA": mA, "maskB": mask_full16,
        })
    return in_maps


def declare_io(nc):
    t = {}
    t["xatt"] = nc.dram_tensor("xatt", [D, TATT], F32, kind="ExternalInput").ap()
    t["wqkv"] = nc.dram_tensor("wqkv", [D, 3 * D], F32, kind="ExternalInput").ap()
    t["wo0"] = nc.dram_tensor("wo0", [D, D], F32, kind="ExternalInput").ap()
    t["wgu0"] = nc.dram_tensor("wgu0", [2 * I0, D], F32, kind="ExternalInput").ap()
    t["wd0"] = nc.dram_tensor("wd0", [D, I0], F32, kind="ExternalInput").ap()
    for n in ["l1wq", "l1wk", "l1wv", "l1wo"]:
        t[n] = nc.dram_tensor(n, [D, D], F32, kind="ExternalInput").ap()
    t["l1wg"] = nc.dram_tensor("l1wg", [D, I1], F32, kind="ExternalInput").ap()
    t["l1wu"] = nc.dram_tensor("l1wu", [D, I1], F32, kind="ExternalInput").ap()
    t["l1wd"] = nc.dram_tensor("l1wd", [I1, D], F32, kind="ExternalInput").ap()
    t["ident"] = nc.dram_tensor("ident", [128, 128], F32, kind="ExternalInput").ap()
    t["onescol"] = nc.dram_tensor("onescol", [128, 1], F32, kind="ExternalInput").ap()
    t["onesrow"] = nc.dram_tensor("onesrow", [1, 128], F32, kind="ExternalInput").ap()
    t["h128"] = nc.dram_tensor("h128", [128, 128], F32, kind="ExternalInput").ap()
    t["ropeq_c"] = nc.dram_tensor("ropeq_c", [HD, TOWN], F32, kind="ExternalInput").ap()
    t["ropeq_s"] = nc.dram_tensor("ropeq_s", [HD, TOWN], F32, kind="ExternalInput").ap()
    t["ropek_c"] = nc.dram_tensor("ropek_c", [HD, TF], F32, kind="ExternalInput").ap()
    t["ropek_s"] = nc.dram_tensor("ropek_s", [HD, TF], F32, kind="ExternalInput").ap()
    t["maskA"] = nc.dram_tensor("maskA", [1024, 512], BF16, kind="ExternalInput").ap()
    t["maskB"] = nc.dram_tensor("maskB", [1024, 512], BF16, kind="ExternalInput").ap()
    t["yout"] = nc.dram_tensor("yout", [D, TOWN], F32, kind="ExternalOutput").ap()
    t["s_wq0"] = nc.dram_tensor("s_wq0", [D, 2 * I0], BF16).ap()
    t["s_wd0"] = nc.dram_tensor("s_wd0", [I0, D], BF16).ap()
    t["s_wg1"] = nc.dram_tensor("s_wg1", [D, I1], BF16).ap()
    t["s_wu1"] = nc.dram_tensor("s_wu1", [D, I1], BF16).ap()
    t["s_wd1"] = nc.dram_tensor("s_wd1", [I1, D], BF16).ap()
    t["s_xq0"] = nc.dram_tensor("s_xq0", [D, TF], BF16).ap()
    t["s_xnh"] = nc.dram_tensor("s_xnh", [D, TATT], BF16).ap()
    t["s_xnl"] = nc.dram_tensor("s_xnl", [D, TATT], BF16).ap()
    t["s_vh"] = nc.dram_tensor("s_vh", [D, TF], BF16).ap()
    t["s_vl"] = nc.dram_tensor("s_vl", [D, TF], BF16).ap()
    t["s_g0"] = nc.dram_tensor("s_g0", [D, TF], F32).ap()
    t["s_xl1"] = nc.dram_tensor("s_xl1", [D, TF], F32).ap()
    t["s_krh"] = nc.dram_tensor("s_krh", [D, TF], BF16).ap()
    t["s_krl"] = nc.dram_tensor("s_krl", [D, TF], BF16).ap()
    t["s_vT"] = nc.dram_tensor("s_vT", [TF, D], F32).ap()
    t["s_s1"] = nc.dram_tensor("s_s1", [D, TOWN], F32).ap()
    t["s_xq1"] = nc.dram_tensor("s_xq1", [D, TOWN], BF16).ap()
    t["s_qrh"] = nc.dram_tensor("s_qrh", [D, TOWN], BF16).ap()
    t["s_qrl"] = nc.dram_tensor("s_qrl", [D, TOWN], BF16).ap()
    t["s_h1"] = nc.dram_tensor("s_h1", [I1, TOWN], F32).ap()
    t["s_yq1"] = nc.dram_tensor("s_yq1", [I1, TOWN], BF16).ap()
    return t




def alloc_persistent(ctx, nc):
    sb = {}
    a = lambda n, shape, dt=F32: ctx.es.enter_context(nc.sbuf_tensor(n, shape, dt))
    sb["ident"] = a("c_ident", [128, 128])
    sb["onescol"] = a("c_onescol", [128, 1])
    sb["onesrow"] = a("c_onesrow", [1, 128])
    sb["h128f"] = a("c_h128f", [128, 128])
    sb["h128b"] = a("c_h128b", [128, 128], BF16)
    sb["identb"] = a("c_identb", [128, 128], BF16)
    for n, v in [("c075", 0.75), ("negC", -C_SM), ("epsln", EPS_LN), ("zero", 0.0),
                 ("m64t", M64C), ("m32t", M32C)]:
        sb[n] = a("c_" + n, [128, 1])
        sb["_val_" + n] = v
    sb["canary"] = a("c_canary", [128, 1])
    sb["can1"] = a("c_can1", [128, 1])
    sb["magicM"] = a("c_magicM", [128, 1])
    sb["negM"] = a("c_negM", [128, 1])
    sb["csum"] = a("g_csum", [128, NH * 12])
    sb["sgu0"] = a("g_sgu0", [128, 88])
    sb["sd0"] = a("g_sd0", [128, 16])
    for n in ["deq_g1", "deq_u1", "deq_d1"]:
        sb[n] = a("g_" + n, [128, 1])
    for n in ["thp_g1", "thn_g1", "thp_u1", "thn_u1", "thp_d1", "thn_d1"]:
        sb[n] = a("g_" + n, [128, 1])
    sb["dxq0"] = a("g_dxq0", [128, 12])
    sb["dxq1"] = a("g_dxq1", [128, 8])
    sb["dyq1"] = a("g_dyq1", [128, 8])
    sb["csrow"] = a("g_csrow", [1, 128])
    return sb


def p0_consts(rec, ctx, t, sb):
    for name in ["ident", "onescol", "onesrow", "h128f"]:
        src = {"h128f": "h128"}.get(name, name)
        rec.dma("sync", sb[name][:], t[src])
    dmadone = ctx.cnt["dma"]
    for n in ["c075", "negC", "epsln", "zero", "m64t", "m32t"]:
        rec.op("gpsimd", lambda eng, ap=sb[n][:], v=sb["_val_" + n]: eng.memset(ap, v),
               inc="gps")
    gv = ctx.cnt["gps"]
    rec.wait("vector", "gps", gv)
    rec.wait("vector", "dma", dmadone)
    rec.op("vector", lambda eng: eng.tensor_add(sb["can1"][:], sb["c075"][:], sb["m64t"][:]))
    rec.fence("vector")
    rec.op("vector", lambda eng: eng.tensor_sub(sb["canary"][:], sb["can1"][:], sb["m64t"][:]))
    rec.fence("vector")
    rec.op("vector", lambda eng: eng.tensor_mul(sb["magicM"][:], sb["canary"][:], sb["m64t"][:]))
    rec.fence("vector")
    rec.op("vector", lambda eng: eng.tensor_add(sb["magicM"][:], sb["magicM"][:], sb["m32t"][:]))
    rec.fence("vector")
    rec.op("vector", lambda eng: eng.tensor_sub(sb["negM"][:], sb["zero"][:], sb["magicM"][:]))
    rec.op("vector", lambda eng: eng.tensor_copy(sb["h128b"][:], sb["h128f"][:]))
    rec.op("vector", lambda eng: eng.tensor_copy(sb["identb"][:], sb["ident"][:]), inc="vec")


def mm3(rec, out_ap, lT_hi, lT_lo, r_hi, r_lo, start, stop):
    rec.op("tensor", lambda eng: eng.matmul(out_ap, lT_hi, r_hi, start=start, stop=False))
    rec.op("tensor", lambda eng: eng.matmul(out_ap, lT_hi, r_lo, start=False, stop=False))
    rec.op("tensor", lambda eng: eng.matmul(out_ap, lT_lo, r_hi, start=False, stop=stop))


def ln_quant_chunk(rec, ctx, sb, bufs, src_hbm, col0, TC, mode, out=None):
    nsub = TC // 128
    for s in range(nsub):
        rec.wait_cur("sync", "tns")
        for ft in range(FT):
            rec.dma("sync", bufs["xch"][:, ft * 128:(ft + 1) * 128],
                    src_hbm[ft * 128:(ft + 1) * 128, col0 + s * 128:col0 + s * 128 + 128],
                    sem="dmc")
        dma_done = ctx.cnt["dmc"]
        rec.wait("tensor", "dmc", dma_done)
        for ft in range(FT):
            pb = bufs["pb0"] if ft % 2 == 0 else bufs["pb1"]
            rec.wait_cur("tensor", "vec")
            rec.op("tensor", lambda eng, pb=pb, ft=ft: eng.transpose(
                pb[:, 0:128], bufs["xch"][:, ft * 128:(ft + 1) * 128],
                sb["ident"][:]), inc="tns")
            rec.wait_cur("vector", "tns")
            rec.op("vector", lambda eng, pb=pb, ft=ft: eng.tensor_copy(
                bufs["xT"][:, ft * 128:(ft + 1) * 128], pb[:, 0:128]), inc="vec")
        rec.fence("vector")
        for i in range(4):
            rec.op("vector", lambda eng, i=i: eng.bn_stats(
                bufs["stats"][:, 6 * i:6 * i + 6], bufs["xT"][:, 512 * i:512 * (i + 1)]))
        rec.fence("vector")
        rec.op("vector", lambda eng: eng.bn_aggr(bufs["mv"][:], bufs["stats"][:]), inc="vec")
        rec.fence("vector")
        rec.wait_cur("scalar", "vec")
        rec.op("scalar", lambda eng: eng.activation(
            bufs["rr"][:], bufs["mv"][:, 1:2], AF.Sqrt, bias=sb["epsln"][:], scale=1.0), inc="act")
        rec.wait_cur("vector", "act")
        rec.op("vector", lambda eng: eng.reciprocal(bufs["rr"][:], bufs["rr"][:]))
        rec.fence("vector")
        rec.op("vector", lambda eng: eng.tensor_mul(bufs["b2"][:], bufs["mv"][:, 0:1], bufs["rr"][:]))
        rec.fence("vector")
        rec.op("vector", lambda eng: eng.tensor_single_scalar(
            bufs["b2"][:], bufs["b2"][:], -1.0, ALU.mult), inc="vec")
        rec.wait_cur("scalar", "vec")
        rec.op("scalar", lambda eng: eng.activation(
            bufs["xnT"][:], bufs["xT"][:], AF.Identity, bias=bufs["b2"][:],
            scale=bufs["rr"][:]), inc="act")
        act_ln = ctx.cnt["act"]
        if out is not None and "hi" in out:
            rec.wait("tensor", "act", act_ln)
            for ft in range(FT):
                pb = bufs["pb0"] if ft % 2 == 0 else bufs["pb1"]
                rec.wait_cur("tensor", "vec")
                rec.op("tensor", lambda eng, pb=pb, ft=ft: eng.transpose(
                    pb[:, 0:128], bufs["xnT"][:, ft * 128:(ft + 1) * 128],
                    sb["ident"][:]), inc="tns")
                rec.wait_cur("vector", "tns")
                co = ft * TC + s * 128
                hi = out["hi"][:, co:co + 128]
                lo = out["lo"][:, co:co + 128]
                tmp = bufs["tmp1"][:, 0:128]
                rec.op("vector", lambda eng, pb=pb, hi=hi: eng.tensor_copy(hi, pb[:, 0:128]))
                rec.fence("vector")
                rec.op("vector", lambda eng, pb=pb, hi=hi, tmp=tmp: eng.tensor_sub(tmp, pb[:, 0:128], hi))
                rec.fence("vector")
                rec.op("vector", lambda eng, lo=lo, tmp=tmp: eng.tensor_copy(lo, tmp), inc="vec")
        if out is not None and "quant" in out:
            q = out["quant"]
            rec.wait("vector", "act", act_ln)
            rec.op("vector", lambda eng: eng.reduce_max(
                bufs["amax"][:], bufs["xnT"][:], AX.X, apply_absolute_value=True))
            rec.fence("vector")
            if q["kind"] == "l0":
                rec.op("vector", lambda eng: eng.tensor_scalar(
                    bufs["sc"][:], bufs["amax"][:], 1.0 / 127.0, 1e-8, ALU.mult, ALU.max))
                rec.fence("vector")
                rec.op("vector", lambda eng: eng.reciprocal(bufs["rin"][:], bufs["sc"][:]), inc="vec")
                deq_src = bufs["sc"]
            else:
                rec.op("vector", lambda eng: eng.tensor_single_scalar(
                    bufs["sc"][:], bufs["amax"][:], 1e-5, ALU.max))
                rec.fence("vector")
                rec.op("vector", lambda eng: eng.reciprocal(bufs["rin"][:], bufs["sc"][:]))
                rec.fence("vector")
                rec.op("vector", lambda eng: eng.tensor_single_scalar(
                    bufs["rin"][:], bufs["rin"][:], 128.0, ALU.mult))
                rec.op("vector", lambda eng: eng.tensor_single_scalar(
                    bufs["sc"][:], bufs["sc"][:], 1.0 / 128.0, ALU.mult), inc="vec")
                deq_src = bufs["sc"]
            rec.fence("vector")
            rec.op("vector", lambda eng, s=s: eng.tensor_copy(
                q["deq_cols"][:, s:s + 1], deq_src[:]))
            rec.wait_cur("scalar", "vec")
            rec.op("scalar", lambda eng: eng.activation(
                bufs["aT"][:], bufs["xnT"][:], AF.Identity, bias=sb["magicM"][:],
                scale=bufs["rin"][:]), inc="act")
            rec.wait_cur("vector", "act")
            rec.op("vector", lambda eng: eng.tensor_single_scalar(
                bufs["aT"][:], bufs["aT"][:], sb["magicM"][:], ALU.subtract))
            if q["kind"] == "l1":
                rec.fence("vector")
                rec.op("vector", lambda eng: eng.tensor_scalar(
                    bufs["aT"][:], bufs["aT"][:], 127.0, -128.0, ALU.min, ALU.max))
            qv = rec.mark("vector", "vec")
            rec.wait("tensor", "vec", qv)
            for ft in range(FT):
                pb = bufs["pb0"] if ft % 2 == 0 else bufs["pb1"]
                rec.wait_cur("tensor", "vec")
                rec.op("tensor", lambda eng, pb=pb, ft=ft: eng.transpose(
                    pb[:, 0:128], bufs["aT"][:, ft * 128:(ft + 1) * 128],
                    sb["ident"][:]), inc="tns")
                rec.wait_cur("vector", "tns")
                if "dma_to" in q:
                    dst, colb = q["dma_to"]
                    stg = q["stage"][ft % 2]
                    rec.wait_cur("vector", "dmag")
                    rec.op("vector", lambda eng, pb=pb, stg=stg: eng.tensor_copy(
                        stg[:], pb[:, 0:128]), inc="vec")
                    vv = ctx.cnt["vec"]
                    rec.wait("gpsimd", "vec", vv)
                    rec.dma("gpsimd", dst[ft * 128:(ft + 1) * 128,
                                          colb + s * 128:colb + s * 128 + 128], stg[:])
                else:
                    co = ft * TC + s * 128
                    rec.op("vector", lambda eng, pb=pb, co=co: eng.tensor_copy(
                        q["q_slab"][:, co:co + 128], pb[:, 0:128]), inc="vec")


def p0b_ln0(rec, ctx, t, sb):
    nc = ctx.nc
    with ExitStack() as ph:
        a = lambda n, shape, dt=F32: ph.enter_context(nc.sbuf_tensor("p0b_" + n, shape, dt))
        bufs = {
            "xch": a("xch", [128, FT * 128]),
            "xT": a("xT", [128, D]), "xnT": a("xnT", [128, D]),
            "stats": a("stats", [128, 24]), "mv": a("mv", [128, 2]),
            "rr": a("rr", [128, 1]), "b2": a("b2", [128, 1]),
            "tmp1": a("tmp1", [128, 1024]),
            "amax": a("amax", [128, 1]), "sc": a("sc", [128, 1]),
            "rin": a("rin", [128, 1]),
        }
        bufs["aT"] = bufs["xT"]
        qstg = [a("qstg0", [128, 128], BF16), a("qstg1", [128, 128], BF16)]
        xn2 = [(a(f"xnh{i}", [128, FT * 512], BF16), a(f"xnl{i}", [128, FT * 512], BF16))
               for i in range(2)]
        with ExitStack() as ps:
            bufs["pb0"] = ps.enter_context(nc.psum_tensor("p0b_pb0", [128, 512], F32))
            bufs["pb1"] = ps.enter_context(nc.psum_tensor("p0b_pb1", [128, 512], F32))
            ship_done = [(0, 0), (0, 0)]
            for tc in range(4):
                pi = tc % 2
                xnh, xnl = xn2[pi]
                rec.wait("vector", "dma", ship_done[pi][0])
                rec.wait("vector", "dmb", ship_done[pi][1])
                out = {"hi": xnh, "lo": xnl}
                if tc >= 1:
                    out["quant"] = {"kind": "l0", "dma_to": (t["s_xq0"], (tc - 1) * 512),
                                    "stage": qstg,
                                    "deq_cols": sb["dxq0"][:, (tc - 1) * 4:(tc - 1) * 4 + 4]}
                ln_quant_chunk(rec, ctx, sb, bufs, t["xatt"], tc * 512, 512, "split",
                               out=out)
                rec.wait_cur("sync", "vec")
                for ft in range(FT):
                    rec.dma("sync", t["s_xnh"][ft * 128:(ft + 1) * 128,
                                               tc * 512:(tc + 1) * 512],
                            xnh[:, ft * 512:(ft + 1) * 512],
                            sem=("dma", "dmb")[ft % 2])
                    rec.dma("sync", t["s_xnl"][ft * 128:(ft + 1) * 128,
                                               tc * 512:(tc + 1) * 512],
                            xnl[:, ft * 512:(ft + 1) * 512],
                            sem=("dma", "dmb")[ft % 2])
                ship_done[pi] = (ctx.cnt["dma"], ctx.cnt["dmb"])
                rec.wait("sync", "dma", ship_done[pi][0])
                rec.wait("sync", "dmb", ship_done[pi][1])
    ctx.barrier(rec)
    rec.flush()


def p1_l0_attention(rec, ctx, t, sb, rounds=4):
    nc = ctx.nc
    NQH = 4
    for rnd in range(rounds):
        with ExitStack() as ph:
            a = lambda n, shape, dt=F32: ph.enter_context(
                nc.sbuf_tensor(f"p1_{n}_{rnd}", shape, dt))
            kh = a("kh", [128, NQH * TATT], BF16)
            kl = a("kl", [128, NQH * TATT], BF16)
            qh = a("qh", [128, NQH * TATT], BF16)
            ql = a("ql", [128, NQH * TATT], BF16)
            bufs = {"tmp1": a("tmp1", [128, 1024])}
            with ExitStack() as ps:
                pj = [ps.enter_context(nc.psum_tensor(f"p1_pj{i}_{rnd}", [128, 512], F32))
                      for i in range(8)]
                wh = ps.enter_context(nc.sbuf_tensor(f"p1_wh_{rnd}", [128, FT * 1024], BF16))
                wl = ps.enter_context(nc.sbuf_tensor(f"p1_wl_{rnd}", [128, FT * 1024], BF16))
                wtmp = ps.enter_context(nc.sbuf_tensor(f"p1_wt_{rnd}", [128, 1024], F32))
                xn2 = [(ps.enter_context(nc.sbuf_tensor(f"p1_xnh{i}_{rnd}", [128, FT * 512], BF16)),
                        ps.enter_context(nc.sbuf_tensor(f"p1_xnl{i}_{rnd}", [128, FT * 512], BF16)))
                       for i in range(2)]
                for ft in range(FT):
                    wsem = ("dma", "dmb")[ft % 2]
                    rec.wait_cur("sync", "vec")
                    rec.dma("sync", wtmp[:, 0:512],
                            t["wqkv"][ft * 128:(ft + 1) * 128, rnd * 512:rnd * 512 + 512],
                            sem=wsem)
                    rec.dma("sync", wtmp[:, 512:1024],
                            t["wqkv"][ft * 128:(ft + 1) * 128, 2048 + rnd * 512:2048 + rnd * 512 + 512],
                            sem=wsem)
                    rec.wait_cur("vector", wsem)
                    co = ft * 1024
                    rec.op("vector", lambda eng, co=co: eng.tensor_copy(wh[:, co:co + 1024], wtmp[:]))
                    rec.fence("vector")
                    rec.op("vector", lambda eng, co=co: eng.tensor_copy(
                        bufs["tmp1"][:, 0:1024], wh[:, co:co + 1024]))
                    rec.fence("vector")
                    rec.op("vector", lambda eng: eng.tensor_sub(wtmp[:], wtmp[:], bufs["tmp1"][:, 0:1024]))
                    rec.fence("vector")
                    rec.op("vector", lambda eng, co=co: eng.tensor_copy(wl[:, co:co + 1024], wtmp[:]),
                           inc="vec")
                wsplit_done = ctx.cnt["vec"]
                pj_free = [0] * 8
                xn_mm = [0, 0]
                for tc in range(4):
                    pi2 = tc % 2
                    xnh, xnl = xn2[pi2]
                    rec.wait("sync", "tns", xn_mm[pi2])
                    for ft in range(FT):
                        rec.dma("sync", xnh[:, ft * 512:(ft + 1) * 512],
                                t["s_xnh"][ft * 128:(ft + 1) * 128, tc * 512:(tc + 1) * 512],
                                sem=("dma", "dmb", "dmc")[ft % 3])
                        rec.dma("sync", xnl[:, ft * 512:(ft + 1) * 512],
                                t["s_xnl"][ft * 128:(ft + 1) * 128, tc * 512:(tc + 1) * 512],
                                sem=("dma", "dmb", "dmc")[ft % 3])
                    rec.wait("tensor", "dma", ctx.cnt["dma"])
                    rec.wait("tensor", "dmb", ctx.cnt["dmb"])
                    rec.wait("tensor", "dmc", ctx.cnt["dmc"])
                    rec.wait("tensor", "vec", wsplit_done)
                    for grp in range(2):
                        for hh in range(NQH):
                            bi = grp * NQH + hh
                            bank = pj[bi]
                            rec.wait("tensor", "vec", pj_free[bi])
                            for ft in range(FT):
                                co = ft * 1024 + grp * 512 + hh * 128
                                mm3(rec, bank[:],
                                    wh[:, co:co + 128], wl[:, co:co + 128],
                                    xnh[:, ft * 512:(ft + 1) * 512],
                                    xnl[:, ft * 512:(ft + 1) * 512],
                                    start=(ft == 0), stop=(ft == FT - 1))
                            tv = rec.mark("tensor", "tns")
                            rec.wait("vector", "tns", tv)
                            dst_h = (qh, kh)[grp]
                            dst_l = (ql, kl)[grp]
                            co = hh * TATT + tc * 512
                            tmp = bufs["tmp1"][:, 512 * (bi % 2):512 * (bi % 2) + 512]
                            rec.op("vector", lambda eng, b=bank, d=dst_h, co=co: eng.tensor_copy(
                                d[:, co:co + 512], b[:]))
                            rec.fence("vector")
                            rec.op("vector", lambda eng, b=bank, d=dst_h, co=co, tmp=tmp: eng.tensor_sub(
                                tmp, b[:], d[:, co:co + 512]))
                            rec.fence("vector")
                            pj_free[bi] = rec.op("vector", lambda eng, d=dst_l, co=co, tmp=tmp: eng.tensor_copy(
                                d[:, co:co + 512], tmp), inc="vec")
                    xn_mm[pi2] = ctx.cnt["tns"]
            ctx.barrier(rec)
            with ExitStack() as ps:
                ps_sc = [ps.enter_context(nc.psum_tensor(f"p1_sc{i}_{rnd}", [128, 1024], F32))
                         for i in range(2)]
                ps_cs = ps.enter_context(nc.psum_tensor(f"p1_cs_{rnd}", [128, 512], F32))
                eb = [ps.enter_context(nc.sbuf_tensor(f"p1_eb{i}_{rnd}", [128, 2048], F32))
                      for i in range(2)]
                zpart = ps.enter_context(nc.sbuf_tensor(f"p1_zp_{rnd}", [128, 4], F32))
                zi2 = ps.enter_context(nc.sbuf_tensor(f"p1_zi_{rnd}", [128, 2], F32))
                eb_free = [0, 0]
                z_guard = [0, 0]
                z_done = [0, 0]
                sc_free = 0
                cv = 0
                pend = None
                def emit_colsum(pend):
                    nonlocal cv
                    pe, phh, pqt, pj_ = pend
                    if pqt == 0:
                        rec.wait("tensor", "vec", cv)
                    rec.wait("tensor", "vec", z_done[pj_])
                    for kt in range(4, 16):
                        rec.op("tensor", lambda eng, e=pe, kt=kt, pqt=pqt, pj_=pj_: eng.matmul(
                            ps_cs[:, kt - 4:kt - 3], e[:, kt * 128:(kt + 1) * 128],
                            zi2[:, pj_:pj_ + 1],
                            start=(pqt == 0 and kt == 4), stop=(pqt == 15 and kt == 15)))
                    m = rec.mark("tensor", "tns")
                    eb_free[pj_] = m
                    z_guard[pj_] = m
                    if pqt == 15:
                        rec.wait("vector", "tns", m)
                        hg = rnd * NQH + phh
                        cv = rec.op("vector", lambda eng, hg=hg: eng.tensor_copy(
                            sb["csum"][:, hg * 12:hg * 12 + 12], ps_cs[:, 0:12]), inc="vec")
                for i in range(NQH * 16):
                    hh, qt = divmod(i, 16)
                    j = i % 2
                    e = eb[j]
                    rec.wait("tensor", "act", sc_free)
                    for hlf in range(2):
                        for kc in range(2):
                            kco = hh * TATT + hlf * 1024 + kc * 512
                            qco = hh * TATT + qt * 128
                            mm3(rec, ps_sc[hlf][:, kc * 512:(kc + 1) * 512],
                                qh[:, qco:qco + 128], ql[:, qco:qco + 128],
                                kh[:, kco:kco + 512], kl[:, kco:kco + 512],
                                start=True, stop=True)
                    tmm = rec.mark("tensor", "tns")
                    if pend is not None:
                        emit_colsum(pend)
                    rec.wait("scalar", "tns", tmm)
                    rec.wait("scalar", "tns", eb_free[j])
                    rec.wait("scalar", "vec", z_done[j])
                    a0 = rec.op("scalar", lambda eng, e=e, j=j: eng.activation(
                        e[:, 0:1024], ps_sc[0][:], AF.Exp, bias=sb["negC"][:],
                        scale=SC_ATT, accum_out=zpart[:, 2 * j:2 * j + 1]), inc="act")
                    a1 = rec.op("scalar", lambda eng, e=e, j=j: eng.activation(
                        e[:, 1024:2048], ps_sc[1][:], AF.Exp, bias=sb["negC"][:],
                        scale=SC_ATT, accum_out=zpart[:, 2 * j + 1:2 * j + 2]), inc="act")
                    sc_free = a1
                    rec.wait("vector", "act", a1)
                    rec.wait("vector", "tns", z_guard[j])
                    rec.op("vector", lambda eng, j=j: eng.tensor_add(
                        zi2[:, j:j + 1], zpart[:, 2 * j:2 * j + 1], zpart[:, 2 * j + 1:2 * j + 2]))
                    rec.fence("vector")
                    z_done[j] = rec.op("vector", lambda eng, j=j: eng.reciprocal(
                        zi2[:, j:j + 1], zi2[:, j:j + 1]), inc="vec")
                    pend = (e, hh, qt, j)
                emit_colsum(pend)
                rec.wait("tensor", "vec", cv)
            ctx.barrier(rec)
            rec.flush()
    ctx.barrier(rec)
    rec.flush()


def p2a_quant_w0(rec, ctx, t, sb):
    nc = ctx.nc
    for (src_t, dst_t, n_in, n_rows, sc_dst) in [
            ("wgu0", "s_wq0", D, 2 * I0, "sgu0"),
            ("wd0", "s_wd0", I0, D, "sd0")]:
        with ExitStack() as ph:
            a = lambda n, shape, dt=F32: ph.enter_context(
                nc.sbuf_tensor(f"p2a_{n}_{src_t}", shape, dt))
            nat = [a(f"nat{i}", [128, n_in]) for i in range(2)]
            aa = [a(f"aa{i}", [128, n_in]) for i in range(2)]
            ib = [a(f"ib{i}", [128, n_in], BF16) for i in range(2)]
            amax = a("amax", [128, 2])
            sc = a("sc", [128, 2])
            rin = a("rin", [128, 2])
            ob = [a(f"ob{i}", [128, 128], BF16) for i in range(2)]
            pt = [ph.enter_context(nc.psum_tensor(f"p2a_pt{i}_{src_t}", [128, 128], BF16))
                  for i in range(4)]
            nt = n_in // 128
            nat_free = [0, 0]
            ib_free = [0, 0]
            pt_free = [0, 0, 0, 0]
            ob_free = [0, 0]
            for g in range(n_rows // 128):
                s = g % 2
                nsem = "dma" if s == 0 else "dmb"
                rec.wait("sync", "act", nat_free[s])
                dv = rec.dma("sync", nat[s][:], t[src_t][g * 128:(g + 1) * 128, :], sem=nsem)
                rec.wait("vector", nsem, dv)
                rec.op("vector", lambda eng, s=s: eng.reduce_max(
                    amax[:, s:s + 1], nat[s][:], AX.X, apply_absolute_value=True))
                rec.fence("vector")
                rec.op("vector", lambda eng, s=s: eng.tensor_scalar(
                    sc[:, s:s + 1], amax[:, s:s + 1], 1.0 / 127.0, 1e-8, ALU.mult, ALU.max))
                rec.fence("vector")
                rec.op("vector", lambda eng, s=s: eng.reciprocal(rin[:, s:s + 1], sc[:, s:s + 1]))
                rec.fence("vector")
                v1 = rec.op("vector", lambda eng, g=g, s=s: eng.tensor_copy(
                    sb[sc_dst][:, g:g + 1], sc[:, s:s + 1]), inc="vec")
                rec.wait("scalar", "vec", v1)
                rec.wait("scalar", "tns", ib_free[s])
                rec.op("scalar", lambda eng, s=s: eng.activation(
                    aa[s][:], nat[s][:], AF.Identity, bias=sb["magicM"][:],
                    scale=rin[:, s:s + 1]))
                rec.fence("scalar")
                av = rec.op("scalar", lambda eng, s=s: eng.activation(
                    ib[s][:], aa[s][:], AF.Identity, bias=sb["negM"][:], scale=1.0),
                    inc="act")
                nat_free[s] = av
                rec.wait("tensor", "act", av)
                for ft in range(nt):
                    i4 = ft % 4
                    i2 = ft % 2
                    osem = "dmag" if i2 == 0 else "dmah"
                    rec.wait("tensor", "vec", pt_free[i4])
                    tv = rec.op("tensor", lambda eng, i4=i4, s=s, ft=ft: eng.transpose(
                        pt[i4][:], ib[s][:, ft * 128:(ft + 1) * 128], sb["identb"][:]),
                        inc="tns")
                    rec.wait("vector", "tns", tv)
                    rec.wait("vector", osem, ob_free[i2])
                    ev = rec.op("vector", lambda eng, i4=i4, i2=i2: eng.tensor_copy(
                        ob[i2][:], pt[i4][:]), inc="vec")
                    pt_free[i4] = ev
                    rec.wait("gpsimd", "vec", ev)
                    ob_free[i2] = rec.dma(
                        "gpsimd", t[dst_t][ft * 128:(ft + 1) * 128, g * 128:(g + 1) * 128],
                        ob[i2][:], sem=osem)
                ib_free[s] = ctx.cnt["tns"]
            ctx.barrier(rec)
            rec.flush()
    ctx.barrier(rec)
    rec.flush()


def p2b_quant_w1(rec, ctx, t, sb):
    nc = ctx.nc
    for (src_t, dst_t, n_k, n_o, deq, thp, thn) in [
            ("l1wg", "s_wg1", D, I1, "deq_g1", "thp_g1", "thn_g1"),
            ("l1wu", "s_wu1", D, I1, "deq_u1", "thp_u1", "thn_u1"),
            ("l1wd", "s_wd1", I1, D, "deq_d1", "thp_d1", "thn_d1")]:
        with ExitStack() as ph:
            a = lambda n, shape, dt=F32: ph.enter_context(
                nc.sbuf_tensor(f"p2b_{n}_{src_t}", shape, dt))
            wt = [a(f"wt{i}", [128, n_o]) for i in range(2)]
            s1 = a("s1", [128, n_o])
            s2 = a("s2", [128, n_o])
            qb = [a(f"qb{i}", [128, n_o], BF16) for i in range(2)]
            acc = a("acc", [128, 16 if n_k == D else 64])
            tot = a("tot", [128, 1])
            pm = ph.enter_context(nc.psum_tensor(f"p2b_pm_{src_t}", [128, 512], F32))
            nkt = n_k // 128
            wt_free = [0, 0]
            for it in range(nkt):
                s = it % 2
                nsem = "dma" if s == 0 else "dmb"
                rec.wait("sync", "vec", wt_free[s])
                dv = rec.dma("sync", wt[s][:], t[src_t][it * 128:(it + 1) * 128, :], sem=nsem)
                rec.wait("vector", nsem, dv)
                wt_free[s] = rec.op("vector", lambda eng, it=it, s=s: eng.reduce_sum(
                    acc[:, it:it + 1], wt[s][:], AX.X, apply_absolute_value=True), inc="vec")
            rec.fence("vector")
            rec.op("vector", lambda eng: eng.reduce_sum(tot[:], acc[:], AX.X), inc="vec")
            tv = ctx.cnt["vec"]
            rec.wait("tensor", "vec", tv)
            rec.op("tensor", lambda eng: eng.matmul(
                pm[0:1, 0:1], tot[:], sb["onescol"][:], start=True, stop=True), inc="tns")
            rec.wait_cur("vector", "tns")
            rec.op("vector", lambda eng, n_k=n_k, n_o=n_o: eng.tensor_single_scalar(
                tot[0:1, 0:1], pm[0:1, 0:1], 1.0 / float(n_k * n_o), ALU.mult))
            rec.fence("vector")
            rec.op("vector", lambda eng: eng.tensor_single_scalar(
                tot[0:1, 0:1], tot[0:1, 0:1], 1e-5, ALU.max), inc="vec")
            tv = ctx.cnt["vec"]
            rec.wait("tensor", "vec", tv)
            rec.op("tensor", lambda eng: eng.matmul(
                pm[:, 1:2], sb["onesrow"][:], tot[0:1, 0:1],
                start=True, stop=True), inc="tns")
            rec.wait_cur("vector", "tns")
            rec.op("vector", lambda eng, deq=deq: eng.tensor_single_scalar(
                sb[deq][:], pm[:, 1:2], 0.5, ALU.mult))
            rec.fence("vector")
            rec.op("vector", lambda eng, deq=deq, thn=thn: eng.tensor_sub(
                sb[thn][:], sb["zero"][:], sb[deq][:]))
            rec.op("vector", lambda eng, deq=deq, thp=thp: eng.tensor_copy(
                sb[thp][:], sb[deq][:]), inc="vec")
            tv = ctx.cnt["vec"]
            rec.wait("scalar", "vec", tv)
            rec.wait_cur("sync", "vec")
            wt_free = [0, 0]
            qb_free = [0, 0]
            for it in range(nkt):
                s = it % 2
                nsem = "dma" if s == 0 else "dmb"
                osem = "dmag" if s == 0 else "dmah"
                rec.wait("sync", "act", wt_free[s])
                dv = rec.dma("sync", wt[s][:], t[src_t][it * 128:(it + 1) * 128, :], sem=nsem)
                rec.wait("scalar", nsem, dv)
                rec.wait_cur("scalar", "vec")
                rec.op("scalar", lambda eng, s=s, thn=thn: eng.activation(
                    s1[:], wt[s][:], AF.Sign, bias=sb[thn][:], scale=1.0))
                av = rec.op("scalar", lambda eng, s=s, thp=thp: eng.activation(
                    s2[:], wt[s][:], AF.Sign, bias=sb[thp][:], scale=1.0), inc="act")
                wt_free[s] = av
                rec.wait("vector", "act", av)
                rec.wait("vector", osem, qb_free[s])
                vv = rec.op("vector", lambda eng, s=s: eng.tensor_add(qb[s][:], s1[:], s2[:]),
                            inc="vec")
                rec.wait("gpsimd", "vec", vv)
                qb_free[s] = rec.dma("gpsimd", t[dst_t][it * 128:(it + 1) * 128, :], qb[s][:],
                                     sem=osem)
            ctx.barrier(rec)
            rec.flush()
    ctx.barrier(rec)
    rec.flush()


def _csum_bcast(rec, ctx, sb, col, ptr, pbc):
    rec.op("tensor", lambda eng: eng.transpose(
        ptr[:], sb["csum"][:, col:col + 1], sb["ident"][:]), inc="tns")
    rec.wait_cur("vector", "tns")
    rec.op("vector", lambda eng: eng.tensor_copy(sb["csrow"][:], ptr[0:1, 0:128]), inc="vec")
    vv = ctx.cnt["vec"]
    rec.wait("tensor", "vec", vv)
    rec.op("tensor", lambda eng: eng.matmul(
        pbc[:], sb["onesrow"][:], sb["csrow"][:], start=True, stop=True), inc="tns")
    return ctx.cnt["tns"]


def p2c_values_xq(rec, ctx, t, sb):
    nc = ctx.nc
    with ExitStack() as ph:
        a = lambda n, shape, dt=F32: ph.enter_context(nc.sbuf_tensor("p2c_" + n, shape, dt))
        bufs = {"tmp1": a("tmp1", [128, 1024])}
        wvh = a("wvh", [128, FT * 1024], BF16)
        wvl = a("wvl", [128, FT * 1024], BF16)
        wtmp = a("wtmp", [128, 1024])
        xnh = a("xnh", [128, FT * 512], BF16)
        xnl = a("xnl", [128, FT * 512], BF16)
        vtile = a("vtile", [128, 512])
        bc_sb = a("bc_sb", [128, 128])
        vh = a("vh", [128, 512], BF16)
        vl = a("vl", [128, 512], BF16)
        csb = a("csb", [128, NH * 512])
        with ExitStack() as ps:
            pp = [ps.enter_context(nc.psum_tensor(f"p2c_pp{i}", [128, 512], F32))
                  for i in range(4)]
            ptr = ps.enter_context(nc.psum_tensor("p2c_ptr", [1, 128], F32))
            pbc = ps.enter_context(nc.psum_tensor("p2c_pbc", [128, 128], F32))
            xn_mm = 0
            for tc in range(3):
                rec.wait("sync", "tns", xn_mm)
                for ft in range(FT):
                    rec.dma("sync", xnh[:, ft * 512:(ft + 1) * 512],
                            t["s_xnh"][ft * 128:(ft + 1) * 128,
                                       512 + tc * 512:1024 + tc * 512],
                            sem=("dma", "dmb", "dmc")[ft % 3])
                    rec.dma("sync", xnl[:, ft * 512:(ft + 1) * 512],
                            t["s_xnl"][ft * 128:(ft + 1) * 128,
                                       512 + tc * 512:1024 + tc * 512],
                            sem=("dma", "dmb", "dmc")[ft % 3])
                rec.wait("tensor", "dma", ctx.cnt["dma"])
                rec.wait("tensor", "dmb", ctx.cnt["dmb"])
                rec.wait("tensor", "dmc", ctx.cnt["dmc"])
                for h in range(NH):
                    for s in range(4):
                        col = h * 12 + tc * 4 + s
                        bt = _csum_bcast(rec, ctx, sb, col, ptr, pbc)
                        rec.wait("vector", "tns", bt)
                        rec.op("vector", lambda eng, h=h, s=s: eng.tensor_copy(
                            csb[:, h * 512 + s * 128:h * 512 + (s + 1) * 128], pbc[:]),
                            inc="vec")
                        rec.wait("tensor", "vec", ctx.cnt["vec"])
                pp_free = [0, 0, 0, 0]
                for wf in range(2):
                  for ft in range(FT):
                    wsem = ("dma", "dmb")[ft % 2]
                    rec.wait_cur("sync", "vec")
                    rec.wait_cur("sync", "tns")
                    rec.dma("sync", wtmp[:],
                            t["wqkv"][ft * 128:(ft + 1) * 128,
                                      4096 + wf * 1024:4096 + (wf + 1) * 1024],
                            sem=wsem)
                    rec.wait_cur("vector", wsem)
                    co = ft * 1024
                    rec.op("vector", lambda eng, co=co: eng.tensor_copy(wvh[:, co:co + 1024], wtmp[:]))
                    rec.fence("vector")
                    rec.op("vector", lambda eng, co=co: eng.tensor_copy(
                        bufs["tmp1"][:, 0:1024], wvh[:, co:co + 1024]))
                    rec.fence("vector")
                    rec.op("vector", lambda eng, co=co: eng.tensor_sub(
                        wtmp[:], wtmp[:], bufs["tmp1"][:, 0:1024]))
                    rec.fence("vector")
                    rec.op("vector", lambda eng, co=co: eng.tensor_copy(wvl[:, co:co + 1024], wtmp[:]),
                           inc="vec")
                  wsplit_done = ctx.cnt["vec"]
                  rec.wait("tensor", "vec", wsplit_done)
                  for h in range(wf * 8, wf * 8 + 8):
                    bi = h % 4
                    bank = pp[bi]
                    rec.wait("tensor", "vec", pp_free[bi])
                    for ft in range(FT):
                        co = ft * 1024 + (h - wf * 8) * 128
                        mm3(rec, bank[:],
                            wvh[:, co:co + 128], wvl[:, co:co + 128],
                            xnh[:, ft * 512:(ft + 1) * 512],
                            xnl[:, ft * 512:(ft + 1) * 512],
                            start=(ft == 0), stop=(ft == FT - 1))
                    mmv = rec.mark("tensor", "tns")
                    rec.wait("vector", "tns", mmv)
                    rec.wait_cur("vector", "dmag")
                    rec.op("vector", lambda eng, bank=bank, h=h: eng.tensor_mul(
                        vtile[:], bank[:], csb[:, h * 512:(h + 1) * 512]), inc="vec")
                    pp_free[bi] = ctx.cnt["vec"]
                    rec.fence("vector")
                    rec.op("vector", lambda eng: eng.tensor_copy(vh[:], vtile[:]))
                    rec.fence("vector")
                    rec.op("vector", lambda eng: eng.tensor_copy(
                        bufs["tmp1"][:, 0:512], vh[:]))
                    rec.fence("vector")
                    rec.op("vector", lambda eng: eng.tensor_sub(
                        vtile[:], vtile[:], bufs["tmp1"][:, 0:512]))
                    rec.fence("vector")
                    rec.op("vector", lambda eng: eng.tensor_copy(vl[:], vtile[:]), inc="vec")
                    vv = ctx.cnt["vec"]
                    rec.wait("gpsimd", "vec", vv)
                    rec.dma("gpsimd", t["s_vh"][h * 128:(h + 1) * 128, tc * 512:(tc + 1) * 512], vh[:])
                    rec.dma("gpsimd", t["s_vl"][h * 128:(h + 1) * 128, tc * 512:(tc + 1) * 512], vl[:])
                xn_mm = ctx.cnt["tns"]
    ctx.barrier(rec)
    rec.flush()


def _row_bcast4(rec, ctx, sb, src_cols, ptr, pbc, evac_sb, n=4):
    rec.wait_cur("tensor", "vec")
    for s in range(n):
        rec.op("tensor", lambda eng, s=s: eng.transpose(
            ptr[:], src_cols[:, s:s + 1], sb["ident"][:]), inc="tns")
        rec.wait_cur("vector", "tns")
        rec.op("vector", lambda eng: eng.tensor_copy(sb["csrow"][:], ptr[0:1, 0:128]), inc="vec")
        vv = ctx.cnt["vec"]
        rec.wait("tensor", "vec", vv)
        rec.op("tensor", lambda eng, s=s, n=n: eng.matmul(
            pbc[:, s * 128:(s + 1) * 128], sb["onesrow"][:], sb["csrow"][:],
            start=(s == 0), stop=(s == n - 1)), inc="tns")
    tv = ctx.cnt["tns"]
    rec.wait("vector", "tns", tv)
    rec.op("vector", lambda eng, n=n: eng.tensor_copy(evac_sb[:], pbc[:, 0:n * 128]), inc="vec")
    rec.fence("vector")
    return ctx.cnt["vec"]


def p2d_ffn0(rec, ctx, t, sb):
    nc = ctx.nc
    for tc in range(3):
        with ExitStack() as ph:
            a = lambda n, shape, dt=F32: ph.enter_context(
                nc.sbuf_tensor(f"p2d_{n}_{tc}", shape, dt))
            xqc = a("xqc", [128, FT * 512], BF16)
            wq2 = [a(f"wq2{i}", [128, FT * 512], BF16) for i in range(2)]
            hfe = a("hfe", [128, 44 * 512])
            hqf = a("hqf", [128, 44 * 512], BF16)
            wdh = [a(f"wdq{i}", [128, 22 * 128], BF16) for i in range(2)]
            sxb = a("sxb", [128, 512])
            shb = a("shb", [128, 512])
            g1t = a("g1t", [128, 512])
            g1tb = a("g1tb", [128, 512])
            slt = a("slt", [128, 512])
            sltb = a("sltb", [128, 512])
            g1t2 = [g1t, g1tb]
            slt2 = [slt, sltb]
            pair_free = [0, 0]
            pt_free = [0, 0]
            hamax = a("hamax", [128, 4])
            hsc = a("hsc", [128, 4])
            tmp0 = a("tmp0", [128, 4])
            with ExitStack() as ps:
                ppg2 = [ps.enter_context(nc.psum_tensor(f"p2d_ppg{i}_{tc}", [128, 512], F32))
                        for i in range(2)]
                ppu2 = [ps.enter_context(nc.psum_tensor(f"p2d_ppu{i}_{tc}", [128, 512], F32))
                        for i in range(2)]
                ppg, ppu = ppg2[0], ppu2[0]
                ptA = ps.enter_context(nc.psum_tensor(f"p2d_ptA_{tc}", [128, 128], F32))
                ptB = ps.enter_context(nc.psum_tensor(f"p2d_ptB_{tc}", [128, 128], F32))
                ptr = ps.enter_context(nc.psum_tensor(f"p2d_ptr_{tc}", [1, 128], F32))
                pbc = ps.enter_context(nc.psum_tensor(f"p2d_pbc_{tc}", [128, 512], F32))
                for ft in range(FT):
                    rec.dma("sync", xqc[:, ft * 512:(ft + 1) * 512],
                            t["s_xq0"][ft * 128:(ft + 1) * 128, tc * 512:(tc + 1) * 512])
                xq_dma = ctx.cnt["dma"]
                _row_bcast4(rec, ctx, sb, sb["dxq0"][:, tc * 4:tc * 4 + 4], ptr, pbc, sxb)
                rec.wait("tensor", "dma", xq_dma)
                rec.op("gpsimd", lambda eng: eng.memset(hamax[:], 0.0), inc="gps")
                rec.wait_cur("vector", "gps")
                step_mm = [0, 0]
                pend_tr = None

                def emit_tr(gt, hv):
                    rec.wait("tensor", "vec", hv)
                    for s in range(4):
                        pt = ptA if s % 2 == 0 else ptB
                        rec.wait("tensor", "vec", pt_free[s % 2])
                        tv = rec.op("tensor", lambda eng, pt=pt, gt=gt, s=s: eng.transpose(
                            pt[:], hfe[:, gt * 512 + s * 128:gt * 512 + s * 128 + 128],
                            sb["ident"][:]), inc="tns")
                        rec.wait("vector", "tns", tv)
                        rec.op("vector", lambda eng, pt=pt, s=s: eng.reduce_max(
                            tmp0[:, s:s + 1], pt[:], AX.X, apply_absolute_value=True))
                        pt_free[s % 2] = rec.fence("vector")
                        rec.op("vector", lambda eng, s=s: eng.tensor_max(
                            hamax[:, s:s + 1], hamax[:, s:s + 1], tmp0[:, s:s + 1]),
                            inc="vec")
                for st in range(22):
                    sbi = st % 2
                    wq = wq2[sbi]
                    rec.wait("sync", "tns", step_mm[sbi])
                    for ft in range(FT):
                        rec.dma("sync", wq[:, ft * 512:ft * 512 + 256],
                                t["s_wq0"][ft * 128:(ft + 1) * 128, st * 256:(st + 1) * 256])
                        rec.dma("sync", wq[:, ft * 512 + 256:ft * 512 + 512],
                                t["s_wq0"][ft * 128:(ft + 1) * 128,
                                           I0 + st * 256:I0 + (st + 1) * 256], sem="dmb")
                    ld_a, ld_b = ctx.cnt["dma"], ctx.cnt["dmb"]
                    rec.wait("tensor", "dma", ld_a)
                    rec.wait("tensor", "dmb", ld_b)
                    for g in range(2):
                        gt = st * 2 + g
                        pi = gt % 2
                        pg, pu = ppg2[pi], ppu2[pi]
                        rec.wait("tensor", "vec", pair_free[pi])
                        for ft in range(FT):
                            rec.op("tensor", lambda eng, ft=ft, g=g, pg=pg, wq=wq: eng.matmul(
                                pg[:], wq[:, ft * 512 + g * 128:ft * 512 + g * 128 + 128],
                                xqc[:, ft * 512:(ft + 1) * 512],
                                start=(ft == 0), stop=(ft == FT - 1)))
                        for ft in range(FT):
                            rec.op("tensor", lambda eng, ft=ft, g=g, pu=pu, wq=wq: eng.matmul(
                                pu[:], wq[:, ft * 512 + 256 + g * 128:ft * 512 + 256 + g * 128 + 128],
                                xqc[:, ft * 512:(ft + 1) * 512],
                                start=(ft == 0), stop=(ft == FT - 1)))
                        mmv = rec.mark("tensor", "tns")
                        if g == 1:
                            step_mm[sbi] = mmv
                        if pend_tr is not None:
                            emit_tr(*pend_tr)
                        rec.wait("vector", "tns", mmv)
                        rec.wait_cur("vector", "act")
                        rec.op("vector", lambda eng, pg=pg, g1x=g1t2[pi]: eng.tensor_mul(
                            g1x[:], pg[:], sxb[:]), inc="vec")
                        rec.wait_cur("scalar", "vec")
                        rec.op("scalar", lambda eng, gt=gt, g1x=g1t2[pi], slx=slt2[pi]: eng.activation(
                            slx[:], g1x[:], AF.Silu, bias=sb["zero"][:],
                            scale=sb["sgu0"][:, gt:gt + 1]), inc="act")
                        rec.wait_cur("vector", "act")
                        rec.op("vector", lambda eng, pu=pu, g1x=g1t2[pi]: eng.tensor_mul(
                            g1x[:], pu[:], sxb[:]))
                        rec.fence("vector")
                        pair_free[pi] = rec.op("vector", lambda eng, gt=gt, g1x=g1t2[pi], slx=slt2[pi]: eng.scalar_tensor_tensor(
                            hfe[:, gt * 512:(gt + 1) * 512], g1x[:],
                            sb["sgu0"][:, 44 + gt:45 + gt], slx[:],
                            ALU.mult, ALU.mult), inc="vec")
                        hv = ctx.cnt["vec"]
                        pend_tr = (gt, hv)
                emit_tr(*pend_tr)
                pend_tr = None
                rec.fence("vector")
                rec.op("vector", lambda eng: eng.tensor_scalar(
                    hsc[:], hamax[:], 1.0 / 127.0, 1e-8, ALU.mult, ALU.max))
                rec.fence("vector")
                rec.op("vector", lambda eng: eng.reciprocal(tmp0[:], hsc[:]), inc="vec")
                _row_bcast4(rec, ctx, sb, tmp0, ptr, pbc, sxb)
                _row_bcast4(rec, ctx, sb, hsc, ptr, pbc, shb)
                for gt in range(44):
                    sl = slice(gt * 512, (gt + 1) * 512)
                    rec.op("vector", lambda eng, sl=sl: eng.tensor_mul(
                        hfe[:, sl], hfe[:, sl], sxb[:]))
                    rec.fence("vector")
                    rec.op("vector", lambda eng, sl=sl: eng.tensor_scalar_add(
                        hfe[:, sl], hfe[:, sl], sb["magicM"][:]))
                    rec.fence("vector")
                    rec.op("vector", lambda eng, sl=sl: eng.tensor_single_scalar(
                        hqf[:, sl], hfe[:, sl], sb["magicM"][:], ALU.subtract), inc="vec")
                hq_done = ctx.cnt["vec"]
                if tc == 0 and "dbg_hqf" in t:
                    rec.wait("gpsimd", "vec", hq_done)
                    rec.dma("gpsimd", t["dbg_hqf"], hqf[:])
                    rec.dma("gpsimd", t["dbg_hamax"], hamax[:])
                    rec.dma("gpsimd", t["dbg_sxb"], sxb[:])
                    rec.dma("gpsimd", t["dbg_shb"], shb[:])
                rec.wait("sync", "vec", hq_done)
                wdq_mm = [0, 0]
                dn_free = [0, 0]
                for o2 in range(FT):
                    pi = o2 % 2
                    for hf in range(2):
                        wd = wdh[hf]
                        rec.wait("sync", "tns", wdq_mm[hf])
                        for it2 in range(22):
                            it = hf * 22 + it2
                            rec.dma("sync", wd[:, it2 * 128:(it2 + 1) * 128],
                                    t["s_wd0"][it * 128:(it + 1) * 128,
                                               o2 * 128:(o2 + 1) * 128],
                                    sem=("dma", "dmb", "out")[it2 % 3])
                        rec.wait("tensor", "dma", ctx.cnt["dma"])
                        rec.wait("tensor", "dmb", ctx.cnt["dmb"])
                        rec.wait("tensor", "out", ctx.cnt["out"])
                        if hf == 0:
                            rec.wait("tensor", "vec", dn_free[pi])
                        for it2 in range(22):
                            it = hf * 22 + it2
                            rec.op("tensor", lambda eng, it=it, it2=it2, pg=ppg2[pi], wd=wd:
                                   eng.matmul(pg[:], wd[:, it2 * 128:(it2 + 1) * 128],
                                              hqf[:, it * 512:(it + 1) * 512],
                                              start=(it == 0), stop=(it == 43)))
                        wdq_mm[hf] = rec.mark("tensor", "tns")
                    mmv = wdq_mm[1]
                    rec.wait("vector", "tns", mmv)
                    rec.wait_cur("vector", "act")
                    dn_free[pi] = rec.op("vector", lambda eng, pg=ppg2[pi], g1x=g1t2[pi]: eng.tensor_mul(
                        g1x[:], pg[:], shb[:]), inc="vec")
                    vv = ctx.cnt["vec"]
                    rec.wait("scalar", "vec", vv)
                    rec.wait_cur("scalar", "dmag")
                    rec.op("scalar", lambda eng, o2=o2, g1x=g1t2[pi], slx=slt2[pi]: eng.activation(
                        slx[:], g1x[:], AF.Silu, bias=sb["zero"][:],
                        scale=sb["sd0"][:, o2:o2 + 1]), inc="act")
                    av = ctx.cnt["act"]
                    rec.wait("gpsimd", "act", av)
                    rec.dma("gpsimd", t["s_g0"][o2 * 128:(o2 + 1) * 128, tc * 512:(tc + 1) * 512],
                            slt2[pi][:])
                ctx.barrier(rec)
                rec.flush()
    ctx.barrier(rec)
    rec.flush()


def p2e_outproj_xl1(rec, ctx, t, sb):
    nc = ctx.nc
    for half in range(2):
        with ExitStack() as ph:
            a = lambda n, shape, dt=F32: ph.enter_context(
                nc.sbuf_tensor(f"p2e_{n}_{half}", shape, dt))
            woh = a("woh", [128, FT * 1024], BF16)
            wol = a("wol", [128, FT * 1024], BF16)
            wtmp = a("wtmp", [128, 1024])
            tmp1 = a("tmp1", [128, 1024])
            vh = a("vh", [128, FT * 512], BF16)
            vl = a("vl", [128, FT * 512], BF16)
            g0c = a("g0c", [128, 8 * 512])
            xfc = a("xfc", [128, 8 * 512])
            s0t = a("s0t", [128, 512])
            xlt = a("xlt", [128, 512])
            with ExitStack() as ps:
                pp = [ps.enter_context(nc.psum_tensor(f"p2e_pp{i}_{half}", [128, 512], F32))
                      for i in range(4)]
                for ft in range(FT):
                    wsem = ("dma", "dmb")[ft % 2]
                    rec.wait_cur("sync", "vec")
                    rec.dma("sync", wtmp[:],
                            t["wo0"][ft * 128:(ft + 1) * 128, half * 1024:(half + 1) * 1024],
                            sem=wsem)
                    rec.wait_cur("vector", wsem)
                    co = ft * 1024
                    rec.op("vector", lambda eng, co=co: eng.tensor_copy(woh[:, co:co + 1024], wtmp[:]))
                    rec.fence("vector")
                    rec.op("vector", lambda eng, co=co: eng.tensor_copy(
                        tmp1[:], woh[:, co:co + 1024]))
                    rec.fence("vector")
                    rec.op("vector", lambda eng: eng.tensor_sub(wtmp[:], wtmp[:], tmp1[:]))
                    rec.fence("vector")
                    rec.op("vector", lambda eng, co=co: eng.tensor_copy(wol[:, co:co + 1024], wtmp[:]),
                           inc="vec")
                wsplit = ctx.cnt["vec"]
                for tc in range(3):
                    rec.wait_cur("sync", "tns")
                    for ft in range(FT):
                        rec.dma("sync", vh[:, ft * 512:(ft + 1) * 512],
                                t["s_vh"][ft * 128:(ft + 1) * 128, tc * 512:(tc + 1) * 512])
                        rec.dma("sync", vl[:, ft * 512:(ft + 1) * 512],
                                t["s_vl"][ft * 128:(ft + 1) * 128, tc * 512:(tc + 1) * 512],
                                sem="dmb")
                    rec.wait_cur("sync", "vec")
                    for ot in range(8):
                        g = half * 8 + ot
                        rec.dma("sync", g0c[:, ot * 512:(ot + 1) * 512],
                                t["s_g0"][g * 128:(g + 1) * 128, tc * 512:(tc + 1) * 512],
                                sem="out")
                        rec.dma("sync", xfc[:, ot * 512:(ot + 1) * 512],
                                t["xatt"][g * 128:(g + 1) * 128,
                                          512 + tc * 512:512 + (tc + 1) * 512],
                                sem="out")
                    rec.wait("tensor", "dma", ctx.cnt["dma"])
                    rec.wait("tensor", "dmb", ctx.cnt["dmb"])
                    rec.wait("tensor", "out", ctx.cnt["out"])
                    rec.wait("tensor", "vec", wsplit)
                    pp_free = [0, 0, 0, 0]
                    for ot in range(8):
                        g = half * 8 + ot
                        bi = ot % 4
                        bank = pp[bi]
                        rec.wait("tensor", "vec", pp_free[bi])
                        for ft in range(FT):
                            co = ft * 1024 + ot * 128
                            mm3(rec, bank[:],
                                woh[:, co:co + 128], wol[:, co:co + 128],
                                vh[:, ft * 512:(ft + 1) * 512],
                                vl[:, ft * 512:(ft + 1) * 512],
                                start=(ft == 0), stop=(ft == FT - 1))
                        mmv = rec.mark("tensor", "tns")
                        rec.wait("vector", "tns", mmv)
                        rec.wait_cur("vector", "dmag")
                        pp_free[bi] = rec.op("vector", lambda eng, bank=bank, ot=ot: eng.tensor_mul(
                            s0t[:], bank[:], g0c[:, ot * 512:(ot + 1) * 512]), inc="vec")
                        rec.fence("vector")
                        rec.op("vector", lambda eng, ot=ot: eng.tensor_add(
                            xlt[:], s0t[:], xfc[:, ot * 512:(ot + 1) * 512]), inc="vec")
                        vv = ctx.cnt["vec"]
                        rec.wait("gpsimd", "vec", vv)
                        rec.dma("gpsimd", t["s_xl1"][g * 128:(g + 1) * 128,
                                                     tc * 512:(tc + 1) * 512], xlt[:])
                ctx.barrier(rec)
                rec.flush()
    ctx.barrier(rec)
    rec.flush()


def p3a_l1_proj(rec, ctx, t, sb):
    nc = ctx.nc
    with ExitStack() as ph:
        a = lambda n, shape, dt=F32: ph.enter_context(nc.sbuf_tensor("p3a_" + n, shape, dt))
        xn1h = a("xn1h", [128, FT * TF], BF16)
        xn1l = a("xn1l", [128, FT * TF], BF16)
        with ExitStack() as ps:
            c = lambda n, shape, dt=F32: ps.enter_context(nc.sbuf_tensor("p3a_" + n, shape, dt))
            bufs = {
                "xch": c("xch", [128, FT * 128]),
                "xT": c("xT", [128, D]), "xnT": c("xnT", [128, D]),
                "stats": c("stats", [128, 24]), "mv": c("mv", [128, 2]),
                "rr": c("rr", [128, 1]), "b2": c("b2", [128, 1]),
                "tmp1": c("tmp1", [128, 1024]),
                "amax": c("amax", [128, 1]), "sc": c("sc", [128, 1]),
                "rin": c("rin", [128, 1]),
                "pb0": ps.enter_context(nc.psum_tensor("p3a_pb0", [128, 512], F32)),
                "pb1": ps.enter_context(nc.psum_tensor("p3a_pb1", [128, 512], F32)),
            }
            bufs["aT"] = bufs["xT"]
            qstg = [ps.enter_context(nc.sbuf_tensor(f"p3a_qstg{i}", [128, 128], BF16))
                    for i in range(2)]

            class _View:
                def __init__(self, slab, tc):
                    self.slab, self.tc = slab, tc

                def __getitem__(self, key):
                    p_sl, c_sl = key
                    ft = c_sl.start // 512
                    off = c_sl.start - ft * 512
                    base = ft * TF + self.tc * 512 + off
                    return self.slab[p_sl, base:base + (c_sl.stop - c_sl.start)]
            for tc in range(3):
                out = {"hi": _View(xn1h, tc), "lo": _View(xn1l, tc)}
                if tc >= 1:
                    out["quant"] = {"kind": "l1", "dma_to": (t["s_xq1"], (tc - 1) * 512),
                                    "stage": qstg,
                                    "deq_cols": sb["dxq1"][:, (tc - 1) * 4:(tc - 1) * 4 + 4]}
                ln_quant_chunk(rec, ctx, sb, bufs, t["s_xl1"], tc * 512, 512, "split",
                               out=out)
            ctx.barrier(rec)
            rec.flush()
        es_r = ExitStack()
        sb["ropeq_c"] = es_r.enter_context(nc.sbuf_tensor("c_ropeq_c", [HD, TOWN], F32))
        sb["ropeq_s"] = es_r.enter_context(nc.sbuf_tensor("c_ropeq_s", [HD, TOWN], F32))
        sb["ropek_c"] = es_r.enter_context(nc.sbuf_tensor("c_ropek_c", [HD, TF], F32))
        sb["ropek_s"] = es_r.enter_context(nc.sbuf_tensor("c_ropek_s", [HD, TF], F32))
        for name in ["ropeq_c", "ropeq_s", "ropek_c", "ropek_s"]:
            rec.dma("sync", sb[name][:], t[name], sem="out")
        rope_dma = ctx.cnt["out"]
        for which, wf in [("k", 0), ("k", 1), ("k", 2), ("k", 3),
                          ("q", 0), ("q", 1), ("q", 2), ("q", 3)]:
            with ExitStack() as ps:
                b = lambda n, shape, dt=F32: ps.enter_context(
                    nc.sbuf_tensor(f"p3a_{n}_{which}{wf}", shape, dt))
                wh = b("wh", [128, FT * 512], BF16)
                wl = b("wl", [128, FT * 512], BF16)
                wtmp = b("wtmp", [128, 512])
                wtm2 = b("wtm2", [128, 512])
                T = TF if which == "k" else TOWN
                col0 = 0 if which == "k" else 512
                kf = b("kf", [128, T])
                ks = b("ks", [128, T])
                kt = b("kt", [128, T])
                kh2 = b("kh2", [128, T], BF16)
                rcs = sb["ropek_c"] if which == "k" else sb["ropeq_c"]
                rss = sb["ropek_s"] if which == "k" else sb["ropeq_s"]
                pj = [ps.enter_context(nc.psum_tensor(f"p3a_pj{i}_{which}{wf}", [128, 512], F32))
                      for i in range(3)]
                wsrc = t["l1wk"] if which == "k" else t["l1wq"]
                for ft in range(FT):
                    wsem = ("dma", "dmb")[ft % 2]
                    rec.wait_cur("sync", "vec")
                    rec.dma("sync", wtmp[:], wsrc[ft * 128:(ft + 1) * 128,
                                                  wf * 512:(wf + 1) * 512], sem=wsem)
                    rec.wait_cur("vector", wsem)
                    co = ft * 512
                    rec.op("vector", lambda eng, co=co: eng.tensor_copy(wh[:, co:co + 512], wtmp[:]))
                    rec.fence("vector")
                    rec.op("vector", lambda eng, co=co: eng.tensor_copy(wtm2[:], wh[:, co:co + 512]))
                    rec.fence("vector")
                    rec.op("vector", lambda eng: eng.tensor_sub(wtmp[:], wtmp[:], wtm2[:]))
                    rec.fence("vector")
                    rec.op("vector", lambda eng, co=co: eng.tensor_copy(wl[:, co:co + 512], wtmp[:]),
                           inc="vec")
                wdone = ctx.cnt["vec"]
                rec.wait("vector", "out", rope_dma)
                rec.wait("tensor", "vec", wdone)
                for h in range(wf * 4, wf * 4 + 4):
                    for tcc in range(T // 512):
                        bank = pj[tcc % 3]
                        rec.wait_cur("tensor", "vec")
                        for ft in range(FT):
                            co = ft * 512 + (h - wf * 4) * 128
                            cx = ft * TF + col0 + tcc * 512
                            mm3(rec, bank[:],
                                wh[:, co:co + 128], wl[:, co:co + 128],
                                xn1h[:, cx:cx + 512], xn1l[:, cx:cx + 512],
                                start=(ft == 0), stop=(ft == FT - 1))
                        mmv = rec.mark("tensor", "tns")
                        rec.wait("vector", "tns", mmv)
                        rec.op("vector", lambda eng, bank=bank, tcc=tcc: eng.tensor_copy(
                            kf[:, tcc * 512:(tcc + 1) * 512], bank[:]), inc="vec")
                    mask = []
                    for i in range(0, 32, 2):
                        mask += [i + 1, i]
                    rec.fence("vector")
                    rec.op("vector", lambda eng: eng.stream_shuffle(ks[:], kf[:], mask))
                    rec.fence("vector")
                    rec.op("vector", lambda eng: eng.tensor_mul(kt[:], kf[:], rcs[:]))
                    rec.op("vector", lambda eng: eng.tensor_mul(ks[:], ks[:], rss[:]))
                    rec.fence("vector")
                    rec.op("vector", lambda eng: eng.tensor_add(kf[:], kt[:], ks[:]))
                    rec.fence("vector")
                    rec.op("vector", lambda eng: eng.tensor_copy(kh2[:], kf[:]))
                    rec.fence("vector")
                    rec.op("vector", lambda eng: eng.tensor_copy(kt[:], kh2[:]))
                    rec.fence("vector")
                    rec.op("vector", lambda eng: eng.tensor_sub(kf[:], kf[:], kt[:]))
                    rec.fence("vector")
                    dsth = t["s_krh"] if which == "k" else t["s_qrh"]
                    dstl = t["s_krl"] if which == "k" else t["s_qrl"]
                    rec.op("vector", lambda eng: eng.tensor_copy(ks[:], kf[:]), inc="vec")
                    rec.fence("vector")
                    vv = ctx.cnt["vec"]
                    rec.wait("gpsimd", "vec", vv)
                    rec.wait_cur("gpsimd", "dmag")
                    rec.dma("gpsimd", dsth[h * 128:(h + 1) * 128, :], kh2[:])
                    rec.wait_cur("vector", "dmag")
                    rec.op("vector", lambda eng: eng.tensor_copy(kh2[:], ks[:]), inc="vec")
                    vv = ctx.cnt["vec"]
                    rec.wait("gpsimd", "vec", vv)
                    rec.dma("gpsimd", dstl[h * 128:(h + 1) * 128, :], kh2[:])
                    rec.wait_cur("vector", "dmag")
                ctx.barrier(rec)
                rec.flush()
        es_r.close()
        for wf in range(4):
          with ExitStack() as ps:
            b = lambda n, shape, dt=F32: ps.enter_context(
                nc.sbuf_tensor(f"p3a_{n}_v{wf}", shape, dt))
            wh = b("wh", [128, FT * 512], BF16)
            wl = b("wl", [128, FT * 512], BF16)
            wtmp = b("wtmp", [128, 512])
            wtm2 = b("wtm2", [128, 512])
            vts = b("vts", [128, 512])
            pj = [ps.enter_context(nc.psum_tensor(f"p3a_pjv{i}{wf}", [128, 512], F32))
                  for i in range(2)]
            for ft in range(FT):
                wsem = ("dma", "dmb")[ft % 2]
                rec.wait_cur("sync", "vec")
                rec.dma("sync", wtmp[:], t["l1wv"][ft * 128:(ft + 1) * 128,
                                                   wf * 512:(wf + 1) * 512], sem=wsem)
                rec.wait_cur("vector", wsem)
                co = ft * 512
                rec.op("vector", lambda eng, co=co: eng.tensor_copy(wh[:, co:co + 512], wtmp[:]))
                rec.fence("vector")
                rec.op("vector", lambda eng, co=co: eng.tensor_copy(wtm2[:], wh[:, co:co + 512]))
                rec.fence("vector")
                rec.op("vector", lambda eng: eng.tensor_sub(wtmp[:], wtmp[:], wtm2[:]))
                rec.fence("vector")
                rec.op("vector", lambda eng, co=co: eng.tensor_copy(wl[:, co:co + 512], wtmp[:]),
                       inc="vec")
            wdone = ctx.cnt["vec"]
            rec.wait("tensor", "vec", wdone)
            for jt in range(12):
                for vo in [wf]:
                    bank = pj[vo % 2]
                    rec.wait_cur("tensor", "vec")
                    for ft in range(FT):
                        cx = ft * TF + jt * 128
                        co = ft * 512
                        mm3(rec, bank[:],
                            xn1h[:, cx:cx + 128], xn1l[:, cx:cx + 128],
                            wh[:, co:co + 512], wl[:, co:co + 512],
                            start=(ft == 0), stop=(ft == FT - 1))
                    mmv = rec.mark("tensor", "tns")
                    rec.wait("vector", "tns", mmv)
                    rec.wait_cur("vector", "dmag")
                    rec.op("vector", lambda eng, bank=bank: eng.tensor_copy(vts[:], bank[:]),
                           inc="vec")
                    vv = ctx.cnt["vec"]
                    rec.wait("gpsimd", "vec", vv)
                    rec.dma("gpsimd", t["s_vT"][jt * 128:(jt + 1) * 128, vo * 512:(vo + 1) * 512],
                            vts[:])
            ctx.barrier(rec)
            rec.flush()
    ctx.barrier(rec)
    rec.flush()


def p3b_l1_attn(rec, ctx, t, sb):
    nc = ctx.nc
    with ExitStack() as ph:
        a = lambda n, shape, dt=F32: ph.enter_context(nc.sbuf_tensor("p3b_" + n, shape, dt))
        for name in ["maskA", "maskB"]:
            sb[name] = ph.enter_context(nc.sbuf_tensor("c_" + name, [128, 8 * 512], BF16))
        for name in ["maskA", "maskB"]:
            for jt in range(8):
                rec.dma("sync", sb[name][:, jt * 512:(jt + 1) * 512],
                        t[name][jt * 128:(jt + 1) * 128, :])
        mdma = ctx.cnt["dma"]
        woh = a("woh", [128, FT * 1024], BF16)
        wol = a("wol", [128, FT * 1024], BF16)
        wtmp = a("wtmp", [128, 1024])
        wtm2 = a("wtm2", [128, 1024])
        khb = a("khb", [128, 1024], BF16)
        klb = a("klb", [128, 1024], BF16)
        qhb = a("qhb", [128, 512], BF16)
        qlb = a("qlb", [128, 512], BF16)
        eb = a("eb", [128, 8 * 512])
        zsb = a("zsb", [128, 4])
        zr = a("zr", [128, 4])
        zbc = a("zbc", [128, 512])
        vts = a("vts", [128, 128])
        onorm = a("onorm", [128, 512])
        oh = a("oh", [128, NH * 512], BF16)
        ol = a("ol", [128, NH * 512], BF16)
        s1e = a("s1e", [128, 512])
        with ExitStack() as ps:
            psc = [ps.enter_context(nc.psum_tensor(f"p3b_psc{i}", [128, 512], F32))
                   for i in range(2)]
            psz = ps.enter_context(nc.psum_tensor("p3b_psz", [128, 512], F32))
            pso = ps.enter_context(nc.psum_tensor("p3b_pso", [128, 512], F32))
            ptr = ps.enter_context(nc.psum_tensor("p3b_ptr", [1, 128], F32))
            pbc = ps.enter_context(nc.psum_tensor("p3b_pbc", [128, 512], F32))
            rec.wait("vector", "dma", mdma)
            for qb in range(2):
                for h in range(NH):
                    rec.wait_cur("sync", "tns")
                    rec.dma("sync", khb[:], t["s_krh"][h * 128:(h + 1) * 128,
                                                       qb * 512:qb * 512 + 1024])
                    rec.dma("sync", klb[:], t["s_krl"][h * 128:(h + 1) * 128,
                                                       qb * 512:qb * 512 + 1024])
                    rec.dma("sync", qhb[:], t["s_qrh"][h * 128:(h + 1) * 128,
                                                       qb * 512:(qb + 1) * 512])
                    rec.dma("sync", qlb[:], t["s_qrl"][h * 128:(h + 1) * 128,
                                                       qb * 512:(qb + 1) * 512])
                    kdma = ctx.cnt["dma"]
                    rec.wait("tensor", "dma", kdma)
                    mname = "maskA" if qb == 0 else "maskB"
                    rec.wait_cur("scalar", "tns")
                    for jt in range(8):
                        bank = psc[jt % 2]
                        rec.wait_cur("tensor", "act")
                        mm3(rec, bank[:],
                            khb[:, jt * 128:(jt + 1) * 128], klb[:, jt * 128:(jt + 1) * 128],
                            qhb[:], qlb[:],
                            start=True, stop=True)
                        mmv = rec.mark("tensor", "tns")
                        rec.wait("vector", "tns", mmv)
                        rec.op("vector", lambda eng, bank=bank, jt=jt, mname=mname: eng.tensor_scalar(
                            bank[:], bank[:], SC_ATT, -C_SM, ALU.mult, ALU.add)
                            if False else eng.tensor_add(
                            bank[:], bank[:], sb[mname][:, jt * 512:(jt + 1) * 512]), inc="vec")
                        vv = ctx.cnt["vec"]
                        rec.wait("scalar", "vec", vv)
                        rec.op("scalar", lambda eng, bank=bank, jt=jt: eng.activation(
                            eb[:, jt * 512:(jt + 1) * 512], bank[:], AF.Exp,
                            bias=sb["negC"][:], scale=SC_ATT), inc="act")
                    edone = ctx.cnt["act"]
                    rec.wait("tensor", "act", edone)
                    for jt in range(8):
                        for qs in range(4):
                            rec.op("tensor", lambda eng, jt=jt, qs=qs: eng.matmul(
                                psz[:, qs:qs + 1],
                                eb[:, jt * 512 + qs * 128:jt * 512 + qs * 128 + 128],
                                sb["onescol"][:],
                                start=(jt == 0 and qs == 0), stop=(jt == 7 and qs == 3)))
                    zt = rec.mark("tensor", "tns")
                    rec.wait("vector", "tns", zt)
                    rec.op("vector", lambda eng: eng.tensor_copy(zsb[:], psz[:, 0:4]))
                    rec.fence("vector")
                    rec.op("vector", lambda eng: eng.reciprocal(zr[:], zsb[:]), inc="vec")
                    _row_bcast4(rec, ctx, sb, zr, ptr, pbc, zbc)
                    for jt in range(8):
                        gjt = qb * 4 + jt
                        rec.wait_cur("sync", "vec" if jt == 0 else "tns")
                        rec.dma("sync", vts[:], t["s_vT"][gjt * 128:(gjt + 1) * 128,
                                                          h * 128:(h + 1) * 128])
                        vdma = ctx.cnt["dma"]
                        rec.wait("tensor", "dma", vdma)
                        for qs in range(2):
                            rec.op("tensor", lambda eng, jt=jt, qs=qs: eng.matmul(
                                pso[:, qs * 256:(qs + 1) * 256], vts[:],
                                eb[:, jt * 512 + qs * 256:jt * 512 + qs * 256 + 256],
                                start=(jt == 0 and qs == 0), stop=(jt == 7 and qs == 1)))
                        rec.mark("tensor", "tns")
                    ot = ctx.cnt["tns"]
                    rec.wait("vector", "tns", ot)
                    rec.op("vector", lambda eng: eng.tensor_mul(onorm[:], pso[:], zbc[:]))
                    rec.fence("vector")
                    co = h * 512
                    rec.op("vector", lambda eng, co=co: eng.tensor_copy(oh[:, co:co + 512], onorm[:]))
                    rec.fence("vector")
                    rec.op("vector", lambda eng, co=co: eng.tensor_copy(zbc[:], oh[:, co:co + 512]))
                    rec.fence("vector")
                    rec.op("vector", lambda eng: eng.tensor_sub(onorm[:], onorm[:], zbc[:]))
                    rec.fence("vector")
                    rec.op("vector", lambda eng, co=co: eng.tensor_copy(ol[:, co:co + 512], onorm[:]),
                           inc="vec")
                rec.wait("tensor", "vec", ctx.cnt["vec"])
                for ohalf in range(2):
                    for ft in range(FT):
                        wsem = ("dma", "dmb")[ft % 2]
                        rec.wait_cur("sync", "vec")
                        rec.wait_cur("sync", "tns")
                        rec.dma("sync", wtmp[:],
                                t["l1wo"][ft * 128:(ft + 1) * 128,
                                          ohalf * 1024:(ohalf + 1) * 1024], sem=wsem)
                        rec.wait_cur("vector", wsem)
                        co = ft * 1024
                        rec.op("vector", lambda eng, co=co: eng.tensor_copy(
                            woh[:, co:co + 1024], wtmp[:]))
                        rec.fence("vector")
                        rec.op("vector", lambda eng, co=co: eng.tensor_copy(
                            wtm2[:], woh[:, co:co + 1024]))
                        rec.fence("vector")
                        rec.op("vector", lambda eng: eng.tensor_sub(wtmp[:], wtmp[:], wtm2[:]))
                        rec.fence("vector")
                        rec.op("vector", lambda eng, co=co: eng.tensor_copy(
                            wol[:, co:co + 1024], wtmp[:]), inc="vec")
                    wodone = ctx.cnt["vec"]
                    rec.wait("tensor", "vec", wodone)
                    for ot2 in range(ohalf * 8, ohalf * 8 + 8):
                        bank = psc[ot2 % 2]
                        rec.wait_cur("tensor", "vec")
                        for hh in range(NH):
                            co = hh * 1024 + (ot2 - ohalf * 8) * 128
                            mm3(rec, bank[:],
                                woh[:, co:co + 128], wol[:, co:co + 128],
                                oh[:, hh * 512:(hh + 1) * 512], ol[:, hh * 512:(hh + 1) * 512],
                                start=(hh == 0), stop=(hh == NH - 1))
                        mmv = rec.mark("tensor", "tns")
                        rec.wait("vector", "tns", mmv)
                        rec.wait_cur("vector", "dmag")
                        rec.op("vector", lambda eng, bank=bank: eng.tensor_copy(s1e[:], bank[:]),
                               inc="vec")
                        vv = ctx.cnt["vec"]
                        rec.wait("gpsimd", "vec", vv)
                        rec.dma("gpsimd", t["s_s1"][ot2 * 128:(ot2 + 1) * 128,
                                                    qb * 512:(qb + 1) * 512], s1e[:])
                ctx.barrier(rec)
                rec.flush()
    ctx.barrier(rec)
    rec.flush()


def p4a_ffn1_gateup(rec, ctx, t, sb):
    nc = ctx.nc
    with ExitStack() as ph:
        a = lambda n, shape, dt=F32: ph.enter_context(nc.sbuf_tensor("p4a_" + n, shape, dt))
        xq1c = a("xq1c", [128, FT * TOWN], BF16)
        wgq = a("wgq", [128, FT * 512], BF16)
        wuq = a("wuq", [128, FT * 512], BF16)
        sxb1 = a("sxb1", [128, 1024])
        g1t = a("g1t", [128, 512])
        slt = a("slt", [128, 512])
        hstr = a("hstr", [128, 512])
        with ExitStack() as ps:
            ppg = ps.enter_context(nc.psum_tensor("p4a_ppg", [128, 512], F32))
            ppu = ps.enter_context(nc.psum_tensor("p4a_ppu", [128, 512], F32))
            ptr = ps.enter_context(nc.psum_tensor("p4a_ptr", [1, 128], F32))
            pbc = ps.enter_context(nc.psum_tensor("p4a_pbc", [128, 512], F32))
            for ft in range(FT):
                rec.dma("sync", xq1c[:, ft * TOWN:(ft + 1) * TOWN],
                        t["s_xq1"][ft * 128:(ft + 1) * 128, :])
            xdma = ctx.cnt["dma"]
            _row_bcast4(rec, ctx, sb, sb["dxq1"][:, 0:4], ptr, pbc, sxb1[:, 0:512])
            _row_bcast4(rec, ctx, sb, sb["dxq1"][:, 4:8], ptr, pbc, sxb1[:, 512:1024])
            rec.wait("tensor", "dma", xdma)
            for oc in range(16):
                rec.wait_cur("sync", "tns")
                for ft in range(FT):
                    rec.dma("sync", wgq[:, ft * 512:(ft + 1) * 512],
                            t["s_wg1"][ft * 128:(ft + 1) * 128, oc * 512:(oc + 1) * 512])
                    rec.dma("sync", wuq[:, ft * 512:(ft + 1) * 512],
                            t["s_wu1"][ft * 128:(ft + 1) * 128, oc * 512:(oc + 1) * 512],
                            sem="dmb")
                rec.wait("tensor", "dma", ctx.cnt["dma"])
                rec.wait("tensor", "dmb", ctx.cnt["dmb"])
                for g in range(4):
                    gt = oc * 4 + g
                    for th in range(2):
                        rec.wait_cur("tensor", "vec")
                        for ft in range(FT):
                            rec.op("tensor", lambda eng, ft=ft, g=g, th=th: eng.matmul(
                                ppg[:], wgq[:, ft * 512 + g * 128:ft * 512 + g * 128 + 128],
                                xq1c[:, ft * TOWN + th * 512:ft * TOWN + (th + 1) * 512],
                                start=(ft == 0), stop=(ft == FT - 1)))
                        for ft in range(FT):
                            rec.op("tensor", lambda eng, ft=ft, g=g, th=th: eng.matmul(
                                ppu[:], wuq[:, ft * 512 + g * 128:ft * 512 + g * 128 + 128],
                                xq1c[:, ft * TOWN + th * 512:ft * TOWN + (th + 1) * 512],
                                start=(ft == 0), stop=(ft == FT - 1)))
                        mmv = rec.mark("tensor", "tns")
                        rec.wait("vector", "tns", mmv)
                        rec.wait_cur("vector", "dmag")
                        rec.wait_cur("vector", "act")
                        rec.op("vector", lambda eng, th=th: eng.tensor_mul(
                            g1t[:], ppg[:], sxb1[:, th * 512:(th + 1) * 512]), inc="vec")
                        rec.wait_cur("scalar", "vec")
                        rec.op("scalar", lambda eng: eng.activation(
                            slt[:], g1t[:], AF.Silu, bias=sb["zero"][:],
                            scale=sb["deq_g1"][:]), inc="act")
                        rec.wait_cur("vector", "act")
                        rec.op("vector", lambda eng, th=th: eng.tensor_mul(
                            g1t[:], ppu[:], sxb1[:, th * 512:(th + 1) * 512]))
                        rec.fence("vector")
                        rec.op("vector", lambda eng: eng.scalar_tensor_tensor(
                            hstr[:], g1t[:], sb["deq_u1"][:], slt[:],
                            ALU.mult, ALU.mult), inc="vec")
                        vv = ctx.cnt["vec"]
                        rec.wait("gpsimd", "vec", vv)
                        rec.dma("gpsimd", t["s_h1"][gt * 128:(gt + 1) * 128,
                                                    th * 512:(th + 1) * 512], hstr[:])
    ctx.barrier(rec)
    rec.flush()


def p4b_fwht_quant(rec, ctx, t, sb):
    nc = ctx.nc
    NTOK = 256
    for tc in range(4):
        with ExitStack() as ph:
            a = lambda n, shape, dt=F32: ph.enter_context(
                nc.sbuf_tensor(f"p4b_{n}_{tc}", shape, dt))
            hfe = a("hfe", [128, 64 * NTOK])
            bt = a("bt", [128, 32 * NTOK])
            hh = a("hh", [128, 64 * NTOK], BF16)
            hl = a("hl", [128, 64 * NTOK], BF16)
            rb = a("rb", [128, NTOK])
            yamax = a("yamax", [128, 2])
            ysc = a("ysc", [128, 2])
            tmp2 = a("tmp2", [128, 2])
            stg = [a("stg0", [128, NTOK], BF16), a("stg1", [128, NTOK], BF16)]
            with ExitStack() as ps:
                pm = ps.enter_context(nc.psum_tensor(f"p4b_pm_{tc}", [128, NTOK], F32))
                ptA = ps.enter_context(nc.psum_tensor(f"p4b_ptA_{tc}", [128, 128], F32))
                ptB = ps.enter_context(nc.psum_tensor(f"p4b_ptB_{tc}", [128, 128], F32))
                ptr = ps.enter_context(nc.psum_tensor(f"p4b_ptr_{tc}", [1, 128], F32))
                pbc = ps.enter_context(nc.psum_tensor(f"p4b_pbc_{tc}", [128, 512], F32))
                rec.wait_cur("sync", "vec")
                rec.wait_cur("gpsimd", "vec")
                for it in range(64):
                    if it % 4 == 3:
                        rec.dma("gpsimd", hfe[:, it * NTOK:(it + 1) * NTOK],
                                t["s_h1"][it * 128:(it + 1) * 128, tc * NTOK:(tc + 1) * NTOK],
                                sem="dmah")
                    else:
                        rec.dma("sync", hfe[:, it * NTOK:(it + 1) * NTOK],
                                t["s_h1"][it * 128:(it + 1) * 128, tc * NTOK:(tc + 1) * NTOK],
                                sem=("dma", "dmb", "out")[it % 4])
                rec.wait("vector", "dma", ctx.cnt["dma"])
                rec.wait("vector", "dmb", ctx.cnt["dmb"])
                rec.wait("vector", "out", ctx.cnt["out"])
                rec.wait("vector", "dmah", ctx.cnt["dmah"])
                for lvl in range(6):
                    d = 32 >> lvl
                    c = d * NTOK
                    va = hfe[:].rearrange("p (b two c) -> p b two c", two=2, c=c)
                    av = va[:, :, 0:1, :]
                    bv = va[:, :, 1:2, :]
                    btv = bt[:, 0:32 * NTOK]
                    rec.op("vector", lambda eng, av=av, bv=bv, btv=btv: eng.tensor_sub(
                        btv, av, bv))
                    rec.fence("vector")
                    rec.op("vector", lambda eng, av=av, bv=bv: eng.tensor_add(av, av, bv))
                    rec.fence("vector")
                    rec.op("vector", lambda eng, bv=bv, btv=btv: eng.tensor_copy(bv, btv))
                    rec.fence("vector")
                rec.op("vector", lambda eng: eng.tensor_single_scalar(
                    hfe[:], hfe[:], float(2.0 ** -6.5), ALU.mult))
                rec.fence("vector")
                rec.op("vector", lambda eng: eng.tensor_copy(hh[:], hfe[:]))
                rec.fence("vector")
                rec.op("vector", lambda eng: eng.tensor_sub(hfe[:], hfe[:], hh[:]))
                rec.fence("vector")
                rec.op("vector", lambda eng: eng.tensor_copy(hl[:], hfe[:]), inc="vec")
                spl = ctx.cnt["vec"]
                rec.wait("tensor", "vec", spl)
                rec.op("gpsimd", lambda eng: eng.memset(yamax[:], 0.0), inc="gps")
                rec.wait_cur("vector", "gps")
                for it in range(64):
                    rec.op("tensor", lambda eng, it=it: eng.matmul(
                        pm[:], sb["h128b"][:], hh[:, it * NTOK:(it + 1) * NTOK],
                        start=True, stop=False))
                    rec.op("tensor", lambda eng, it=it: eng.matmul(
                        pm[:], sb["h128b"][:], hl[:, it * NTOK:(it + 1) * NTOK],
                        start=False, stop=True))
                    mmv = rec.mark("tensor", "tns")
                    rec.wait("vector", "tns", mmv)
                    rec.op("vector", lambda eng, it=it: eng.tensor_copy(
                        hfe[:, it * NTOK:(it + 1) * NTOK], pm[:]), inc="vec")
                    ev = ctx.cnt["vec"]
                    rec.wait("tensor", "vec", ev)
                    for s in range(2):
                        pt = ptA if s % 2 == 0 else ptB
                        rec.wait_cur("tensor", "vec")
                        rec.op("tensor", lambda eng, pt=pt, it=it, s=s: eng.transpose(
                            pt[:], hfe[:, it * NTOK + s * 128:it * NTOK + s * 128 + 128],
                            sb["ident"][:]), inc="tns")
                        rec.wait_cur("vector", "tns")
                        rec.op("vector", lambda eng, pt=pt, s=s: eng.reduce_max(
                            tmp2[:, s:s + 1], pt[:], AX.X, apply_absolute_value=True))
                        rec.fence("vector")
                        rec.op("vector", lambda eng, s=s: eng.tensor_max(
                            yamax[:, s:s + 1], yamax[:, s:s + 1], tmp2[:, s:s + 1]),
                            inc="vec")
                rec.fence("vector")
                rec.op("vector", lambda eng: eng.tensor_single_scalar(
                    ysc[:], yamax[:], 1e-5, ALU.max))
                rec.fence("vector")
                rec.op("vector", lambda eng: eng.reciprocal(tmp2[:], ysc[:]))
                rec.fence("vector")
                rec.op("vector", lambda eng: eng.tensor_single_scalar(
                    tmp2[:], tmp2[:], 128.0, ALU.mult))
                rec.op("vector", lambda eng, tc=tc: eng.tensor_single_scalar(
                    sb["dyq1"][:, tc * 2:tc * 2 + 2], ysc[:], 1.0 / 128.0, ALU.mult),
                    inc="vec")
                _row_bcast4(rec, ctx, sb, tmp2, ptr, pbc, rb, n=2)
                for it in range(64):
                    sl = slice(it * NTOK, (it + 1) * NTOK)
                    sg = stg[it % 2]
                    rec.op("vector", lambda eng, sl=sl: eng.tensor_mul(hfe[:, sl], hfe[:, sl], rb[:]))
                    rec.fence("vector")
                    rec.op("vector", lambda eng, sl=sl: eng.tensor_scalar_add(
                        hfe[:, sl], hfe[:, sl], sb["magicM"][:]))
                    rec.fence("vector")
                    rec.op("vector", lambda eng, sl=sl: eng.tensor_single_scalar(
                        hfe[:, sl], hfe[:, sl], sb["magicM"][:], ALU.subtract))
                    rec.fence("vector")
                    rec.wait_cur("vector", "dmag")
                    rec.op("vector", lambda eng, sl=sl, sg=sg: eng.tensor_scalar(
                        sg[:], hfe[:, sl], 127.0, -128.0, ALU.min, ALU.max), inc="vec")
                    vv = ctx.cnt["vec"]
                    rec.wait("gpsimd", "vec", vv)
                    rec.dma("gpsimd", t["s_yq1"][it * 128:(it + 1) * 128,
                                                 tc * NTOK:(tc + 1) * NTOK], sg[:])
                ctx.barrier(rec)
                rec.flush()
    ctx.barrier(rec)
    rec.flush()


def p4c_down_out(rec, ctx, t, sb):
    nc = ctx.nc
    for tc2 in range(2):
        with ExitStack() as ph:
            a = lambda n, shape, dt=F32: ph.enter_context(
                nc.sbuf_tensor(f"p4c_{n}_{tc2}", shape, dt))
            yqc = a("yqc", [128, 64 * 512], BF16)
            wdq2 = [a(f"wdq{i}", [128, 64 * 128], BF16) for i in range(2)]
            syb = a("syb", [128, 512])
            d12 = [a(f"d1{i}", [128, 512]) for i in range(2)]
            g1c2 = [a(f"g1c{i}", [128, 512]) for i in range(2)]
            s1c2 = [a(f"s1c{i}", [128, 512]) for i in range(2)]
            xl1c2 = [a(f"xl1c{i}", [128, 512]) for i in range(2)]
            with ExitStack() as ps:
                pm2 = [ps.enter_context(nc.psum_tensor(f"p4c_pm{i}_{tc2}", [128, 512], F32))
                       for i in range(2)]
                ptr = ps.enter_context(nc.psum_tensor(f"p4c_ptr_{tc2}", [1, 128], F32))
                pbc = ps.enter_context(nc.psum_tensor(f"p4c_pbc_{tc2}", [128, 512], F32))
                rec.wait_cur("sync", "tns")
                rec.wait_cur("gpsimd", "tns")
                for it in range(64):
                    if it % 4 == 3:
                        rec.dma("gpsimd", yqc[:, it * 512:(it + 1) * 512],
                                t["s_yq1"][it * 128:(it + 1) * 128, tc2 * 512:(tc2 + 1) * 512],
                                sem="dmah")
                    else:
                        rec.dma("sync", yqc[:, it * 512:(it + 1) * 512],
                                t["s_yq1"][it * 128:(it + 1) * 128, tc2 * 512:(tc2 + 1) * 512],
                                sem=("dma", "dmb", "out")[it % 4])
                _row_bcast4(rec, ctx, sb, sb["dyq1"][:, tc2 * 4:tc2 * 4 + 4], ptr, pbc, syb)
                rec.wait("tensor", "dma", ctx.cnt["dma"])
                rec.wait("tensor", "dmb", ctx.cnt["dmb"])
                rec.wait("tensor", "out", ctx.cnt["out"])
                rec.wait("tensor", "dmah", ctx.cnt["dmah"])
                wdq_mm = [0, 0]
                pm_free = [0, 0]
                sx_used = [0, 0]
                d1_read = [0, 0]
                g1_ship = [0, 0]
                for o2 in range(FT):
                    pi = o2 % 2
                    wd = wdq2[pi]
                    rec.wait("sync", "tns", wdq_mm[pi])
                    for it in range(64):
                        rec.dma("sync", wd[:, it * 128:(it + 1) * 128],
                                t["s_wd1"][it * 128:(it + 1) * 128, o2 * 128:(o2 + 1) * 128],
                                sem=("dma", "dmb", "out")[it % 3])
                    rec.wait("sync", "vec", sx_used[pi])
                    rec.dma("sync", s1c2[pi][:], t["s_s1"][o2 * 128:(o2 + 1) * 128,
                                                           tc2 * 512:(tc2 + 1) * 512])
                    rec.dma("sync", xl1c2[pi][:],
                            t["s_xl1"][o2 * 128:(o2 + 1) * 128,
                                       512 + tc2 * 512:512 + (tc2 + 1) * 512],
                            sem="dmb")
                    rec.wait("tensor", "dma", ctx.cnt["dma"])
                    rec.wait("tensor", "dmb", ctx.cnt["dmb"])
                    rec.wait("tensor", "out", ctx.cnt["out"])
                    rec.wait("tensor", "vec", pm_free[pi])
                    for it in range(64):
                        rec.op("tensor", lambda eng, it=it, pm=pm2[pi], wd=wd: eng.matmul(
                            pm[:], wd[:, it * 128:(it + 1) * 128],
                            yqc[:, it * 512:(it + 1) * 512],
                            start=(it == 0), stop=(it == 63)))
                    mmv = rec.mark("tensor", "tns")
                    wdq_mm[pi] = mmv
                    rec.wait("vector", "tns", mmv)
                    rec.wait("vector", "act", d1_read[pi])
                    pm_free[pi] = rec.op("vector", lambda eng, pm=pm2[pi], d1=d12[pi]:
                                         eng.tensor_mul(d1[:], pm[:], syb[:]), inc="vec")
                    vv = ctx.cnt["vec"]
                    rec.wait("scalar", "vec", vv)
                    rec.wait("scalar", "dmah", g1_ship[pi])
                    d1_read[pi] = rec.op("scalar", lambda eng, d1=d12[pi], g1c=g1c2[pi]:
                                         eng.activation(g1c[:], d1[:], AF.Silu,
                                                        bias=sb["zero"][:],
                                                        scale=sb["deq_d1"][:]), inc="act")
                    av = ctx.cnt["act"]
                    rec.wait("vector", "act", av)
                    rec.wait("vector", "dmah", g1_ship[pi])
                    rec.op("vector", lambda eng, g1c=g1c2[pi], s1c=s1c2[pi]:
                           eng.tensor_mul(g1c[:], g1c[:], s1c[:]))
                    rec.fence("vector")
                    rec.op("vector", lambda eng, g1c=g1c2[pi], xl1c=xl1c2[pi]:
                           eng.tensor_add(g1c[:], g1c[:], xl1c[:]), inc="vec")
                    vv = ctx.cnt["vec"]
                    sx_used[pi] = vv
                    rec.wait("gpsimd", "vec", vv)
                    g1_ship[pi] = rec.dma("gpsimd",
                                          t["yout"][o2 * 128:(o2 + 1) * 128,
                                                    tc2 * 512:(tc2 + 1) * 512],
                                          g1c2[pi][:], sem="dmah")
                    rec.wait("gpsimd", "dmah", g1_ship[pi])
                ctx.barrier(rec)
                rec.flush()
    rec.wait("gpsimd", "dmah", ctx.cnt["dmah"])
    ctx.barrier(rec)
    rec.flush()


def _dump_dram(rec, ctx, t, dbg_name, src_ap, rows, cols, dt):
    nc = ctx.nc
    with nc.sbuf_tensor(f"dumpbuf_{dbg_name}", [128, cols], dt) as buf:
        for rt in range(rows // 128):
            dv = rec.dma("sync", buf[:], src_ap[rt * 128:(rt + 1) * 128, :])
            rec.wait("gpsimd", "dma", dv)
            dv2 = rec.dma("gpsimd", t[dbg_name][rt * 128:(rt + 1) * 128, :], buf[:])
            rec.wait("sync", "dmag", dv2)
        rec.wait("gpsimd", "dmag", ctx.cnt["dmag"])
        rec.flush()


def _dbg_dump(rec, ctx, t, pairs):
    for name, ap in pairs:
        rec.dma("gpsimd", t["dbg_" + name], ap)
    dv = ctx.cnt["dmag"]
    rec.wait("gpsimd", "dmag", dv)
    rec.flush()


def build(debug=None, stop_after=None, p1_rounds=4):
    nc = bass.Bass("TRN2", target_bir_lowering=False)
    t = declare_io(nc)
    if debug:
        for name, shape, dt in debug:
            t["dbg_" + name] = nc.dram_tensor("dbg_" + name, shape, dt,
                                              kind="ExternalOutput").ap()
    ctx = Ctx(nc)
    sb = alloc_persistent(ctx, nc)
    rec = Rec(ctx)
    p0_consts(rec, ctx, t, sb)
    ctx.barrier(rec)
    rec.flush()
    if stop_after == "p0":
        _dbg_dump(rec, ctx, t, [("canary", sb["canary"][:]), ("magicM", sb["magicM"][:]),
                                ("can1", sb["can1"][:]), ("m64t", sb["m64t"][:]),
                                ("m32t", sb["m32t"][:]), ("c075", sb["c075"][:])])
        ctx.es.close()
        return nc
    if stop_after == "ln":
        with ExitStack() as ph:
            a = lambda n, shape, dt=F32: ph.enter_context(nc.sbuf_tensor("lnt_" + n, shape, dt))
            bufs = {
                "xch": a("xch", [128, FT * 128]),
                "xT": a("xT", [128, D]), "xnT": a("xnT", [128, D]),
                "stats": a("stats", [128, 24]), "mv": a("mv", [128, 2]),
                "rr": a("rr", [128, 1]), "b2": a("b2", [128, 1]),
                "tmp1": a("tmp1", [128, 1024]), "aT": a("aT", [128, D]),
                "amax": a("amax", [128, 1]), "sc": a("sc", [128, 1]),
                "rin": a("rin", [128, 1]),
                "pb0": ph.enter_context(nc.psum_tensor("lnt_pb0", [128, 512], F32)),
                "pb1": ph.enter_context(nc.psum_tensor("lnt_pb1", [128, 512], F32)),
            }
            xnh = a("xnh", [128, FT * 128], BF16)
            xnl = a("xnl", [128, FT * 128], BF16)
            qsl = a("qsl", [128, FT * 128], BF16)
            deq = a("deq", [128, 1])
            ln_quant_chunk(rec, ctx, sb, bufs, t["xatt"], 0, 128, "split",
                           out={"hi": xnh, "lo": xnl,
                                "quant": {"kind": "l0", "q_slab": qsl, "deq_cols": deq}})
            ctx.barrier(rec)
            hi32 = a("hi32", [128, FT * 128])
            lo32 = a("lo32", [128, FT * 128])
            q32 = a("q32", [128, FT * 128])
            rec.op("vector", lambda eng: eng.tensor_copy(hi32[:], xnh[:]))
            rec.op("vector", lambda eng: eng.tensor_copy(lo32[:], xnl[:]))
            rec.op("vector", lambda eng: eng.tensor_copy(q32[:], qsl[:]), inc="vec")
            rec.fence("vector")
            _dbg_dump(rec, ctx, t, [("xT", bufs["xT"][:]), ("xnT", bufs["xnT"][:]),
                                    ("mv", bufs["mv"][:]), ("hi32", hi32[:]),
                                    ("lo32", lo32[:]), ("q32", q32[:]), ("deq", deq[:]),
                                    ("stats", bufs["stats"][:])])
        ctx.es.close()
        return nc
    p0b_ln0(rec, ctx, t, sb)
    p1_l0_attention(rec, ctx, t, sb, rounds=p1_rounds)
    if stop_after == "p1":
        _dbg_dump(rec, ctx, t, [("csum", sb["csum"][:])])
        ctx.es.close()
        return nc
    p2a_quant_w0(rec, ctx, t, sb)
    p2b_quant_w1(rec, ctx, t, sb)
    p2c_values_xq(rec, ctx, t, sb)
    p2d_ffn0(rec, ctx, t, sb)
    p2e_outproj_xl1(rec, ctx, t, sb)
    if stop_after == "p2":
        _dump_dram(rec, ctx, t, "dbg_xl1", t["s_xl1"], D, TF, F32)
        if "dbg_g0" in t:
            _dump_dram(rec, ctx, t, "dbg_g0", t["s_g0"], D, TF, F32)
        if "dbg_vh" in t:
            _dump_dram(rec, ctx, t, "dbg_vh", t["s_vh"], D, TF, BF16)
        if "dbg_vl" in t:
            _dump_dram(rec, ctx, t, "dbg_vl", t["s_vl"], D, TF, BF16)
        if "dbg_xq" in t:
            _dump_dram(rec, ctx, t, "dbg_xq", t["s_xq0"], D, TF, BF16)
        ctx.es.close()
        return nc
    p3a_l1_proj(rec, ctx, t, sb)
    p3b_l1_attn(rec, ctx, t, sb)
    if stop_after == "p3":
        _dump_dram(rec, ctx, t, "dbg_s1", t["s_s1"], D, TOWN, F32)
        ctx.es.close()
        return nc
    p4a_ffn1_gateup(rec, ctx, t, sb)
    p4b_fwht_quant(rec, ctx, t, sb)
    p4c_down_out(rec, ctx, t, sb)
    ctx.es.close()
    return nc


def kernel(**inputs):
    in_maps = _stage_inputs(inputs)
    nc = build()
    res = run_bass_kernel_spmd(nc, in_maps, core_ids=list(range(8)))
    out = np.empty((4, 2048, 2048), dtype=np.float32)
    for c in range(8):
        b, r = c // 2, c % 2
        y = np.asarray(res.results[c]["yout"])
        out[b, r * 1024:(r + 1) * 1024, :] = y.T
    return out
